# version 05 (final)
# baseline (speedup 1.0000x reference)
import numpy as np
import ml_dtypes
import concourse.bass as bass
import concourse.mybir as mybir
from concourse.bass_utils import run_bass_kernel_spmd
from contextlib import ExitStack

F32 = mybir.dt.float32
BF16 = mybir.dt.bfloat16
AF = mybir.ActivationFunctionType
ALU = mybir.AluOpType
AX = mybir.AxisListType

SEQ, D = 4096, 1024
TB = 512
NB = SEQ // TB
NP = NB // 2
NCH = D // 128
GP = 8
CMW = GP + SEQ
NL = 6
DI = 2730
DIP = 2816
MT = DIP // 128
NH, HD, WIN = 16, 64, 256
QK_SCALE = 8.0
FD1 = 16
X2S = float(1.0 / (np.sqrt(2.0) * 4.0))
YSC = 32.0
LNEPS = 1e-5
RMSEPS = 1e-5

ENGS = ("sync", "tensor", "vector", "scalar", "gpsimd")
B_PHASES = 3
FNL = None
FSKIP_FINAL = False
FCONV_MIN = False
FDBG_SCRATCH = False
FNO_GUARD = False
FDRAIN = False


def set_seq(seq):
    global SEQ, NB, NP, CMW
    SEQ = seq
    NB = SEQ // TB
    NP = NB // 2
    CMW = GP + SEQ


class Kb:
    def __init__(self, nc, ctx):
        self.nc = nc
        self.ctx = ctx
        self.E = ctx.enter_context
        self.eng = {"sync": nc.sync, "tensor": nc.tensor, "vector": nc.vector,
                    "scalar": nc.scalar, "gpsimd": nc.gpsimd}
        self.sem = {}
        self.count = {}
        self.mode = "straight"
        self.regs = {}
        self.regpool = {e: [] for e in ENGS}
        self.nregs = {e: 0 for e in ENGS}
        self.rec_last = None
        self.rec_delta = None
        self.body_delta = None

    def s(self, name):
        if name not in self.sem:
            self.sem[name] = self.E(self.nc.semaphore(name))
            self.count[name] = 0
        return self.sem[name]

    def inc(self, inst, name, n=1):
        inst.then_inc(self.s(name), n)
        if self.mode in ("straight", "record"):
            self.count[name] += n
            if self.mode == "record":
                self.rec_delta[name] = self.rec_delta.get(name, 0) + n
        else:
            self.body_delta[name] = self.body_delta.get(name, 0) + n
        return inst

    def _reg(self, engine):
        if self.regpool[engine]:
            return self.regpool[engine].pop()
        self.nregs[engine] += 1
        return self.E(self.eng[engine].register(
            f"r_{engine}_{self.nregs[engine]}"))

    def wait(self, engine, name, target):
        eng = self.eng[engine]
        sem = self.s(name)
        if self.mode == "straight":
            if target > 0:
                eng.wait_ge(sem, target)
            return
        key = (engine, name)
        t = max(target, 0)
        if self.mode == "record":
            if key in self.rec_last and t <= self.rec_last[key]:
                return
            eng.wait_ge(sem, t)
            self.rec_last[key] = t
            return
        assert key in self.regs, f"loop wait on {key} never recorded"
        if t <= self.rec_last[key]:
            return
        diff = t - self.rec_last[key]
        r = self.regs[key]
        eng.reg_add(r, r, diff)
        eng.wait_ge(sem, r)
        self.rec_last[key] = t

    def loop_record(self):
        assert self.mode == "straight"
        self.mode = "record"
        self.rec_last = {}
        self.rec_delta = {}

    def loop_enter(self):
        assert self.mode == "record"
        for key, last in self.rec_last.items():
            engine, name = key
            r = self._reg(engine)
            self.regs[key] = r
            self.eng[engine].reg_mov(r, last)
        self.mode = "loop"
        self.body_delta = {}

    def loop_iter_end(self):
        for name, d in self.body_delta.items():
            assert d == self.rec_delta.get(name, 0), (
                f"delta mismatch {name}: body {d} vs rec "
                f"{self.rec_delta.get(name, 0)}")
        self.body_delta = {}

    def loop_exit(self, extra_iters):
        assert self.mode == "loop"
        for name, d in self.rec_delta.items():
            self.count[name] += d * extra_iters
        for key, r in self.regs.items():
            self.regpool[key[0]].append(r)
        self.regs = {}
        self.mode = "straight"
        self.rec_last = self.rec_delta = self.body_delta = None

    DMA_PREF = ("xin", "xi", "wld", "wl", "out", "tb", "wot")

    def stage_fence(self):
        assert self.mode == "straight"
        for name in list(self.sem):
            if any(name.startswith(p) for p in self.DMA_PREF) and \
                    self.count[name]:
                self.eng["sync"].wait_ge(self.sem[name], self.count[name])
        f = self.s("fence")
        for e in ("tensor", "vector", "scalar", "sync"):
            self.eng[e].nop().then_inc(f, 1)
        self.count["fence"] += 4
        tgt = self.count["fence"]
        for e in ("tensor", "vector", "scalar", "sync"):
            self.eng[e].wait_ge(f, tgt)

    def stage_reset(self, drain=None):
        assert self.mode == "straight"
        if drain is None:
            drain = [n for n in self.sem
                     if any(n.startswith(p) for p in self.DMA_PREF)]
        for name in drain:
            if name in self.sem and self.count[name]:
                self.eng["sync"].wait_ge(self.sem[name], self.count[name])
        self.nc.all_engine_barrier()
        for name, h in self.sem.items():
            if name.startswith("wcv") or name.startswith("wco"):
                continue
            self.nc.gpsimd.sem_clear(h)
            self.count[name] = 0
        self.nc.all_engine_barrier()


def dview(t, offset, pattern):
    return bass.AP(t, offset, pattern)


UNROLL = True


def run_streamed(kb, emit_pair, n_pairs=None):
    nc = kb.nc
    if n_pairs is None:
        n_pairs = NP
    if UNROLL or n_pairs == 1:
        kb.loop_record()
        emit_pair(0, None)
        kb.mode = "straight"
        kb.rec_last = kb.rec_delta = None
        for jp in range(1, n_pairs):
            emit_pair(jp, None)
        return
    kb.loop_record()
    emit_pair(0, None)
    kb.loop_enter()
    with nc.Fori(1, n_pairs) as it:
        emit_pair(1, it)
        kb.loop_iter_end()
    kb.loop_exit(n_pairs - 1)


SP_A, SV_A, SA_A = 32, 37, 25


def emit_stageA(kb, st, wh, lv, src, dst):
    nc = kb.nc
    t, v, a, sy, g = nc.tensor, nc.vector, nc.scalar, nc.sync, nc.gpsimd

    w1h = wh["w1bf"]
    w2h = wh["w2bf"]
    b1h = wh["b1bf"]
    gbh = wh["gbb"]
    e_sp = kb.count.get("sp", 0)
    e_sv = kb.count.get("sv", 0)
    e_sa = kb.count.get("sa", 0)
    e_wld = kb.count.get("wld", 0)
    e_xin = kb.count.get("xin", 0)

    kb.inc(sy.dma_start(out=st.w2res[:], in_=dview(
        w2h, lv * (DIP * D), [[D, 128], [128 * D, MT], [1, D]])), "xin", 16)
    kb.inc(sy.dma_start(out=st.gbcol[:], in_=dview(
        gbh, lv * (128 * 3 * NCH), [[3 * NCH, 128], [1, 3 * NCH]])), "xin", 16)
    kb.inc(sy.dma_start(out=st.b1row[:], in_=dview(
        b1h, lv * (MT * 256), [[MT * 256, 1], [256, MT], [1, 256]])), "wld", 16)
    kb.inc(sy.dma_start(out=st.brow2bf[1:2, :], in_=st.cst_ones_row[0:1, 0:TB]),
           "xin", 16)
    kb.wait("vector", "xin", kb.count["xin"])
    kb.inc(v.tensor_copy(
        st.gb2bf[:],
        st.gbcol[:, 0:2 * NCH].rearrange("p (t c) -> p c t", c=NCH)), "sv")
    for m in range(MT):
        par = m % 2
        if m >= 2:
            kb.wait("sync", "sp", e_sp + m - 1)
        kb.inc(sy.dma_start(out=st.w1t[par][:], in_=dview(
            w1h, lv * (D * MT * 256) + m * 256,
            [[MT * 256, 128], [128 * MT * 256, NCH], [1, 256]])), "wld", 16)
        kb.wait("tensor", "wld", e_wld + 16 * (m + 2))
        kb.wait("tensor", "sv", e_sv + 1)
        if m >= 1:
            kb.wait("tensor", "sv", e_sv + 1 + (m - 1) + 1)
        for c in range(NCH):
            t.matmul(st.ps_rows[0:2, 0:256], st.gb2bf[:, c, :],
                     st.w1t[par][:, c, :], start=(c == 0), stop=False)
        mm = t.matmul(st.ps_rows[0:2, 0:256], st.zo_bf[:],
                      st.b1row[0:1, m, :], start=False, stop=True)
        kb.inc(mm, "sp")
        kb.wait("vector", "sp", e_sp + m + 1)
        kb.inc(v.tensor_copy(st.colsAB[0:2, m, :], st.ps_rows[0:2, 0:256]),
               "sv")
    cnt = lambda n: kb.count.get(n, 0)
    spB, svB, saB = cnt("sp"), cnt("sv"), cnt("sa")
    wldB, xinB, outB = cnt("wld"), cnt("xin"), cnt("out")
    assert (spB - e_sp, svB - e_sv, saB - e_sa) == (MT, 1 + MT, 0)
    assert wldB - e_wld == 16 * (MT + 1) and xinB - e_xin == 3 * 16

    def sp_of(j, k):
        return spB + j * SP_A + k

    def sv_of(j, k):
        return svB + j * SV_A + k

    def sa_of(j, k):
        return saB + j * SA_A + k

    def emit_blk(j, jd):
        par = j % 2
        x_t = st.xa[par]

        kb.wait("sync", "sv", sv_of(j - 2, SV_A))
        kb.wait("sync", "sa", sa_of(j - 2, 1))
        kb.inc(sy.dma_start(out=x_t[:], in_=dview(
            src, jd * TB + (GP - 1),
            [[CMW, 128], [128 * CMW, NCH], [1, TB + 1]])), "xin", 16)

        for m in range(MT):
            gidx = j * MT + m
            pj, pm = divmod(gidx - 2, MT)
            kb.wait("sync", "sp", sp_of(pj, 2 + pm + 1) if gidx >= 2
                    else spB)
            kb.inc(sy.dma_start(out=st.w1t[m % 2][:], in_=dview(
                w1h, lv * (D * MT * 256) + m * 256,
                [[MT * 256, 128], [128 * MT * 256, NCH], [1, 256]])),
                "wld", 16)

        kb.wait("vector", "xin", xinB + 16 * (j + 1))
        kb.wait("vector", "sp", sp_of(j - 1, 1))
        v.tensor_copy(st.xsbf[:, 0:4, :], x_t[:, 0:4, 1:TB + 1])
        kb.inc(v.tensor_copy(st.xsbf[:, 4:8, :], x_t[:, 4:8, 0:TB]), "sv")

        kb.wait("scalar", "xin", xinB + 16 * (j + 1))
        kb.wait("scalar", "sp", sp_of(j - 1, 1))
        a.activation(st.x2bf[:, 0:4, :], x_t[:, 0:4, 1:TB + 1], AF.Square)
        kb.inc(a.activation(st.x2bf[:, 4:8, :], x_t[:, 4:8, 0:TB], AF.Square),
               "sa")

        kb.wait("tensor", "sv", sv_of(j, 1))
        kb.wait("tensor", "sa", sa_of(j, 1))
        kb.wait("tensor", "sa", sa_of(j - 1, 2))
        kb.wait("tensor", "sv", sv_of(j - 1, 7))
        for c in range(NCH):
            t.matmul(st.ps_rows[0:1, 0:512], st.onescol_bf[:],
                     st.xsbf[:, c, :], start=(c == 0), stop=(c == NCH - 1))
        for c in range(NCH):
            mm = t.matmul(st.ps_ea[0:1, 0:512], st.onescol_bf[:],
                          st.x2bf[:, c, :], start=(c == 0), stop=(c == NCH - 1))
        kb.inc(mm, "sp")

        kb.wait("scalar", "sp", sp_of(j, 1))
        kb.wait("scalar", "sv", sv_of(j - 1, 6))
        a.copy(st.rows_sb[0:1, 0:512], st.ps_rows[0:1, 0:512])
        kb.inc(a.copy(st.rows_sb[0:1, 512:1024], st.ps_ea[0:1, 0:512]), "sa")

        kb.wait("vector", "sa", sa_of(j, 2))
        sx = st.rows_sb[0:1, 0:512]
        sxx = st.rows_sb[0:1, 512:1024]
        u = st.rows_sb[0:1, 1024:1536]
        var = st.rows_sb[0:1, 1536:2048]
        kb.inc(v.scalar_tensor_tensor(u, sx, 1.0 / D, sx, ALU.mult,
                                      ALU.mult), "sv")
        kb.wait("vector", "sv", sv_of(j, 2))
        kb.inc(v.tensor_sub(var, sxx, u), "sv")

        kb.wait("scalar", "sv", sv_of(j, 3))
        kb.inc(a.activation(u, var, AF.Sqrt, bias=st.C("lneps", 1),
                            scale=st.C("invD", 1)), "sa")

        kb.wait("vector", "sa", sa_of(j, 3))
        kb.wait("vector", "sp", sp_of(j - 1, 2 + MT))
        kb.inc(v.reciprocal(st.arow[0:1, :], u), "sv")
        kb.wait("vector", "sv", sv_of(j, 4))
        kb.inc(v.scalar_tensor_tensor(st.brow2[0:1, :], sx, -1.0 / D,
                                      st.arow[0:1, :], ALU.mult, ALU.mult),
               "sv")
        kb.wait("vector", "sv", sv_of(j, 5))
        kb.inc(v.tensor_copy(st.brow2bf[0:1, :], st.brow2[0:1, :]), "sv")

        kb.wait("tensor", "sv", sv_of(j, 6))
        kb.wait("tensor", "sv", sv_of(j - 1, 7))
        kb.inc(t.matmul(st.ps_ea[:], st.ones1_f32[:], st.arow[0:1, :],
                        start=True, stop=True), "sp")

        kb.wait("vector", "sp", sp_of(j, 2))
        kb.wait("vector", "sp", sp_of(j - 2, 2 + MT))
        for c in range(NCH):
            xv = x_t[:, c, 1:TB + 1] if c < 4 else x_t[:, c, 0:TB]
            zi = v.scalar_tensor_tensor(st.za[par][:, c, :], xv,
                                        st.gbcol[:, c:c + 1], st.ps_ea[:],
                                        ALU.mult, ALU.mult)
        kb.inc(zi, "sv")

        for m in range(MT):
            wpar = m % 2
            pa, pg = st.ps_ag[wpar * 2], st.ps_ag[wpar * 2 + 1]
            gidx = j * MT + m
            kb.wait("tensor", "wld", wldB + 16 * (gidx + 1))
            kb.wait("tensor", "sv", sv_of(j, 7))
            pj, pm = divmod(gidx - 2, MT)
            if gidx >= 2:
                kb.wait("tensor", "sv", sv_of(pj, 7 + pm + 1))
                kb.wait("tensor", "sa", sa_of(pj, 3 + pm + 1))
            for c in range(NCH):
                t.matmul(pa[:], st.w1t[wpar][:, c, 0:128],
                         st.za[par][:, c, :], start=(c == 0), stop=False)
            t.matmul(pa[:], st.colsAB[0:2, m, 0:128], st.brow2bf[0:2, :],
                     start=False, stop=True)
            for c in range(NCH):
                t.matmul(pg[:], st.w1t[wpar][:, c, 128:256],
                         st.za[par][:, c, :], start=(c == 0), stop=False)
            mm = t.matmul(pg[:], st.colsAB[0:2, m, 128:256],
                          st.brow2bf[0:2, :], start=False, stop=True)
            kb.inc(mm, "sp")

            kb.wait("scalar", "sp", sp_of(j, 2 + m + 1))
            if gidx >= 2:
                kb.wait("scalar", "sv", sv_of(pj, 7 + pm + 1))
            kb.inc(a.activation(st.gelu_t[wpar][:], pg[:], AF.Gelu), "sa")

            kb.wait("vector", "sa", sa_of(j, 3 + m + 1))
            kb.inc(v.tensor_mul(st.ha[:, m, :], pa[:], st.gelu_t[wpar][:]),
                   "sv")

        for o in range(NCH):
            py = st.ps_y[o % 2]
            kb.wait("tensor", "sv", sv_of(j, 7 + MT))
            oj, oo = (j, o - 2) if o >= 2 else (j - 1, NCH - 2 + o)
            kb.wait("tensor", "sv", sv_of(oj, 7 + MT + oo + 1))
            for m in range(MT):
                mm = t.matmul(py[:], st.w2res[:, m, o * 128:(o + 1) * 128],
                              st.ha[:, m, :], start=(m == 0),
                              stop=(m == MT - 1))
            kb.inc(mm, "sp")

            kb.wait("vector", "sp", sp_of(j, 2 + MT + o + 1))
            if o == 0:
                kb.wait("vector", "out", outB + 16 * j)
            rz = v.scalar_tensor_tensor(
                st.xout[:, o, :], py[:],
                st.gbcol[:, 2 * NCH + o:2 * NCH + o + 1],
                x_t[:, o, 1:TB + 1], ALU.add, ALU.add)
            kb.inc(rz, "sv")

        kb.wait("sync", "sv", sv_of(j, 7 + MT + NCH))
        kb.inc(sy.dma_start(out=dview(dst, jd * TB + GP,
                                      [[CMW, 128], [128 * CMW, NCH], [1, TB]]),
                            in_=st.xout[:]), "out", 16)

    def emit_pair(jp, jv):
        for p in range(2):
            j = jp * 2 + p
            jd = j if jv is None else jv * 2 + p
            emit_blk(j, jd)

    run_streamed(kb, emit_pair)
    assert kb.count["sp"] == spB + NB * SP_A, kb.count["sp"]
    assert kb.count["sv"] == svB + NB * SV_A
    assert kb.count["sa"] == saB + NB * SA_A


ARENA_BYTES = 207 * 1024


class Arena:
    def __init__(self, kb, size=ARENA_BYTES):
        self.buf = kb.E(kb.nc.sbuf_tensor("arena", [128, size],
                                          mybir.dt.int8))
        self.size = size
        self.perm_end = 0
        self.off = 0
        self.hw = 0

    log = []

    def _alloc(self, shape, dtype, off):
        Arena.log.append((off, shape, dtype))
        esz = {F32: 4, BF16: 2}[dtype]
        fe = int(np.prod(shape[1:]))
        nb = fe * esz
        assert off + nb <= self.size, f"arena overflow at {off}+{nb}"
        v = self.buf[0:shape[0], off:off + nb].bitcast(dtype)
        if len(shape) == 3:
            v = v.rearrange("p (a b) -> p a b", a=shape[1])
        elif len(shape) == 4:
            v = v.rearrange("p (a b c) -> p a b c", a=shape[1], b=shape[2])
        return v, off + ((nb + 63) // 64) * 64

    def perm(self, shape, dtype):
        v, self.perm_end = self._alloc(shape, dtype, self.perm_end)
        self.off = max(self.off, self.perm_end)
        self.hw = max(self.hw, self.perm_end)
        return v

    def region_begin(self):
        self.off = self.perm_end

    def alloc(self, shape, dtype):
        v, self.off = self._alloc(shape, dtype, self.off)
        self.hw = max(self.hw, self.off)
        return v


class Pool:
    pass


def alloc_common(kb, ar):
    nc, E = kb.nc, kb.E
    st = Pool()
    st.psum8 = E(nc.psum_tensor("banks8", [128, 8, 512], F32))
    st.bank = [st.psum8[:, i, :] for i in range(8)]
    st.ps_rows = st.bank[0]
    st.ps_ea = st.bank[1]
    st.ps_ag = [st.bank[2], st.bank[3], st.bank[4], st.bank[5]]
    st.ps_y = [st.bank[6], st.bank[7]]
    st.cst = ar.perm([128, CONSTW], F32)
    st.C = lambda name, parts=128: st.cst[0:parts, CCOL[name]:CCOL[name] + 1]
    st.onescol_bf = ar.perm([128, 1], BF16)
    st.ones1_f32 = ar.perm([1, 128], F32)
    st.zo_bf = ar.perm([1, 2], BF16)
    st.cst_ones_row = ar.perm([1, TB], BF16)
    return st


def alloc_stageA(kb, ar, st):
    ar.region_begin()
    st.xa = [ar.alloc([128, NCH, TB + 1], F32) for _ in range(2)]
    st.xsbf = ar.alloc([128, NCH, TB], BF16)
    st.x2bf = ar.alloc([128, NCH, TB], BF16)
    st.za = [ar.alloc([128, NCH, TB], BF16) for _ in range(2)]
    st.ha = ar.alloc([128, MT, TB], BF16)
    st.w1t = [ar.alloc([128, NCH, 256], BF16) for _ in range(2)]
    st.w2res = ar.alloc([128, MT, D], BF16)
    st.gelu_t = [ar.alloc([128, TB], BF16) for _ in range(2)]
    st.xout = ar.alloc([128, NCH, TB], F32)
    st.gbcol = ar.alloc([128, 3 * NCH], F32)
    st.gb2bf = ar.alloc([128, NCH, 2], BF16)
    st.b1row = ar.alloc([1, MT, 256], BF16)
    st.colsAB = ar.alloc([2, MT, 256], BF16)
    st.rows_sb = ar.alloc([1, 2048], F32)
    st.arow = ar.alloc([1, TB], F32)
    st.brow2 = ar.alloc([2, TB], F32)
    st.brow2bf = ar.alloc([2, TB], BF16)
    return st


def emit_const_init(kb, st, ch):
    nc = kb.nc
    v, sy = nc.vector, nc.sync
    kb.inc(sy.dma_start(out=st.cst[:], in_=dview(
        ch, 0, [[CONSTW, 128], [1, CONSTW]])), "xin", 16)
    kb.wait("vector", "xin", kb.count["xin"])
    v.tensor_copy(st.onescol_bf[:], st.cst[:, 0:1])
    v.tensor_copy(st.ones1_f32[0:1, :], st.cst[0:1, 2:130])
    v.tensor_copy(st.zo_bf[0:1, :], st.cst[0:1, 130:132])
    kb.inc(v.tensor_copy(
        st.cst_ones_row[0:1, :],
        st.cst[0:1, 0:1].broadcast_to([1, TB])), "sv")


CONSTW = 256


CVALS = {"lneps": LNEPS, "invD": 1.0 / D, "poly_b": 4.0,
         "rmseps": RMSEPS * YSC * YSC, "invHD": 1.0 / HD, "one": 1.0,
         "neg1": -1.0, "half": 0.5, "qks": QK_SCALE, "tiny": 1e-12}
CCOL = {k: 132 + i for i, k in enumerate(CVALS)}


def host_consts():
    c = np.zeros((128, CONSTW), np.float32)
    c[:, 0] = 1.0
    c[0, 2:130] = 1.0
    c[0, 130] = 0.0
    c[0, 131] = 1.0
    c[2, 2:130] = np.sqrt(8.0)
    for k, val in CVALS.items():
        c[:, CCOL[k]] = val
    return c


def build_dev_A():
    ctx = ExitStack()
    nc = bass.Bass("TRN2", target_bir_lowering=False)
    kb = Kb(nc, ctx)
    with ctx:
        xin_h = nc.dram_tensor("xin_cm", [D, CMW], F32, kind="ExternalInput")
        xout_h = nc.dram_tensor("xout_cm", [D, CMW], F32, kind="ExternalOutput")
        w1i_h = nc.dram_tensor("w1i", [NL, D, MT, 256], F32,
                               kind="ExternalInput")
        b1i_h = nc.dram_tensor("b1i", [NL, MT, 256], F32, kind="ExternalInput")
        w2p_h = nc.dram_tensor("w2p", [NL, DIP, D], F32, kind="ExternalInput")
        gbb_h = nc.dram_tensor("gbb", [NL, 128, 3 * NCH], F32,
                               kind="ExternalInput")
        cst_h = nc.dram_tensor("csts", [128, CONSTW], F32, kind="ExternalInput")
        w1bf_h = nc.dram_tensor("w1bf", [NL, D, MT, 256], BF16)
        w2bf_h = nc.dram_tensor("w2bf", [NL, DIP, D], BF16)
        b1bf_h = nc.dram_tensor("b1bf", [NL, MT, 256], BF16)
        wh = {"w1bf": w1bf_h, "w2bf": w2bf_h, "b1bf": b1bf_h, "gbb": gbb_h}
        ar = Arena(kb)
        st = alloc_common(kb, ar)
        emit_const_init(kb, st, cst_h)
        emit_w_convert(kb, ar, [(w1i_h, w1bf_h, NL * D * MT * 256),
                                (w2p_h, w2bf_h, NL * DIP * D),
                                (b1i_h, b1bf_h, NL * MT * 256)])
        alloc_stageA(kb, ar, st)
        print(f"[kernel] arena high-water: {ar.hw} bytes/partition")
        kb.stage_reset(drain=("xin", "wld"))
        emit_stageA(kb, st, wh, 0, xin_h, xout_h)
        kb.wait("sync", "out", kb.count["out"])
        kb.inc(nc.sync.dma_start(
            out=dview(xout_h, 0, [[CMW, 128], [128 * CMW, NCH], [1, GP]]),
            in_=dview(xin_h, 0, [[CMW, 128], [128 * CMW, NCH], [1, GP]])),
            "out", 16)
        kb.wait("sync", "out", kb.count["out"])
    return nc


def prep_weights_host(ins):
    W1 = np.asarray(ins["L0_ffn_W1"])
    b1 = np.asarray(ins["L0_ffn_b1"])
    W2 = np.asarray(ins["L0_ffn_W2"])
    w1p = np.zeros((NL, D, 2, DIP), np.float32)
    w1p[:, :, 0, :DI] = W1[:, :, :DI]
    w1p[:, :, 1, :DI] = W1[:, :, DI:]
    w1i = np.empty((NL, D, MT, 256), np.float32)
    w1i[:, :, :, 0:128] = w1p[:, :, 0].reshape(NL, D, MT, 128)
    w1i[:, :, :, 128:256] = w1p[:, :, 1].reshape(NL, D, MT, 128)
    b1p = np.zeros((NL, 2, DIP), np.float32)
    b1p[:, 0, :DI] = b1[:, :DI]
    b1p[:, 1, :DI] = b1[:, DI:]
    b1i = np.empty((NL, MT, 256), np.float32)
    b1i[:, :, 0:128] = b1p[:, 0].reshape(NL, MT, 128)
    b1i[:, :, 128:256] = b1p[:, 1].reshape(NL, MT, 128)
    w2p = np.zeros((NL, DIP, D), np.float32)
    w2p[:, :DI, :] = W2
    gbb = np.empty((NL, 128, 3 * NCH), np.float32)
    gbb[:, :, 0:NCH] = np.asarray(ins["L0_ffn_ln_g"]).reshape(
        NL, NCH, 128).transpose(0, 2, 1)
    gbb[:, :, NCH:2 * NCH] = np.asarray(ins["L0_ffn_ln_b"]).reshape(
        NL, NCH, 128).transpose(0, 2, 1)
    gbb[:, :, 2 * NCH:] = np.asarray(ins["L0_ffn_b2"]).reshape(
        NL, NCH, 128).transpose(0, 2, 1)
    return {"w1i": np.ascontiguousarray(w1i),
            "b1i": np.ascontiguousarray(b1i),
            "w2p": np.ascontiguousarray(w2p),
            "gbb": np.ascontiguousarray(gbb)}


def emit_w_convert(kb, ar, pairs, drain=True, bounce=None):
    nc = kb.nc
    g, sy = nc.gpsimd, nc.sync
    BW = 1024
    TILE = 128 * BW
    if bounce is None:
        ar.region_begin()
        bounce = [ar.alloc([128, BW], BF16) for _ in range(2)]
    bb = bounce
    k = 0
    marks = []
    for pr in pairs:
        (srct, dstt, total), base = pr[:3], (pr[3] if len(pr) > 3 else 0)
        nt = (total + TILE - 1) // TILE
        for i in range(nt):
            off = i * TILE
            cols = min(BW, (total - off) // 128)
            assert cols * 128 == min(TILE, total - off), (total, off)
            par = k % 2
            if k >= 2:
                kb.wait("gpsimd", f"wco{par}", kb.count.get(f"wco{par}", 0))
            kb.inc(g.dma_start(out=bb[par][:, 0:cols], in_=dview(
                srct, base + off, [[cols, 128], [1, cols]])), f"wcv{par}", 16)
            kb.wait("gpsimd", f"wcv{par}", kb.count[f"wcv{par}"])
            kb.inc(g.dma_start(out=dview(
                dstt, base + off, [[cols, 128], [1, cols]]),
                in_=bb[par][:, 0:cols]), f"wco{par}", 16)
            k += 1
        marks.append((kb.count.get("wco0", 0), kb.count.get("wco1", 0)))
    if drain:
        for p in range(2):
            kb.wait("sync", f"wco{p}", kb.count.get(f"wco{p}", 0))
    return marks


class Sched:
    def __init__(self, kb, names):
        self.kb = kb
        self.base = {n: kb.count.get(n, 0) for n in names}
        self.idx = {}
        self.per = {}
        self._ctr = {n: 0 for n in names}
        self._fin = False

    def blk_start(self, j):
        self.cur_par = j % 2
        if j == 2 and not self._fin:
            self.per = dict(self._ctr)
            self._fin = True

    DMA_SEMS = ("xin", "xi", "wld", "wl", "out", "tb", "wot", "wcv", "wco")

    def inc(self, inst, sem, point, n=None):
        if n is None:
            n = 16 if any(sem.startswith(p) for p in self.DMA_SEMS) else 1
        self.kb.inc(inst, sem, n)
        if not self._fin:
            self._ctr[sem] += n
            key = (point, self.cur_par)
            assert key not in self.idx, f"dup point {key}"
            self.idx[key] = (sem, self._ctr[sem])
        return inst

    def _sem_of(self, point):
        if point.startswith("wo"):
            return "wot0"
        if point == "x":
            return "xi0"
        if point.startswith("tbl"):
            return "tb0"
        if point == "outS":
            return "out"
        if point[0] == "p":
            return "sp"
        if point[0] == "a":
            return "sa"
        return "sv"

    def wait(self, eng, jj, point):
        key = (point, jj % 2)
        if key not in self.idx:
            assert jj < 0, f"forward ref {point} at block {jj}"
            self.kb.wait(eng, self._sem_of(point), 0)
            return
        sem, k = self.idx[key]
        if jj < 0:
            self.kb.wait(eng, sem, 0)
            return
        pair = jj // 2
        if pair > 0:
            assert self._fin, f"fwd ref to {point} before peel done"
            per = self.per[sem]
        else:
            per = 0
        self.kb.wait(eng, sem, self.base[sem] + pair * per + k)


def alloc_stageB(kb, ar, st):
    ar.region_begin()
    st.xb = [ar.alloc([128, NCH, TB + 1], F32) for _ in range(2)]
    st.xbbf = ar.alloc([128, NCH, TB], BF16)
    st.wqkvR = ar.alloc([128, NCH, 3072], BF16)
    st.woT = [ar.alloc([128, NCH, 128], BF16) for _ in range(2)]
    st.ftbl = [ar.alloc([128, 4, 4, 64], F32) for _ in range(2)]
    st.qsmat = ar.alloc([128, 4, 64], F32)
    st.qsrow = ar.alloc([1, 2, 64], F32)
    st.q2sb = ar.alloc([128, 16, 64], BF16)
    st.nrm = ar.alloc([128, 2, 16], F32)
    st.qn = ar.alloc([128, 16, 64], F32)
    st.qc = ar.alloc([128, 16, 64], F32)
    st.qr = ar.alloc([128, 16, 64], F32)
    st.qtokbf = ar.alloc([128, 16, 64], BF16)
    st.ktokbf = ar.alloc([128, 16, 64], BF16)
    st.q_cm = ar.alloc([128, NCH, TB], BF16)
    st.k_cm = ar.alloc([128, NCH, 256 + TB], BF16)
    st.vT = ar.alloc([128, 6, 1024], BF16)
    st.Em = [ar.alloc([128, 2, 384], BF16) for _ in range(2)]
    st.PT = [ar.alloc([128, 2, 384], BF16) for _ in range(2)]
    st.Ybf = ar.alloc([128, NCH, TB], BF16)
    st.xoutB = ar.alloc([128, NCH, TB], F32)
    st.nmax = ar.alloc([128, 4], F32)
    st.diag = [ar.alloc([128, 2, 128], BF16) for _ in range(2)]
    return st


def emit_stageB(kb, st, wh, lv, src, dst):
    nc = kb.nc
    t, v, a, sy, g = nc.tensor, nc.vector, nc.scalar, nc.sync, nc.gpsimd
    w3h = wh["wqkvbf"]
    w4h = wh["wobf"]
    qskh = wh["qsk"]
    tblh = wh["tbl4"]

    kb.inc(sy.dma_start(out=st.wqkvR[:, :, 0:1536], in_=dview(
        w3h, lv * (D * 3072), [[3072, 128], [128 * 3072, NCH], [1, 1536]])),
        "wld", 16)
    kb.inc(sy.dma_start(out=st.wqkvR[:, :, 1536:3072], in_=dview(
        w3h, lv * (D * 3072) + 1536,
        [[3072, 128], [128 * 3072, NCH], [1, 1536]])), "wld", 16)
    kb.inc(sy.dma_start(out=st.qsrow[:], in_=dview(
        qskh, lv * 128, [[128, 1], [64, 2], [1, 64]])), "wld", 16)
    kb.wait("tensor", "wld", kb.count["wld"])
    pb = st.bank[0]
    for i, (row, sw) in enumerate([(0, 0), (0, 1), (1, 0), (1, 1)]):
        kb.wait("tensor", "sv", kb.count["sv"])
        if not sw:
            mm = t.matmul(pb[:, 0:64], st.sq8row[:], st.qsrow[0:1, row, :],
                          start=True, stop=True)
        else:
            t.matmul(pb[:, 0:32], st.sq8row[:], st.qsrow[0:1, row, 32:64],
                     start=True, stop=True)
            mm = t.matmul(pb[:, 32:64], st.sq8row[:],
                          st.qsrow[0:1, row, 0:32], start=True, stop=True)
        kb.inc(mm, "sp")
        kb.wait("vector", "sp", kb.count["sp"])
        kb.inc(v.tensor_copy(st.qsmat[:, i, :], pb[:, 0:64]), "sv")
    v.memset(st.k_cm[:, :, 0:256], 0.0)
    kb.inc(v.memset(st.vT[:, 0:2, :], 0.0), "sv")

    s = Sched(kb, ("sp", "sv", "sa", "xi0", "xi1", "tb0", "tb1",
                   "wot0", "wot1", "out"))

    def emit_blk(j, jd):
        s.blk_start(j)
        par = j % 2
        x_t = st.xb[par]

        s.wait("sync", j - 2, "vXO7")
        ldx = sy.dma_start(out=x_t[:], in_=dview(
            src, jd * TB + (GP - 1),
            [[CMW, 128], [128 * CMW, NCH], [1, TB + 1]]))
        s.inc(ldx, f"xi{par}", "x")
        s.wait("sync", j - 2, "kTok3")
        for ti in range(4):
            ldt = sy.dma_start(out=st.ftbl[par][:, :, ti, :], in_=dview(
                tblh, ti * (SEQ * 64) + jd * (TB * 64),
                [[64, 128], [128 * 64, 4], [1, 64]]))
            s.inc(ldt, f"tb{par}", f"tbl{ti}")

        s.wait("vector", j, "x")
        s.wait("vector", j - 1, "pV3")
        v.tensor_copy(st.xbbf[:, 0:4, :], x_t[:, 0:4, 1:TB + 1])
        s.inc(v.tensor_copy(st.xbbf[:, 4:8, :], x_t[:, 4:8, 0:TB]), "sv",
              "vXS")
        s.wait("vector", j, "tbl3")
        for ti in range(4):
            fi = v.tensor_mul(
                st.ftbl[par][:, :, ti, :], st.ftbl[par][:, :, ti, :],
                st.qsmat[:, ti, :].unsqueeze(1).broadcast_to([128, 4, 64]))
        s.inc(fi, "sv", "vTF")

        pq = [st.bank[0], st.bank[1]]
        pk = [st.bank[2], st.bank[3]]
        pv = [st.bank[4], st.bank[5]]
        for tt in range(4):
            s.wait("tensor", j, "vXS")
            if tt == 0:
                if B_PHASES >= 2:
                    s.wait("tensor", j - 1, "aPTE31")
            else:
                s.wait("tensor", j, f"aQ2_{tt - 1}")
                s.wait("tensor", j, f"vQE_{tt - 1}")
            for hf in range(2):
                for c in range(NCH):
                    mm = t.matmul(pq[hf][:], st.xbbf[:, c, bass.ts(tt, 128)],
                                  st.wqkvR[:, c, hf * 512:(hf + 1) * 512],
                                  start=(c == 0), stop=(c == NCH - 1))
            s.inc(mm, "sp", f"pQ{tt}")
            if tt == 0:
                if B_PHASES >= 2:
                    s.wait("tensor", j - 1, "aPTE31")
            else:
                s.wait("tensor", j, f"aK2_{tt - 1}")
                s.wait("tensor", j, f"vKE_{tt - 1}")
            for hf in range(2):
                for c in range(NCH):
                    mm = t.matmul(pk[hf][:],
                                  st.xbbf[:, c, bass.ts(tt, 128)],
                                  st.wqkvR[:, c,
                                           1024 + hf * 512:1024 + (hf + 1) * 512],
                                  start=(c == 0), stop=(c == NCH - 1))
            s.inc(mm, "sp", f"pK{tt}")
            if tt == 0:
                if B_PHASES >= 2:
                    s.wait("tensor", j - 1, "vYE31")
            else:
                s.wait("tensor", j, f"vVE_{tt - 1}")
            for hf in range(2):
                for c in range(NCH):
                    mm = t.matmul(pv[hf][:],
                                  st.xbbf[:, c, bass.ts(tt, 128)],
                                  st.wqkvR[:, c,
                                           2048 + hf * 512:2048 + (hf + 1) * 512],
                                  start=(c == 0), stop=(c == NCH - 1))
            s.inc(mm, "sp", f"pV{tt}")

            for side, pqk, tix, tok in (("Q", pq, 0, st.qtokbf),
                                        ("K", pk, 2, st.ktokbf)):
                s.wait("scalar", j, f"p{side}{tt}")
                prevn = (f"vNQ{tt}" if side == "K" else
                         (f"vNK{tt - 1}" if tt > 0 else None))
                if prevn:
                    s.wait("scalar", j, prevn)
                elif j >= 1:
                    s.wait("scalar", j - 1, "vNK3")
                a.activation(st.q2sb[:, 0:8, :], pqk[0][:], AF.Square)
                s.inc(a.activation(st.q2sb[:, 8:16, :], pqk[1][:],
                                   AF.Square), "sa", f"a{side}2_{tt}")
                s.wait("vector", j, f"a{side}2_{tt}")
                s.inc(v.reduce_sum(st.nrm[:, 0, :].unsqueeze(2), st.q2sb[:],
                                   AX.X), "sv", f"vN{side}{tt}")
                s.wait("scalar", j, f"vN{side}{tt}")
                s.inc(a.activation(st.nrm[:, 0, :], st.nrm[:, 0, :],
                                   AF.Sqrt), "sa", f"aS{side}{tt}")
                s.wait("vector", j, f"aS{side}{tt}")
                if tt > 0:
                    s.wait("vector", j, f"pT{side}{tt - 1}")
                else:
                    s.wait("vector", j - 1, f"pT{side}3")
                s.inc(v.tensor_scalar_max(st.nrm[:, 0, :], st.nrm[:, 0, :],
                                           1e-12), "sv", f"vCl{side}{tt}")
                s.wait("vector", j, f"vCl{side}{tt}")
                s.inc(v.reciprocal(st.nrm[:, 1, :], st.nrm[:, 0, :]), "sv",
                      f"vRc{side}{tt}")
                s.wait("vector", j, f"vRc{side}{tt}")
                rq0 = st.nrm[:, 1, 0:8].unsqueeze(1).broadcast_to([128, 64, 8])
                rq1 = st.nrm[:, 1, 8:16].unsqueeze(1).broadcast_to([128, 64, 8])
                qnT = st.qn[:].rearrange("p h d -> p d h")
                v.tensor_mul(qnT[:, :, 0:8],
                             pqk[0][:].rearrange("p (h d) -> p d h", d=64),
                             rq0)
                qe = v.tensor_mul(qnT[:, :, 8:16],
                                  pqk[1][:].rearrange("p (h d) -> p d h",
                                                      d=64),
                                  rq1)
                s.inc(qe, "sv", f"v{side}E_{tt}")
                cosb = st.ftbl[par][:, tt, tix, :].unsqueeze(1) \
                    .broadcast_to([128, 16, 64])
                sinb = st.ftbl[par][:, tt, tix + 1, :]
                v.tensor_mul(st.qc[:], st.qn[:], cosb)
                v.scalar_tensor_tensor(
                    st.qr[:, :, 0:32], st.qn[:, :, 32:64], -1.0,
                    sinb[:, 0:32].unsqueeze(1).broadcast_to([128, 16, 32]),
                    ALU.mult, ALU.mult)
                v.scalar_tensor_tensor(
                    st.qr[:, :, 32:64], st.qn[:, :, 0:32], 1.0,
                    sinb[:, 32:64].unsqueeze(1).broadcast_to([128, 16, 32]),
                    ALU.mult, ALU.mult)
                s.inc(v.tensor_add(tok[:], st.qc[:], st.qr[:]), "sv",
                      f"{side.lower()}Tok{tt}")

                s.wait("tensor", j, f"{side.lower()}Tok{tt}")
                if side == "Q" and tt == 0:
                    s.wait("tensor", j - 1, "aTEK3")
                elif side == "Q":
                    s.wait("tensor", j, f"aTEK{tt - 1}")
                else:
                    s.wait("tensor", j, f"aTEQ{tt}")
                for c8 in range(8):
                    bk = st.bank[6 + c8 // 4][:].bitcast(BF16)
                    mm = t.transpose(
                        bk[:, (c8 % 4) * 128:(c8 % 4 + 1) * 128],
                        tok.rearrange("p h d -> p (h d)")
                        [:, c8 * 128:(c8 + 1) * 128], st.identbf[:])
                s.inc(mm, "sp", f"pT{side}{tt}")
                s.wait("scalar", j, f"pT{side}{tt}")
                for bi in range(2):
                    srcp = st.bank[6 + bi][:].bitcast(BF16)[:, 0:512] \
                        .rearrange("p (c w) -> p c w", c=4)
                    if side == "Q":
                        dcm = st.q_cm[:, bi * 4:(bi + 1) * 4,
                                      bass.ts(tt, 128)]
                    else:
                        dcm = st.k_cm[:, bi * 4:(bi + 1) * 4,
                                      256 + tt * 128:256 + (tt + 1) * 128]
                    ev = a.copy(dcm, srcp)
                s.inc(ev, "sa", f"aTE{side}{tt}")

            s.wait("vector", j, f"pV{tt}")
            if B_PHASES >= 2:
                s.wait("vector", j - 1, "pPV31")
            v.tensor_copy(st.vT[:, 2 + tt, 0:512], pv[0][:])
            s.inc(v.tensor_copy(st.vT[:, 2 + tt, 512:1024], pv[1][:]), "sv",
                  f"vVE_{tt}")

        for c in range(NCH):
            if B_PHASES < 2:
                break
            for qt in range(4):
                u = c * 4 + qt
                upar = u % 2
                sb = [st.bank[2 * upar], st.bank[2 * upar + 1]]
                ptb = sb
                if u == 0:
                    s.wait("tensor", j, "aTEK3")
                    s.wait("tensor", j, "vVE_3")
                    s.wait("tensor", j - 1, "kH")
                    s.wait("tensor", j - 1, "vH")
                elif u >= 2:
                    s.wait("tensor", j, f"aPTE{u - 2}")
                t.matmul(sb[0][:, 0:384], st.q_cm[0:64, c, bass.ts(qt, 128)],
                         st.k_cm[0:64, c, qt * 128:qt * 128 + 384],
                         start=True, stop=True)
                s.inc(t.matmul(sb[1][:, 0:384],
                               st.q_cm[64:128, c, bass.ts(qt, 128)],
                               st.k_cm[64:128, c, qt * 128:qt * 128 + 384],
                               start=True, stop=True), "sp", f"pS{u}")
                s.wait("vector", j, f"pS{u}")
                sbp = st.psum8[:, 2 * upar:2 * upar + 2, 0:384]
                s.inc(v.tensor_reduce(st.nmax[:, 0:2].unsqueeze(2), sbp,
                                      AX.X, ALU.max, negate=True), "sv",
                      f"vMX{u}")
                s.wait("scalar", j, f"vMX{u}")
                if u >= 2:
                    s.wait("scalar", j, f"pPT{u - 2}")
                a.activation(st.Em[upar][:, 0, :], sb[0][:, 0:384], AF.Exp,
                             bias=st.nmax[:, 0:1])
                s.inc(a.activation(st.Em[upar][:, 1, :], sb[1][:, 0:384],
                                   AF.Exp, bias=st.nmax[:, 1:2]), "sa",
                      f"aEXP{u}")
                mask = st.maskB[:, 1 + qt, :] if (j == 0 and qt < 2) \
                    else st.maskB[:, 0, :]
                s.wait("vector", j, f"aEXP{u}")
                if u >= 2:
                    s.wait("vector", j, f"pPT{u - 2}")
                s.inc(v.tensor_mul(
                    st.Em[upar][:, 0:2, :], st.Em[upar][:, 0:2, :],
                    mask.unsqueeze(1).broadcast_to([128, 2, 384])), "sv",
                    f"vMk{u}")
                s.wait("vector", j, f"vMk{u}")
                s.inc(v.reduce_sum(st.nmax[:, 2:4].unsqueeze(2),
                                   st.Em[upar][:, 0:2, :], AX.X), "sv",
                      f"vSm{u}")
                s.wait("vector", j, f"vSm{u}")
                s.inc(v.reciprocal(st.nmax[:, 2:4], st.nmax[:, 2:4]), "sv",
                      f"vRr{u}")
                s.wait("vector", j, f"vRr{u}")
                v.tensor_scalar(st.diag[upar][:, 0, :], st.identbf[:],
                                st.nmax[:, 2:3], None, ALU.mult)
                s.inc(v.tensor_scalar(st.diag[upar][:, 1, :], st.identbf[:],
                                      st.nmax[:, 3:4], None, ALU.mult),
                      "sv", f"vD{u}")
                s.wait("tensor", j, f"vD{u}")
                for h in range(2):
                    for kc in range(3):
                        mm = t.matmul(
                            ptb[h][:, kc * 128:(kc + 1) * 128],
                            st.Em[upar][:, h, kc * 128:(kc + 1) * 128],
                            st.diag[upar][:, h, :], start=True, stop=True)
                s.inc(mm, "sp", f"pPT{u}")
                s.wait("scalar", j, f"pPT{u}")
                if u >= 2:
                    s.wait("scalar", j, f"pPV{u - 2}")
                s.inc(a.copy(st.PT[upar][:, 0:2, :],
                             st.psum8[:, 2 * upar:2 * upar + 2, 0:384]),
                      "sa", f"aPTE{u}")
                yb = st.bank[5][:, (u % 4) * 128:(u % 4 + 1) * 128]
                s.wait("tensor", j, f"aPTE{u}")
                if u >= 4:
                    s.wait("tensor", j, f"vYE{u - 4}")
                for h in range(2):
                    for kc in range(3):
                        mm = t.matmul(
                            yb[h * 64:(h + 1) * 64, :],
                            st.vT[:, qt + kc,
                                  (2 * c + h) * 64:(2 * c + h + 1) * 64],
                            st.PT[upar][:, h, kc * 128:(kc + 1) * 128],
                            start=(kc == 0), stop=(kc == 2),
                            tile_position=(0, 64 * h))
                s.inc(mm, "sp", f"pPV{u}")
                s.wait("vector", j, f"pPV{u}")
                if u == 0 and B_PHASES >= 3:
                    s.wait("vector", j - 1, "pWo7")
                s.inc(v.tensor_copy(st.Ybf[:, c, bass.ts(qt, 128)], yb),
                      "sv", f"vYE{u}")

        for o in range(NCH if B_PHASES >= 3 else 0):
            gidx = j * NCH + o
            if gidx >= 2:
                pj, pw = divmod(gidx - 2, NCH)
                s.wait("sync", pj, f"pWo{pw}")
            ldw = sy.dma_start(out=st.woT[o % 2][:], in_=dview(
                w4h, lv * (D * D) + o * 128,
                [[D, 128], [128 * D, NCH], [1, 128]]))
            s.inc(ldw, f"wot{o % 2}", f"wo{o}")

            po = st.bank[6 + o % 2]
            s.wait("tensor", j, "vYE31")
            s.wait("tensor", j, f"wo{o}")
            if o >= 2:
                s.wait("tensor", j, f"vXO{o - 2}")
            else:
                s.wait("tensor", j - 1, "vXO7")
            for c2 in range(NCH):
                mm = t.matmul(po[:], st.woT[o % 2][:, c2, :],
                              st.Ybf[:, c2, :], start=(c2 == 0),
                              stop=(c2 == NCH - 1))
            s.inc(mm, "sp", f"pWo{o}")
            s.wait("vector", j, f"pWo{o}")
            if o == 0:
                s.wait("vector", j - 1, "outS")
            s.inc(v.tensor_add(st.xoutB[:, o, :], po[:],
                               x_t[:, o, 1:TB + 1]), "sv", f"vXO{o}")

        if B_PHASES >= 2:
            s.wait("vector", j, "pS31")
        s.inc(v.tensor_copy(st.k_cm[:, :, 0:256],
                            st.k_cm[:, :, TB:TB + 256]), "sv", "kH")
        if B_PHASES >= 2:
            s.wait("vector", j, "pPV31")
        s.inc(v.tensor_copy(st.vT[:, 0:2, :], st.vT[:, 4:6, :]), "sv", "vH")

        if B_PHASES >= 3:
            s.wait("sync", j, "vXO7")
        s.inc(sy.dma_start(out=dview(dst, jd * TB + GP,
                                     [[CMW, 128], [128 * CMW, NCH], [1, TB]]),
                           in_=st.xoutB[:]), "out", "outS")

    def emit_pair(jp, jv):
        for p in range(2):
            j = jp * 2 + p
            jd = j if jv is None else jv * 2 + p
            emit_blk(j, jd)

    run_streamed(kb, emit_pair)


def host_tables():
    half = np.arange(0, HD, 2, dtype=np.float64)
    inv_freq = 1.0 / 10000.0 ** (half / HD)
    tpos = np.arange(SEQ, dtype=np.float64)
    freqs = tpos[:, None] * inv_freq[None, :]
    freqs = np.concatenate([freqs, freqs], -1)
    base_scale = (half + 0.4 * HD) / (1.4 * HD)
    power = (tpos - SEQ // 2) / (WIN // 2)
    scale = base_scale[None, :] ** power[:, None]
    scale = np.concatenate([scale, scale], -1)
    cosS = np.cos(freqs) * scale
    sinS = np.sin(freqs) * scale
    cosI = np.cos(freqs) / scale
    sinI = np.sin(freqs) / scale
    return np.stack([cosS, sinS, cosI, sinI]).astype(np.float32)


def host_masks():
    i = np.arange(128)[:, None]
    cc = np.arange(384)[None, :]
    rel = np.zeros((3, 128, 384), np.float32)
    rel[0] = ((cc >= i) & (cc <= i + 256)).astype(np.float32)
    for qt in range(2):
        keyg = qt * 128 - 256 + cc
        rel[1 + qt] = (((cc >= i) & (cc <= i + 256)) &
                       (keyg >= 0)).astype(np.float32)
    return rel


def build_dev_B(races=True):
    ctx = ExitStack()
    nc = bass.Bass("TRN2", target_bir_lowering=False,
                   detect_race_conditions=races)
    kb = Kb(nc, ctx)
    with ctx:
        xin_h = nc.dram_tensor("xin_cm", [D, CMW], F32, kind="ExternalInput")
        xout_h = nc.dram_tensor("xout_cm", [D, CMW], F32,
                                kind="ExternalOutput")
        wqkv_h = nc.dram_tensor("wqkv", [NL, D, 3072], F32,
                                kind="ExternalInput")
        wo_h = nc.dram_tensor("wo", [NL, D, D], F32, kind="ExternalInput")
        qsk_h = nc.dram_tensor("qsk", [NL, 2, 64], F32, kind="ExternalInput")
        tbl_h = nc.dram_tensor("tbl4", [4, SEQ, 64], F32,
                               kind="ExternalInput")
        zg_h = nc.dram_tensor("zguard", [128, 64], F32, kind="ExternalInput")
        msk_h = nc.dram_tensor("maskP", [3, 128, 384], F32,
                               kind="ExternalInput")
        id_h = nc.dram_tensor("identf", [128, 128], F32, kind="ExternalInput")
        cst_h = nc.dram_tensor("csts", [128, CONSTW], F32,
                               kind="ExternalInput")
        wqkvbf_h = nc.dram_tensor("wqkvbf", [NL, D, 3072], BF16)
        wobf_h = nc.dram_tensor("wobf", [NL, D, D], BF16)
        wh = {"wqkvbf": wqkvbf_h, "wobf": wobf_h, "qsk": qsk_h,
              "tbl4": tbl_h}
        ar = Arena(kb)
        st = alloc_common(kb, ar)
        emit_const_extras(kb, st, ar, cst_h, msk_h, id_h)
        emit_const_init(kb, st, cst_h)
        emit_w_convert(kb, ar, [(wqkv_h, wqkvbf_h, NL * D * 3072),
                                (wo_h, wobf_h, NL * D * D)])
        alloc_stageB(kb, ar, st)
        print(f"[kernel] arena high-water: {ar.hw} bytes/partition")
        kb.stage_reset(drain=("xin", "wld"))
        emit_stageB(kb, st, wh, 0, xin_h, xout_h)
        kb.wait("sync", "out", kb.count["out"])
        dbg_w = nc.dram_tensor("dbg_w", [256, 1024], BF16,
                               kind="ExternalOutput")
        kb.inc(nc.sync.dma_start(out=dview(dbg_w, 0, [[1024, 128], [1, 1024]]),
                                 in_=dview(wqkvbf_h, 256 * 3072 + 2048,
                                           [[3072, 128], [1, 1024]])),
               "out", 16)
        kb.inc(nc.sync.dma_start(
            out=dview(dbg_w, 128 * 1024, [[1024, 128], [1, 1024]]),
            in_=dview(wqkvbf_h, 384 * 3072 + 2048, [[3072, 128], [1, 1024]])),
            "out", 16)
        kb.wait("sync", "out", kb.count["out"])
        dbg_q = nc.dram_tensor("dbg_q", [128, NCH * TB], F32,
                               kind="ExternalOutput")
        dbg_k = nc.dram_tensor("dbg_k", [128, NCH * (256 + TB)], F32,
                               kind="ExternalOutput")
        dbg_v = nc.dram_tensor("dbg_v", [128, 6 * 1024], F32,
                               kind="ExternalOutput")
        dbg_y = nc.dram_tensor("dbg_y", [128, NCH * TB], F32,
                               kind="ExternalOutput")
        nc.all_engine_barrier()
        v = nc.vector
        dscr = st.xb[0]
        flat = dscr.rearrange("p a b -> p (a b)")
        dbg_qn = nc.dram_tensor("dbg_qn", [128, 1024], F32,
                                kind="ExternalOutput")
        dbg_sq = nc.dram_tensor("dbg_sq", [128, 1024], F32,
                                kind="ExternalOutput")
        dbg_nr = nc.dram_tensor("dbg_nr", [128, 32], F32,
                                kind="ExternalOutput")
        dbg_qt = nc.dram_tensor("dbg_qt", [128, 1024], F32,
                                kind="ExternalOutput")
        dbg_kt = nc.dram_tensor("dbg_kt", [128, 1024], F32,
                                kind="ExternalOutput")
        for name, buf, width in (
                ("dbg_qn", st.qn.rearrange("p a b -> p (a b)"), 1024),
                ("dbg_sq", st.q2sb.rearrange("p a b -> p (a b)"), 1024),
                ("dbg_nr", st.nrm.rearrange("p a b -> p (a b)"), 32),
                ("dbg_qt", st.qtokbf.rearrange("p a b -> p (a b)"), 1024),
                ("dbg_kt", st.ktokbf.rearrange("p a b -> p (a b)"), 1024),
                ("dbg_q", st.q_cm.rearrange("p a b -> p (a b)"), NCH * TB),
                                 ("dbg_k", st.k_cm.rearrange("p a b -> p (a b)"), NCH * (256 + TB)),
                                 ("dbg_v", st.vT.rearrange("p a b -> p (a b)"), 6 * 1024),
                                 ("dbg_y", st.Ybf.rearrange("p a b -> p (a b)"), NCH * TB)):
            h = {"dbg_q": dbg_q, "dbg_k": dbg_k, "dbg_v": dbg_v,
                 "dbg_y": dbg_y, "dbg_qn": dbg_qn, "dbg_sq": dbg_sq,
                 "dbg_nr": dbg_nr, "dbg_qt": dbg_qt, "dbg_kt": dbg_kt}[name]
            for c0 in range(0, width, 4096):
                w = min(4096, width - c0)
                kb.inc(v.tensor_copy(flat[:, 0:w], buf[:, c0:c0 + w]), "sv")
                kb.wait("sync", "sv", kb.count["sv"])
                kb.inc(nc.sync.dma_start(
                    out=dview(h, c0, [[width, 128], [1, w]]),
                    in_=flat[:, 0:w]), "out", 16)
                kb.wait("vector", "out", kb.count["out"])
        kb.wait("sync", "out", kb.count["out"])
        kb.inc(nc.sync.dma_start(
            out=dview(xout_h, 0, [[CMW, 128], [128 * CMW, NCH], [1, GP]]),
            in_=dview(xin_h, 0, [[CMW, 128], [128 * CMW, NCH], [1, GP]])),
            "out", 16)
        kb.wait("sync", "out", kb.count["out"])
    return nc


def emit_const_extras(kb, st, ar, cst_h, msk_h, id_h, pat_h=None):
    nc = kb.nc
    st.sq8row = ar.perm([1, 128], F32)
    st.identbf = ar.perm([128, 128], BF16)
    st.identf = ar.perm([128, 128], F32)
    st.maskB = ar.perm([128, 3, 384], BF16)
    kb.inc(nc.sync.dma_start(out=st.sq8row[:], in_=dview(
        cst_h, 2 * CONSTW + 2, [[CONSTW, 1], [1, 128]])), "xin", 16)
    kb.inc(nc.sync.dma_start(out=st.identf[:], in_=dview(
        id_h, 0, [[128, 128], [1, 128]])), "xin", 16)
    kb.inc(nc.gpsimd.dma_start(out=st.identbf[:], in_=dview(
        id_h, 0, [[128, 128], [1, 128]])), "wcv0", 16)
    for mi in range(3):
        kb.inc(nc.gpsimd.dma_start(out=st.maskB[:, mi, :], in_=dview(
            msk_h, mi * 128 * 384, [[384, 128], [1, 384]])), "wcv0", 16)
    if pat_h is not None:
        st.P2pat = ar.perm([128, 128], BF16)
        st.P1apat = ar.perm([128, 128], BF16)
        st.P1bpat = ar.perm([128, 128], BF16)
        st.trilm = ar.perm([128, 128], BF16)
        st.pairpat = ar.perm([128, 8, 16], BF16)
        st.expdpat = ar.perm([16, 8, 128], BF16)
        offs = 0
        for ap, rows, cols in ((st.P2pat, 128, 128), (st.P1apat, 128, 128),
                               (st.P1bpat, 128, 128), (st.trilm, 128, 128),
                               (st.pairpat, 128, 8 * 16),
                               (st.expdpat, 16, 8 * 128)):
            dst = ap if len(ap.shape) == 2 else ap.rearrange(
                "p a b -> p (a b)")
            kb.inc(nc.sync.dma_start(out=dst, in_=dview(
                pat_h, offs, [[cols, rows], [1, cols]])), "xin", 16)
            offs += rows * cols
    kb.wait("sync", "xin", kb.count["xin"])
    kb.wait("sync", "wcv0", kb.count["wcv0"])


def alloc_stageS1(kb, ar, st):
    ar.region_begin()
    st.x0t = [ar.alloc([128, 1024], F32) for _ in range(2)]
    st.x0cm = ar.alloc([128, NCH, TB], F32)
    st.statc = ar.alloc([128, 4, 2], F32)
    st.z1 = ar.alloc([128, NCH, TB], BF16)
    st.wqp = ar.alloc([128, NCH, 512], BF16)
    st.wkp = ar.alloc([128, NCH, 512], BF16)
    st.wvr = ar.alloc([128, NCH, 1024], BF16)
    st.wor = ar.alloc([128, NCH, 1024], BF16)
    st.colsQ = ar.alloc([1, 512], BF16)
    st.colsK = ar.alloc([1, 512], BF16)
    st.colsV = ar.alloc([1, 1024], BF16)
    st.q1 = ar.alloc([128, 4, TB], BF16)
    st.k1 = ar.alloc([128, 4, TB], BF16)
    st.kTb = ar.alloc([128, 4, TB], BF16)
    st.vb = ar.alloc([128, 4, 1024], BF16)
    st.acol = ar.alloc([128, 4], F32)
    st.qf0 = ar.alloc([128, TB], BF16)
    st.qf1 = ar.alloc([128, TB], BF16)
    st.qf2 = ar.alloc([32, TB], BF16)
    st.kfT = ar.alloc([128, 4, 288], BF16)
    st.M2e = ar.alloc([128, TB], BF16)
    st.tsb = [ar.alloc([128, 128], F32) for _ in range(2)]
    st.Asb = [ar.alloc([128, 128], BF16) for _ in range(2)]
    st.Sst = ar.alloc([128, 16, 3, 64], F32)
    st.Sstbf = ar.alloc([128, 16, 3, 64], BF16)
    st.y2bf = ar.alloc([128, TB], BF16)
    st.Y1 = ar.alloc([128, NCH, TB], BF16)
    st.att = ar.alloc([128, NCH, TB], F32)
    st.x1sb = ar.alloc([128, NCH, TB], F32)
    st.r16 = ar.alloc([16, TB], F32)
    st.r16b = ar.alloc([16, TB], BF16)
    st.statc2 = ar.alloc([128, 2], F32)
    st.sqscr = ar.alloc([128, 512], BF16)
    return st


def emit_stageS1(kb, st, wh, x0h, x1h, xi0h):
    nc = kb.nc
    t, v, a, sy, g = nc.tensor, nc.vector, nc.scalar, nc.sync, nc.gpsimd
    wq_h, wk_h, wv_h, wo_h = wh["wqpbf"], wh["wkpbf"], wh["wvtbf"], wh["wotbf"]
    gw_h = wh["gw"]

    for (dst_t, src_t, cols) in ((st.wqp, wq_h, 512), (st.wkp, wk_h, 512),
                                 (st.wvr, wv_h, 1024), (st.wor, wo_h, 1024)):
        kb.inc(sy.dma_start(out=dst_t[:], in_=dview(
            src_t, 0, [[cols, 128], [128 * cols, NCH], [1, cols]])),
            "wld", 16)
    kb.inc(sy.dma_start(out=st.acol[0:64, 0:1], in_=dview(
        gw_h, 0, [[1, 64], [1, 1]])), "wld", 16)
    kb.inc(sy.dma_start(out=st.acol[64:128, 0:1], in_=dview(
        gw_h, 0, [[1, 64], [1, 1]])), "wld", 16)
    kb.wait("vector", "wld", kb.count["wld"])
    kb.wait("tensor", "wld", kb.count["wld"])
    for c in range(NCH):
        kb.inc(v.tensor_scalar(st.wor[:, c, :], st.wor[:, c, :],
                               st.acol[:, 0:1], None, ALU.mult), "sv")
    kb.wait("tensor", "sv", kb.count["sv"])
    for (wt, colsb, cols) in ((st.wqp, st.colsQ, 512),
                              (st.wkp, st.colsK, 512),
                              (st.wvr, st.colsV, 1024)):
        for half in range(cols // 512):
            for c in range(NCH):
                mm = t.matmul(st.bank[0][0:1, 0:512], st.onescol_bf[:],
                              wt[:, c, half * 512:(half + 1) * 512],
                              start=(c == 0), stop=(c == NCH - 1))
            kb.inc(mm, "sp")
            kb.wait("vector", "sp", kb.count["sp"])
            kb.inc(v.tensor_copy(colsb[0:1, half * 512:(half + 1) * 512],
                                 st.bank[0][0:1, 0:512]), "sv")
            kb.wait("tensor", "sv", kb.count["sv"])
    kb.inc(v.memset(st.Sst[:], 0.0), "sv")
    kb.inc(v.memset(st.Sstbf[:], 0.0), "sv")
    kb.inc(v.memset(st.kfT[:, :, 272:288], 0.0), "sv")
    kb.wait("sync", "sv", kb.count["sv"])
    kb.inc(sy.dma_start(out=st.qf2[16:17, :], in_=dview(
        wh["c32row"], 0, [[TB, 1], [1, TB]])), "wld", 16)
    kb.inc(sy.dma_start(out=st.kfT[:, :, 272:273].squeeze(2), in_=dview(
        wh["onecol128"], 0, [[4, 128], [1, 4]]),
        allow_slow_non_contiguous=True), "wld", 16)
    kb.inc(sy.dma_start(out=st.brow2bf[1:2, :],
                        in_=st.cst_ones_row[0:1, 0:TB]), "wld", 16)
    kb.wait("sync", "wld", kb.count["wld"])

    s = Sched(kb, ("sp", "sv", "sa", "xi0", "xi1", "out"))
    P1 = [(it, jt) for it in range(4) for jt in range(it + 1)]

    def emit_blk(j, jd):
        s.blk_start(j)
        sx = st.rows_sb[0:1, 0:512]
        sxx = st.rows_sb[0:1, 512:1024]
        u = st.rows_sb[0:1, 1024:1536]
        var = st.rows_sb[0:1, 1536:2048]

        for tt in range(4):
            tpar = tt % 2
            gidx = j * 4 + tt
            if gidx >= 2:
                pj, ptt = divmod(gidx - 2, 4)
                s.wait("sync", pj, f"vST{ptt}")
                s.wait("sync", pj, f"aSQb{ptt}")
                s.wait("sync", pj, f"pXT{ptt}")
            ldx = sy.dma_start(out=st.x0t[tpar][:], in_=dview(
                x0h, (jd * TB + tt * 128) * D, [[D, 128], [1, D]]))
            s.inc(ldx, f"xi{tpar}", f"x{tt}")
            s.wait("tensor", j, f"x{tt}")
            if gidx >= 1:
                pj, ptt = divmod(gidx - 1, 4)
                s.wait("tensor", pj, f"aXT{ptt}")
            if j >= 1 and tt == 0:
                s.wait("tensor", j - 1, "aWo7")
            for c8 in range(8):
                bk = st.bank[6 + c8 // 4]
                mm = t.transpose(bk[:, (c8 % 4) * 128:(c8 % 4 + 1) * 128],
                                 st.x0t[tpar][:, c8 * 128:(c8 + 1) * 128],
                                 st.identf[:])
            s.inc(mm, "sp", f"pXT{tt}")
            s.wait("scalar", j, f"pXT{tt}")
            if j >= 1 and tt == 0:
                s.wait("scalar", j - 1, "vT1")
            for bi in range(2):
                ev = a.copy(st.x0cm[:, bi * 4:(bi + 1) * 4,
                                    tt * 128:(tt + 1) * 128],
                            st.bank[6 + bi][:, 0:512]
                            .rearrange("p (c w) -> p c w", c=4))
            s.inc(ev, "sa", f"aXT{tt}")
            s.wait("scalar", j, f"x{tt}")
            if j >= 1:
                s.wait("scalar", j - 1, "pSR")
            s.inc(a.activation(st.sqscr[:], st.x0t[tpar][:, 0:512],
                               AF.Square, accum_out=st.statc2[:, 0:1]), "sa",
                  f"aSQa{tt}")
            s.inc(a.activation(st.sqscr[:], st.x0t[tpar][:, 512:1024],
                               AF.Square, accum_out=st.statc2[:, 1:2]),
                  "sa", f"aSQb{tt}")
            s.wait("vector", j, f"x{tt}")
            if j >= 1:
                s.wait("vector", j - 1, "pSR")
            s.inc(v.reduce_sum(st.statc[:, tt, 0:1], st.x0t[tpar][:], AX.X),
                  "sv", f"vSTa{tt}")
            s.wait("vector", j, f"aSQb{tt}")
            s.inc(v.tensor_add(st.statc[:, tt, 1:2], st.statc2[:, 0:1],
                               st.statc2[:, 1:2]), "sv", f"vST{tt}")

        for tt in range(4):
            s.wait("tensor", j, f"vST{tt}")
        if j >= 1:
            s.wait("tensor", j - 1, "vRM0")
            s.wait("tensor", j - 1, "vU0a")
            s.wait("tensor", j - 1, "vXI")
        for tt in range(4):
            t.matmul(st.bank[0][0:1, tt * 128:(tt + 1) * 128],
                     st.statc[:, tt, 0:1], st.identf[:],
                     start=True, stop=True)
        for tt in range(4):
            mm = t.matmul(st.ps_ea[0:1, tt * 128:(tt + 1) * 128],
                          st.statc[:, tt, 1:2], st.identf[:],
                          start=True, stop=True)
        s.inc(mm, "sp", "pSR")
        s.wait("vector", j, "pSR")
        v.tensor_copy(sx, st.bank[0][0:1, 0:512])
        s.inc(v.tensor_copy(sxx, st.ps_ea[0:1, 0:512]), "sv", "vRM0")
        s.wait("vector", j, "vRM0")
        s.inc(v.scalar_tensor_tensor(u, sx, 1.0 / D, sx, ALU.mult, ALU.mult),
              "sv", "vRM1")
        s.wait("vector", j, "vRM1")
        s.inc(v.tensor_sub(var, sxx, u), "sv", "vRM2")
        s.wait("scalar", j, "vRM2")
        s.inc(a.activation(u, var, AF.Sqrt, bias=st.C("lneps", 1),
                           scale=st.C("invD", 1)), "sa", "aRS")
        s.wait("vector", j, "aRS")
        s.inc(v.reciprocal(st.arow[0:1, :], u), "sv", "vRM3")
        s.wait("vector", j, "vRM3")
        s.inc(v.scalar_tensor_tensor(st.brow2[0:1, :], sx, -1.0 / D,
                                     st.arow[0:1, :], ALU.mult, ALU.mult),
              "sv", "vRM4")
        s.wait("vector", j, "vRM4")
        s.inc(v.tensor_copy(st.brow2bf[0:1, :], st.brow2[0:1, :]), "sv",
              "vRM")
        s.wait("tensor", j, "vRM")
        if j >= 1:
            s.wait("tensor", j - 1, "vXI")
        s.inc(t.matmul(st.ps_ea[:], st.ones1_f32[:], st.arow[0:1, :],
                       start=True, stop=True), "sp", "pEA")
        if j >= 1:
            s.wait("tensor", j - 1, "vAC")
        for tt in range(4):
            mm = t.matmul(st.bank[4][:, tt:tt + 1],
                          st.arow[0:1, bass.ts(tt, 128)],
                          st.cst[0:1, 0:1], start=True, stop=True)
        s.inc(mm, "sp", "pAC")
        s.wait("vector", j, "pAC")
        s.inc(v.tensor_copy(st.acol[:], st.bank[4][:, 0:4]), "sv", "vAC")
        s.wait("vector", j, "pEA")
        s.wait("vector", j, "aXT3")
        if j >= 1:
            s.wait("vector", j - 1, "pVl")
        for c in range(NCH):
            zi = v.tensor_mul(st.z1[:, c, :], st.x0cm[:, c, :], st.ps_ea[:])
        s.inc(zi, "sv", "vZ")

        for mt4 in range(4):
            pp = st.bank[2 + mt4 % 2]
            s.wait("tensor", j, "vZ")
            if mt4 == 0 and j >= 1:
                s.wait("tensor", j - 1, "vYE15")
            if mt4 == 1 and j >= 1:
                s.wait("tensor", j - 1, "vSN15")
            if mt4 >= 2:
                s.wait("tensor", j, f"vQ{mt4 - 2}")
            for c in range(NCH):
                t.matmul(pp[:], st.wqp[:, c, bass.ts(mt4, 128)],
                         st.z1[:, c, :], start=(c == 0), stop=False)
            mm = t.matmul(pp[:], st.colsQ[0:1, bass.ts(mt4, 128)],
                          st.brow2bf[0:1, :], start=False, stop=True)
            s.inc(mm, "sp", f"pQ{mt4}")
            s.wait("vector", j, f"pQ{mt4}")
            s.inc(v.tensor_copy(st.q1[:, mt4, :], pp[:]), "sv",
                  f"vQ{mt4}")
        for mt4 in range(4):
            pp = st.bank[2 + mt4 % 2]
            s.wait("tensor", j, f"vK{mt4 - 2}" if mt4 >= 2
                   else f"vQ{mt4 + 2}")
            for c in range(NCH):
                t.matmul(pp[:], st.wkp[:, c, bass.ts(mt4, 128)],
                         st.z1[:, c, :], start=(c == 0), stop=False)
            mm = t.matmul(pp[:], st.colsK[0:1, bass.ts(mt4, 128)],
                          st.brow2bf[0:1, :], start=False, stop=True)
            s.inc(mm, "sp", f"pK{mt4}")
            s.wait("vector", j, f"pK{mt4}")
            s.inc(v.tensor_copy(st.k1[:, mt4, :], pp[:]), "sv",
                  f"vK{mt4}")
        for tt in range(4):
            ppA = st.bank[2 + tt % 2]
            ppB = st.bank[2 + (tt + 1) % 2]
            s.wait("tensor", j, f"vVE{tt - 1}" if tt >= 1 else "vK2")
            for c in range(NCH):
                t.matmul(ppA[:], st.z1[:, c, bass.ts(tt, 128)],
                         st.wkp[:, c, :], start=(c == 0), stop=False)
            mm = t.matmul(ppA[:], st.brow2bf[0:1, bass.ts(tt, 128)],
                          st.colsK[0:1, :], start=False, stop=True)
            s.inc(mm, "sp", f"pKT{tt}")
            s.wait("vector", j, f"pKT{tt}")
            s.wait("vector", j, "vAC")
            s.inc(v.tensor_copy(st.kTb[:, tt, :], ppA[:]),
                  "sv", f"vKT{tt}")
            s.wait("tensor", j, f"vKT{tt - 1}" if tt >= 1 else "vK3")
            for half in range(2):
                if half == 1:
                    s.wait("tensor", j, f"vVEh{tt}")
                for c in range(NCH):
                    t.matmul(ppB[:], st.z1[:, c, bass.ts(tt, 128)],
                             st.wvr[:, c, half * 512:(half + 1) * 512],
                             start=(c == 0), stop=False)
                mm = t.matmul(ppB[:], st.brow2bf[0:1, bass.ts(tt, 128)],
                              st.colsV[0:1, half * 512:(half + 1) * 512],
                              start=False, stop=True)
                s.inc(mm, "sp", f"pV{tt}_{half}")
                s.wait("vector", j, f"pV{tt}_{half}")
                s.inc(v.tensor_copy(
                    st.vb[:, tt, half * 512:(half + 1) * 512], ppB[:]), "sv",
                    f"vVEh{tt}" if half == 0 else f"vVE{tt}")
        s.inc(v.tensor_copy(st.statc2[:, 0:1], st.statc2[:, 0:1]), "sv",
              "pVl")

        for h in range(16):
            ht, hr = h // 4, 32 * (h % 4)
            qh = st.q1[hr:hr + 32, ht, :]
            kh = st.k1[hr:hr + 32, ht, :]
            hpar = h % 2
            s.wait("tensor", j, f"vQ{ht}")
            s.wait("tensor", j, f"vK{ht}")
            s.wait("tensor", j, "vVE3")
            if h >= 1:
                s.wait("tensor", j, f"vF{h - 1}")
            tp = (hr, 0)
            t.matmul(st.bank[4][:], st.P2pat[hr:hr + 32, :], qh,
                     start=True, stop=True, tile_position=tp)
            s.inc(t.matmul(st.bank[5][:], st.P1apat[hr:hr + 32, :], qh,
                           start=True, stop=True, tile_position=tp), "sp",
                  f"pF1{h}")
            s.wait("vector", j, f"pF1{h}")
            if h >= 1:
                s.wait("vector", j, f"pY{h - 1}_9")
            s.inc(v.tensor_copy(st.M2e[:], st.bank[4][:]), "sv", f"vM2{h}")
            s.wait("vector", j, f"vM2{h}")
            s.inc(v.tensor_mul(st.qf0[:], st.bank[5][:], st.M2e[:]), "sv",
                  f"vF0{h}")
            s.wait("tensor", j, f"vF0{h}")
            s.inc(t.matmul(st.bank[5][:], st.P1bpat[hr:hr + 32, :], qh,
                           start=True, stop=True, tile_position=tp), "sp",
                  f"pF2{h}")
            s.wait("vector", j, f"pF2{h}")
            v.tensor_mul(st.qf1[:], st.bank[5][:], st.M2e[:])
            s.inc(v.tensor_scalar(st.qf2[0:16, :], qh[0:16, :], 16.0, None,
                                  ALU.mult), "sv", f"vF{h}")
            for tt in range(4):
                kslice = st.kTb[:, tt, 32 * h:32 * h + 16]
                i0 = kslice.unsqueeze(2).broadcast_to([128, 16, 16])
                i1 = kslice.unsqueeze(1).broadcast_to([128, 16, 16])
                s.wait("vector", j, f"vKT{tt}")
                if h >= 1 and tt == 0:
                    s.wait("vector", j, f"pSN{h - 1}")
                v.scalar_tensor_tensor(
                    st.kfT[:, tt, 0:256].rearrange("p (a b) -> p a b", a=16),
                    i0, X2S, i1, ALU.mult, ALU.mult)
                fi = v.tensor_scalar(st.kfT[:, tt, 256:272], kslice, 0.5,
                                     None, ALU.mult)
            s.inc(fi, "sv", f"vKF{h}")
            yps = st.bank[2]
            s.wait("tensor", j, "vVE3")
            s.wait("tensor", j, f"vF{h}")
            if h >= 2:
                s.wait("tensor", j, f"vYE{h - 1}" if h % 2 == 0 else
                       f"vYE{h - 2}")
            t.matmul(yps[64 * hpar:64 * hpar + 64, :],
                     st.Sstbf[0:128, h, 0, :], st.qf0[:],
                     start=True, stop=False, tile_position=(0, 64 * hpar))
            t.matmul(yps[64 * hpar:64 * hpar + 64, :],
                     st.Sstbf[0:128, h, 1, :], st.qf1[:],
                     start=False, stop=False, tile_position=(0, 64 * hpar))
            t.matmul(yps[64 * hpar:64 * hpar + 64, :],
                     st.Sstbf[0:17, h, 2, :], st.qf2[0:17, :],
                     start=False, stop=False, tile_position=(0, 64 * hpar))
            for pi, (it, jt) in enumerate(P1):
                sb = st.bank[6 + pi % 2]
                s.wait("tensor", j, f"vKF{h}")
                if pi >= 2:
                    s.wait("tensor", j, f"aPL{h}_{pi - 2}")
                elif h >= 1:
                    s.wait("tensor", j, f"aPL{h - 1}_{8 + pi}")
                else:
                    s.wait("tensor", j, "aXT3")
                    if j >= 1:
                        s.wait("tensor", j - 1, "aWo7")
                mm = t.matmul(sb[:, 0:128], kh[:, bass.ts(jt, 128)],
                              qh[:, bass.ts(it, 128)], start=True, stop=True,
                              tile_position=tp)
                s.inc(mm, "sp", f"pS{h}_{pi}")
                s.wait("scalar", j, f"pS{h}_{pi}")
                if pi >= 2:
                    s.wait("scalar", j, f"vA{h}_{pi - 2}")
                elif h >= 1:
                    s.wait("scalar", j, f"vA{h - 1}_{8 + pi}")
                s.inc(a.activation(st.tsb[pi % 2][:], sb[:, 0:128],
                                   AF.Square, bias=st.C("poly_b")), "sa",
                      f"aPL{h}_{pi}")
                s.wait("vector", j, f"aPL{h}_{pi}")
                if pi >= 2:
                    s.wait("vector", j, f"pY{h}_{pi - 2}")
                elif h >= 1:
                    s.wait("vector", j, f"pY{h - 1}_{8 + pi}")
                if it == jt:
                    va = v.scalar_tensor_tensor(st.Asb[pi % 2][:],
                                                st.tsb[pi % 2][:], 16.0,
                                                st.trilm[:], ALU.add,
                                                ALU.mult)
                else:
                    va = v.tensor_scalar_add(st.Asb[pi % 2][:],
                                             st.tsb[pi % 2][:], 16.0)
                s.inc(va, "sv", f"vA{h}_{pi}")
                s.wait("tensor", j, f"vA{h}_{pi}")
                mm = t.matmul(yps[64 * hpar:64 * hpar + 64,
                                  bass.ts(it, 128)],
                              st.vb[:, jt, 64 * h:64 * h + 64],
                              st.Asb[pi % 2][:], start=False,
                              stop=(pi == len(P1) - 1),
                              tile_position=(0, 64 * hpar))
                s.inc(mm, "sp", f"pY{h}_{pi}")
            s.wait("tensor", j, f"vKF{h}")
            if h >= 1:
                s.wait("tensor", j, f"vSN{h - 1}")
            for f3 in range(3):
                rows = (128, 128, 17)[f3]
                lhs = (lambda tt_: st.kfT[:, tt_, 128 * f3:128 * f3 + rows]
                       if f3 < 2 else st.kfT[:, tt_, 256:273])
                for tt in range(4):
                    mm = t.matmul(
                        st.bank[3][0:rows, 64 * f3:64 * f3 + 64],
                        lhs(tt), st.vb[:, tt, 64 * h:64 * h + 64],
                        start=(tt == 0), stop=(tt == 3))
            s.inc(mm, "sp", f"pSN{h}")
            s.wait("vector", j, f"pSN{h}")
            s.wait("vector", j, f"pY{h}_9")
            for f3 in range(3):
                rows = (128, 128, 17)[f3]
                ad = v.tensor_add(st.Sst[0:rows, h, f3, :],
                                  st.Sst[0:rows, h, f3, :],
                                  st.bank[3][0:rows, 64 * f3:64 * f3 + 64])
            s.inc(ad, "sv", f"vSA{h}")
            s.wait("vector", j, f"vSA{h}")
            for f3 in range(3):
                rows = (128, 128, 17)[f3]
                cp = v.tensor_copy(st.Sstbf[0:rows, h, f3, :],
                                   st.Sst[0:rows, h, f3, :])
            s.inc(cp, "sv", f"vSN{h}")
            if hpar == 1:
                s.wait("vector", j, f"pY{h}_9")
                s.wait("vector", j, f"pY{h - 1}_9")
                if j >= 1 and h == 1:
                    s.wait("vector", j - 1, "pTU2")
                s.inc(v.tensor_copy(st.Y1[:, h // 2, :], yps[:]), "sv",
                      f"vYE{h}")

        for p8 in range(8):
            s.wait("scalar", j, f"vYE{2 * p8 + 1}")
            if p8 >= 1:
                s.wait("scalar", j, f"pRA{p8 - 1}")
            s.inc(a.activation(st.y2bf[:], st.Y1[:, p8, :], AF.Square),
                  "sa", f"aY2{p8}")
            s.wait("tensor", j, f"aY2{p8}")
            if p8 == 0:
                s.wait("tensor", j, "pSR")
            mm = t.matmul(st.bank[0][0:16, :], st.pairpat[:, p8, :],
                          st.y2bf[:], start=(p8 == 0), stop=(p8 == 7))
            s.inc(mm, "sp", f"pRA{p8}")
        s.wait("scalar", j, "pRA7")
        s.inc(a.activation(st.r16[:], st.bank[0][0:16, :], AF.Sqrt,
                           bias=st.C("rmseps", 16), scale=st.C("invHD", 16)),
              "sa", "aRR")
        s.wait("vector", j, "aRR")
        s.inc(v.reciprocal(st.r16[:], st.r16[:]), "sv", "vRR0")
        s.wait("vector", j, "vRR0")
        s.inc(v.tensor_copy(st.r16b[:], st.r16[:]), "sv", "vRR")
        for p8 in range(8):
            s.wait("tensor", j, "vRR")
            s.wait("tensor", j, f"vYN{p8 - 1}" if p8 >= 1 else "vF15")
            mm = t.matmul(st.bank[5][:], st.expdpat[:, p8, :], st.r16b[:],
                          start=True, stop=True)
            s.inc(mm, "sp", f"pRE{p8}")
            s.wait("vector", j, f"pRE{p8}")
            s.inc(v.tensor_mul(st.Y1[:, p8, :], st.Y1[:, p8, :],
                               st.bank[5][:]), "sv", f"vYN{p8}")

        for o in range(NCH):
            po = st.bank[6 + o % 2]
            s.wait("tensor", j, "vYN7")
            if o >= 2:
                s.wait("tensor", j, f"aWo{o - 2}")
            for c in range(NCH):
                mm = t.matmul(po[:], st.wor[:, c, o * 128:(o + 1) * 128],
                              st.Y1[:, c, :], start=(c == 0),
                              stop=(c == NCH - 1))
            s.inc(mm, "sp", f"pWo{o}")
            s.wait("scalar", j, f"pWo{o}")
            if o < 2:
                s.wait("scalar", j - 1, "outI")
                s.wait("scalar", j - 1, "vXI")
            s.inc(a.copy(st.att[:, o, :], po[:]), "sa", f"aWo{o}")
        for c in range(NCH):
            s.wait("scalar", j, f"aWo{c}")
            if c >= 1:
                s.wait("scalar", j, f"pTS1_{c - 1}")
            elif j >= 1:
                s.wait("scalar", j - 1, "vU0a")
            s.inc(a.activation(st.y2bf[:], st.att[:, c, :], AF.Square),
                  "sa", f"aTS1_{c}")
            s.wait("tensor", j, f"aTS1_{c}")
            if c == 0:
                s.wait("tensor", j, "pRA7")
            s.inc(t.matmul(st.bank[0][0:1, 0:512], st.onescol_bf[:],
                           st.y2bf[:], start=(c == 0), stop=(c == NCH - 1)),
                  "sp", f"pTS1_{c}")
        for c in range(NCH):
            s.wait("vector", j, f"aWo{c}")
            if c == 0:
                s.wait("vector", j, "vYN7")
                s.wait("vector", j, "pWo7")
            cp = v.tensor_copy(st.Y1[:, c, :], st.att[:, c, :])
        s.inc(cp, "sv", "vAB")
        s.wait("tensor", j, "vAB")
        s.wait("tensor", j, "vZ")
        for c in range(NCH):
            mm = t.matmul(st.ps_ea[0:1, 0:512], st.onescol_bf[:],
                          st.Y1[:, c, :], start=(c == 0), stop=(c == NCH - 1))
        s.inc(mm, "sp", "pTS2")
        s.wait("vector", j, "pTS1_7")
        s.wait("vector", j, "pTS2")
        v.tensor_copy(sxx, st.bank[0][0:1, 0:512])
        s.inc(v.tensor_copy(sx, st.ps_ea[0:1, 0:512]), "sv", "vT0a")
        s.wait("vector", j, "vT0a")
        s.inc(v.scalar_tensor_tensor(u, sx, 1.0 / D, sx, ALU.mult, ALU.mult),
              "sv", "vT0b")
        s.wait("vector", j, "vT0b")
        s.inc(v.tensor_sub(var, sxx, u), "sv", "vT0c")
        s.wait("scalar", j, "vT0c")
        s.inc(a.activation(u, var, AF.Sqrt, bias=st.C("lneps", 1),
                           scale=st.C("invD", 1)), "sa", "aT0")
        s.wait("vector", j, "aT0")
        s.inc(v.reciprocal(st.arow[0:1, :], u), "sv", "vT0d")
        s.wait("vector", j, "vT0d")
        s.inc(v.scalar_tensor_tensor(st.brow2[0:1, :], sx, -1.0 / D,
                                     st.arow[0:1, :], ALU.mult, ALU.mult),
              "sv", "vT0")
        s.wait("tensor", j, "vT0")
        s.inc(t.matmul(st.ps_ea[:], st.ones1_f32[:], st.arow[0:1, :],
                       start=True, stop=True), "sp", "pT0")
        s.wait("tensor", j, "vYN7")
        s.wait("tensor", j, "pRE7")
        s.inc(t.matmul(st.bank[5][:], st.ones1_f32[:], st.brow2[0:1, :],
                       start=True, stop=True), "sp", "pT0b")
        s.wait("vector", j, "pT0")
        s.wait("vector", j, "pT0b")
        if j >= 1:
            s.wait("vector", j - 1, "outX")
        for c in range(NCH):
            v.tensor_mul(st.x1sb[:, c, :], st.att[:, c, :], st.ps_ea[:])
            vi = v.tensor_add(st.x1sb[:, c, :], st.x1sb[:, c, :],
                              st.bank[5][:])
        s.inc(vi, "sv", "vT1a")
        s.wait("vector", j, "vT1a")
        for c in range(NCH):
            vi = v.tensor_add(st.x1sb[:, c, :], st.x1sb[:, c, :],
                              st.x0cm[:, c, :])
        s.inc(vi, "sv", "vT1")
        for c in range(NCH):
            s.wait("scalar", j, "vT1")
            if c >= 1:
                s.wait("scalar", j, f"pTU1_{c - 1}")
            s.inc(a.activation(st.y2bf[:], st.x1sb[:, c, :], AF.Square),
                  "sa", f"aTU1_{c}")
            s.wait("tensor", j, f"aTU1_{c}")
            if c == 0:
                s.wait("tensor", j, "pTS1_7")
                s.wait("tensor", j, "vT0a")
            s.inc(t.matmul(st.bank[0][0:1, 0:512], st.onescol_bf[:],
                           st.y2bf[:], start=(c == 0), stop=(c == NCH - 1)),
                  "sp", f"pTU1_{c}")
        for c in range(NCH):
            s.wait("vector", j, "vT1")
            cp = v.tensor_copy(st.Y1[:, c, :], st.x1sb[:, c, :])
        s.inc(cp, "sv", "vXB")
        s.wait("tensor", j, "vXB")
        s.wait("tensor", j, "vT1a")
        for c in range(NCH):
            mm = t.matmul(st.ps_ea[0:1, 0:512], st.onescol_bf[:],
                          st.Y1[:, c, :], start=(c == 0), stop=(c == NCH - 1))
        s.inc(mm, "sp", "pTU2")
        s.wait("vector", j, "pTU1_7")
        s.wait("vector", j, "pTU2")
        v.tensor_copy(sxx, st.bank[0][0:1, 0:512])
        s.inc(v.tensor_copy(sx, st.ps_ea[0:1, 0:512]), "sv", "vU0a")
        s.wait("vector", j, "vU0a")
        s.inc(v.scalar_tensor_tensor(u, sx, 1.0 / D, sx, ALU.mult, ALU.mult),
              "sv", "vU0b")
        s.wait("vector", j, "vU0b")
        s.inc(v.tensor_sub(var, sxx, u), "sv", "vU0c")
        s.wait("scalar", j, "vU0c")
        s.inc(a.activation(u, var, AF.Sqrt, bias=st.C("lneps", 1),
                           scale=st.C("invD", 1)), "sa", "aU0")
        s.wait("vector", j, "aU0")
        s.inc(v.reciprocal(st.arow[0:1, :], u), "sv", "vU0d")
        s.wait("vector", j, "vU0d")
        s.inc(v.scalar_tensor_tensor(st.brow2[0:1, :], sx, -1.0 / D,
                                     st.arow[0:1, :], ALU.mult, ALU.mult),
              "sv", "vU0")
        s.wait("tensor", j, "vU0")
        s.wait("tensor", j, "vT1a")
        s.inc(t.matmul(st.ps_ea[:], st.ones1_f32[:], st.arow[0:1, :],
                       start=True, stop=True), "sp", "pU0")
        s.inc(t.matmul(st.bank[5][:], st.ones1_f32[:], st.brow2[0:1, :],
                       start=True, stop=True), "sp", "pU0b")
        s.wait("vector", j, "pU0")
        s.wait("vector", j, "pU0b")
        for c in range(NCH):
            v.tensor_mul(st.att[:, c, :], st.x1sb[:, c, :], st.ps_ea[:])
            vi = v.tensor_add(st.att[:, c, :], st.att[:, c, :], st.bank[5][:])
        s.inc(vi, "sv", "vXI")
        s.wait("sync", j, "vT1")
        s.inc(sy.dma_start(out=dview(x1h, jd * TB + GP,
                                     [[CMW, 128], [128 * CMW, NCH], [1, TB]]),
                           in_=st.x1sb[:]), "out", "outX")
        s.wait("sync", j, "vXI")
        s.inc(sy.dma_start(out=dview(xi0h, jd * TB + GP,
                                     [[CMW, 128], [128 * CMW, NCH], [1, TB]]),
                           in_=st.att[:]), "out", "outI")

    def emit_pair(jp, jv):
        for p in range(2):
            j = jp * 2 + p
            jd = j if jv is None else jv * 2 + p
            emit_blk(j, jd)

    run_streamed(kb, emit_pair)


def host_s1_patterns():
    import ml_dtypes
    P2 = np.zeros((128, 128), np.float32)
    P1a = np.zeros((128, 128), np.float32)
    P1b = np.zeros((128, 128), np.float32)
    for b in range(4):
        for m in range(128):
            P2[32 * b + m % 16, m] = 1.0
            P1a[32 * b + m // 16, m] = YSC * X2S
            P1b[32 * b + 8 + m // 16, m] = YSC * X2S
    tril = np.triu(np.ones((128, 128), np.float32))
    pair = np.zeros((128, 8, 16), np.float32)
    for k in range(128):
        for p8 in range(8):
            pair[k, p8, 2 * p8 + k // 64] = 1.0
    expd = np.zeros((16, 8, 128), np.float32)
    for p8 in range(8):
        for m in range(128):
            expd[2 * p8 + m // 64, p8, m] = 1.0
    flat = np.concatenate([P2.ravel(), P1a.ravel(), P1b.ravel(),
                           tril.ravel(), pair.ravel(), expd.ravel()])
    return flat.astype(ml_dtypes.bfloat16)


def prep_s1_weights_host(ins):
    Wq = np.asarray(ins["L0_attn_Wq"])
    Wk = np.asarray(ins["L0_attn_Wk"])
    Wv = np.asarray(ins["L0_attn_Wv"])
    Wo = np.asarray(ins["L0_attn_Wo"])
    wqp = np.zeros((D, 512), np.float32)
    wkp = np.zeros((D, 512), np.float32)
    for h in range(16):
        wqp[:, 32 * h:32 * h + 16] = Wq[16 * h:16 * h + 16, :].T
        wkp[:, 32 * h:32 * h + 16] = Wk[16 * h:16 * h + 16, :].T
    return {"wqp": np.ascontiguousarray(wqp),
            "wkp": np.ascontiguousarray(wkp),
            "wvt": np.ascontiguousarray(Wv.T.astype(np.float32)),
            "wot": np.ascontiguousarray(Wo.T.astype(np.float32))}


def build_dev_S1(races=True):
    import ml_dtypes
    ctx = ExitStack()
    nc = bass.Bass("TRN2", target_bir_lowering=False,
                   detect_race_conditions=races)
    kb = Kb(nc, ctx)
    with ctx:
        x0_h = nc.dram_tensor("x0", [SEQ, D], F32, kind="ExternalInput")
        x1_h = nc.dram_tensor("x1_cm", [D, CMW], F32, kind="ExternalOutput")
        xi0_h = nc.dram_tensor("xi0_cm", [D, CMW], F32, kind="ExternalOutput")
        wqp_h = nc.dram_tensor("wqp", [D, 512], F32, kind="ExternalInput")
        wkp_h = nc.dram_tensor("wkp", [D, 512], F32, kind="ExternalInput")
        wvt_h = nc.dram_tensor("wvt", [D, D], F32, kind="ExternalInput")
        wot_h = nc.dram_tensor("wot", [D, D], F32, kind="ExternalInput")
        gw_h = nc.dram_tensor("gw", [64], F32, kind="ExternalInput")
        msk_h = nc.dram_tensor("maskP", [3, 128, 384], F32,
                               kind="ExternalInput")
        id_h = nc.dram_tensor("identfi", [128, 128], F32,
                              kind="ExternalInput")
        cst_h = nc.dram_tensor("csts", [128, CONSTW], F32,
                               kind="ExternalInput")
        pat_h = nc.dram_tensor("s1pats", [host_s1_patterns().size], BF16,
                               kind="ExternalInput")
        c32_h = nc.dram_tensor("c32row", [TB], BF16, kind="ExternalInput")
        oc_h = nc.dram_tensor("onecol128", [128, 4], BF16,
                              kind="ExternalInput")
        wqbf = nc.dram_tensor("wqpbf", [D, 512], BF16)
        wkbf = nc.dram_tensor("wkpbf", [D, 512], BF16)
        wvbf = nc.dram_tensor("wvtbf", [D, D], BF16)
        wobf = nc.dram_tensor("wotbf", [D, D], BF16)
        wh = {"wqpbf": wqbf, "wkpbf": wkbf, "wvtbf": wvbf, "wotbf": wobf,
              "gw": gw_h, "c32row": c32_h, "onecol128": oc_h}
        ar = Arena(kb)
        st = alloc_common(kb, ar)
        emit_const_extras(kb, st, ar, cst_h, msk_h, id_h, pat_h)
        emit_const_init(kb, st, cst_h)
        emit_w_convert(kb, ar, [(wqp_h, wqbf, D * 512),
                                (wkp_h, wkbf, D * 512),
                                (wvt_h, wvbf, D * D),
                                (wot_h, wobf, D * D)])
        alloc_stageS1(kb, ar, st)
        st.rows_sb = ar.alloc([1, 2048], F32)
        st.arow = ar.alloc([1, TB], F32)
        st.brow2 = ar.alloc([2, TB], F32)
        st.brow2bf = ar.alloc([2, TB], BF16)
        print(f"[kernel] S1 arena high-water: {ar.hw} bytes/partition")
        kb.stage_reset(drain=("xin", "wld"))
        emit_stageS1(kb, st, wh, x0_h, x1_h, xi0_h)
        kb.wait("sync", "out", kb.count["out"])
        nc.all_engine_barrier()
        v = nc.vector
        dumps = [("dbg_z", st.z1.rearrange("p a b -> p (a b)"), NCH * TB),
                 ("dbg_q1", st.q1.rearrange("p a b -> p (a b)"), 4 * TB),
                 ("dbg_kT", st.kTb.rearrange("p a b -> p (a b)"), 4 * TB),
                 ("dbg_vb", st.vb.rearrange("p a b -> p (a b)"), 4 * 1024),
                 ("dbg_ya", st.Y1.rearrange("p a b -> p (a b)"), 8 * TB),
                 ("dbg_Y1", st.Y1.rearrange("p a b -> p (a b)"), NCH * TB),
                 ("dbg_at", st.att.rearrange("p a b -> p (a b)"), NCH * TB),
                 ("dbg_S", st.Sstbf.rearrange("p a b c -> p (a b c)"),
                  16 * 3 * 64),
                 ("dbg_qf", st.qf0, TB),
                 ("dbg_wq", st.wqp.rearrange("p a b -> p (a b)"), NCH * 512),
                 ("dbg_cq", st.colsQ, 512),
                 ("dbg_br", st.brow2bf, TB)]
        nparts = {"dbg_cq": 1, "dbg_br": 2}
        hnd = {}
        for nm, buf, w in dumps:
            hnd[nm] = nc.dram_tensor(nm, [nparts.get(nm, 128), w], F32,
                                     kind="ExternalOutput")
        stg = st.x0t[0]
        for nm, buf, w in dumps:
            np_ = nparts.get(nm, 128)
            for c0 in range(0, w, 1024):
                ww = min(1024, w - c0)
                kb.inc(v.tensor_copy(stg[0:np_, 0:ww],
                                     buf[0:np_, c0:c0 + ww]), "sv")
                kb.wait("sync", "sv", kb.count["sv"])
                kb.inc(nc.sync.dma_start(
                    out=dview(hnd[nm], c0, [[w, np_], [1, ww]]),
                    in_=stg[0:np_, 0:ww]), "out", 16)
                kb.wait("vector", "out", kb.count["out"])
        kb.wait("sync", "out", kb.count["out"])
    return nc


def alloc_final(kb, ar, st):
    ar.region_begin()
    st.f_x1 = [ar.alloc([128, NCH, TB], F32) for _ in range(2)]
    st.f_xi = [ar.alloc([128, NCH, TB], F32) for _ in range(2)]
    st.f_xib = ar.alloc([128, NCH, TB], BF16)
    st.f_x2b = ar.alloc([128, NCH, TB], BF16)
    st.f_out = ar.alloc([128, NCH, TB], F32)
    st.f_rows = ar.alloc([1, 2048], F32)
    st.f_arow = ar.alloc([1, TB], F32)
    st.f_brow = ar.alloc([1, TB], F32)
    return st


def emit_final(kb, st, x1h, xih, dst):
    nc = kb.nc
    t, v, a, sy = nc.tensor, nc.vector, nc.scalar, nc.sync
    s = Sched(kb, ("sp", "sv", "sa", "xi0", "xi1", "tb0", "tb1", "out"))

    def emit_blk(j, jd):
        s.blk_start(j)
        par = j % 2
        x1t, xit = st.f_x1[par], st.f_xi[par]
        sx = st.f_rows[0:1, 0:512]
        sxx = st.f_rows[0:1, 512:1024]
        u = st.f_rows[0:1, 1024:1536]
        var = st.f_rows[0:1, 1536:2048]

        s.wait("sync", j - 2, "vR7")
        s.inc(sy.dma_start(out=x1t[:], in_=dview(
            x1h, jd * TB + GP, [[CMW, 128], [128 * CMW, NCH], [1, TB]])),
            f"xi{par}", "x1")
        s.wait("sync", j - 2, "aSQ")
        s.wait("sync", j - 2, "vXB")
        s.inc(sy.dma_start(out=xit[:], in_=dview(
            xih, jd * TB + GP, [[CMW, 128], [128 * CMW, NCH], [1, TB]])),
            f"tb{par}", "xz")

        s.wait("vector", j, "xz")
        s.wait("vector", j - 1, "pST")
        s.inc(v.tensor_copy(st.f_xib[:], xit[:]), "sv", "vXB")
        s.wait("scalar", j, "xz")
        s.wait("scalar", j - 1, "pST")
        s.inc(a.activation(st.f_x2b[:], xit[:], AF.Square), "sa", "aSQ")

        s.wait("tensor", j, "vXB")
        s.wait("tensor", j, "aSQ")
        s.wait("tensor", j - 1, "aEV")
        for c in range(NCH):
            t.matmul(st.ps_rows[0:1, 0:512], st.onescol_bf[:],
                     st.f_xib[:, c, :], start=(c == 0), stop=(c == NCH - 1))
        for c in range(NCH):
            mm = t.matmul(st.ps_ea[0:1, 0:512], st.onescol_bf[:],
                          st.f_x2b[:, c, :], start=(c == 0),
                          stop=(c == NCH - 1))
        s.inc(mm, "sp", "pST")

        s.wait("scalar", j, "pST")
        s.wait("scalar", j - 1, "vBr")
        a.copy(sx, st.ps_rows[0:1, 0:512])
        s.inc(a.copy(sxx, st.ps_ea[0:1, 0:512]), "sa", "aEV")
        s.wait("vector", j, "aEV")
        s.inc(v.scalar_tensor_tensor(u, sx, 1.0 / D, sx, ALU.mult, ALU.mult),
              "sv", "vV1")
        s.wait("vector", j, "vV1")
        s.inc(v.tensor_sub(var, sxx, u), "sv", "vV2")
        s.wait("scalar", j, "vV2")
        s.inc(a.activation(u, var, AF.Sqrt, bias=st.C("lneps", 1),
                           scale=st.C("invD", 1)), "sa", "aSq")
        s.wait("vector", j, "aSq")
        s.inc(v.reciprocal(st.f_arow[0:1, :], u), "sv", "vRc")
        s.wait("vector", j, "vRc")
        s.inc(v.scalar_tensor_tensor(st.f_brow[0:1, :], sx, -1.0 / D,
                                     st.f_arow[0:1, :], ALU.mult, ALU.mult),
              "sv", "vBr")

        s.wait("tensor", j, "vBr")
        s.wait("tensor", j - 1, "vR7")
        t.matmul(st.bank[2][:], st.ones1_f32[:], st.f_arow[0:1, :],
                 start=True, stop=True)
        s.inc(t.matmul(st.bank[3][:], st.ones1_f32[:], st.f_brow[0:1, :],
                       start=True, stop=True), "sp", "pEx")

        s.wait("vector", j, "pEx")
        s.wait("vector", j, "x1")
        s.wait("vector", j - 1, "outS7")
        for c in range(NCH):
            v.tensor_mul(st.f_out[:, c, :], xit[:, c, :], st.bank[2][:])
            v.tensor_add(st.f_out[:, c, :], st.f_out[:, c, :], st.bank[3][:])
            if c == NCH - 1:
                s.inc(v.tensor_add(st.f_out[:, c, :], st.f_out[:, c, :],
                                   x1t[:, c, :]), "sv", "vR7")
            else:
                v.tensor_add(st.f_out[:, c, :], st.f_out[:, c, :],
                             x1t[:, c, :])

        s.wait("sync", j, "vR7")
        di = sy.dma_start(out=dview(dst, jd * TB,
                                    [[SEQ, 128], [128 * SEQ, NCH], [1, TB]]),
                          in_=st.f_out[:])
        s.inc(di, "out", "outS7")

    def emit_pair(jp, jv):
        for p in range(2):
            j = jp * 2 + p
            jd = j if jv is None else jv * 2 + p
            emit_blk(j, jd)

    run_streamed(kb, emit_pair)


def build_full(races=False):
    ctx = ExitStack()
    nc = bass.Bass("TRN2", target_bir_lowering=False,
                   detect_race_conditions=races,
                   dynamic_dma_scratch_size=16384)
    kb = Kb(nc, ctx)
    with ctx:
        x0_h = nc.dram_tensor("x0", [SEQ, D], F32, kind="ExternalInput")
        xout_h = nc.dram_tensor("xout", [D, SEQ], F32, kind="ExternalOutput")
        wqp_h = nc.dram_tensor("wqp", [D, 512], F32, kind="ExternalInput")
        wkp_h = nc.dram_tensor("wkp", [D, 512], F32, kind="ExternalInput")
        wvt_h = nc.dram_tensor("wvt", [D, D], F32, kind="ExternalInput")
        wot_h = nc.dram_tensor("wot", [D, D], F32, kind="ExternalInput")
        gw_h = nc.dram_tensor("gw", [64], F32, kind="ExternalInput")
        wqkv_h = nc.dram_tensor("wqkv", [NL, D, 3072], F32,
                                kind="ExternalInput")
        wo_h = nc.dram_tensor("wo", [NL, D, D], F32, kind="ExternalInput")
        qsk_h = nc.dram_tensor("qsk", [NL, 2, 64], F32, kind="ExternalInput")
        w1i_h = nc.dram_tensor("w1i", [NL, D, MT, 256], F32,
                               kind="ExternalInput")
        b1i_h = nc.dram_tensor("b1i", [NL, MT, 256], F32,
                               kind="ExternalInput")
        w2p_h = nc.dram_tensor("w2p", [NL, DIP, D], F32, kind="ExternalInput")
        gbb_h = nc.dram_tensor("gbb", [NL, 128, 3 * NCH], F32,
                               kind="ExternalInput")
        cst_h = nc.dram_tensor("csts", [128, CONSTW], F32,
                               kind="ExternalInput")
        msk_h = nc.dram_tensor("maskP", [3, 128, 384], F32,
                               kind="ExternalInput")
        id_h = nc.dram_tensor("identfi", [128, 128], F32,
                              kind="ExternalInput")
        pat_h = nc.dram_tensor("s1pats", [host_s1_patterns().size], BF16,
                               kind="ExternalInput")
        c32_h = nc.dram_tensor("c32row", [TB], BF16, kind="ExternalInput")
        oc_h = nc.dram_tensor("onecol128", [128, 4], BF16,
                              kind="ExternalInput")
        tbl_h = nc.dram_tensor("tbl4", [4, SEQ, 64], F32,
                               kind="ExternalInput")
        zg_h = nc.dram_tensor("zguard", [128, 64], F32, kind="ExternalInput")
        wqbf = nc.dram_tensor("wqpbf", [D, 512], BF16)
        wkbf = nc.dram_tensor("wkpbf", [D, 512], BF16)
        wvbf = nc.dram_tensor("wvtbf", [D, D], BF16)
        wobfS = nc.dram_tensor("wotbf", [D, D], BF16)
        wqkvbf = nc.dram_tensor("wqkvbf", [NL, D, 3072], BF16)
        wobf = nc.dram_tensor("wobf", [NL, D, D], BF16)
        w1bf = nc.dram_tensor("w1bf", [NL, D, MT, 256], BF16)
        w2bf = nc.dram_tensor("w2bf", [NL, DIP, D], BF16)
        b1bf = nc.dram_tensor("b1bf", [NL, MT, 256], BF16)
        x1cm_h = nc.dram_tensor("x1cm", [D, CMW], F32,
                                kind="ExternalOutput" if FDBG_SCRATCH
                                else "Internal")
        xiA_h = nc.dram_tensor("xiA", [D, CMW], F32,
                               kind="ExternalOutput" if FDBG_SCRATCH
                               else "Internal")
        xiB_h = nc.dram_tensor("xiB", [D, CMW], F32,
                               kind="ExternalOutput" if FDBG_SCRATCH
                               else "Internal")

        whS = {"wqpbf": wqbf, "wkpbf": wkbf, "wvtbf": wvbf, "wotbf": wobfS,
               "gw": gw_h, "c32row": c32_h, "onecol128": oc_h}
        whB = {"wqkvbf": wqkvbf, "wobf": wobf, "qsk": qsk_h, "tbl4": tbl_h}
        whA = {"w1bf": w1bf, "w2bf": w2bf, "b1bf": b1bf, "gbb": gbb_h}

        ar = Arena(kb, size=ARENA_BYTES)
        st = alloc_common(kb, ar)
        emit_const_extras(kb, st, ar, cst_h, msk_h, id_h, pat_h)
        emit_const_init(kb, st, cst_h)
        bnc = [ar.perm([128, 1024], BF16) for _ in range(2)]
        conv_list = [(wqp_h, wqbf, D * 512, 0),
                     (wkp_h, wkbf, D * 512, 0),
                     (wvt_h, wvbf, D * D, 0),
                     (wot_h, wobfS, D * D, 0)]
        for lv in range(NL):
            conv_list += [
                (wqkv_h, wqkvbf, D * 3072, lv * D * 3072),
                (wo_h, wobf, D * D, lv * D * D),
                (w1i_h, w1bf, D * MT * 256, lv * D * MT * 256),
                (w2p_h, w2bf, DIP * D, lv * DIP * D),
                (b1i_h, b1bf, MT * 256, lv * MT * 256)]
        cmarks = emit_w_convert(kb, ar, conv_list, drain=FDRAIN, bounce=bnc)

        def conv_wait(idx):
            c0, c1 = cmarks[idx]
            kb.wait("sync", "wco0", c0)
            kb.wait("sync", "wco1", c1)
        alloc_stageS1(kb, ar, st)
        st.rows_sb = ar.alloc([1, 2048], F32)
        st.arow = ar.alloc([1, TB], F32)
        st.brow2 = ar.alloc([2, TB], F32)
        st.brow2bf = ar.alloc([2, TB], BF16)
        if not FNO_GUARD:
            for hh in (xiA_h, xiB_h):
                kb.inc(nc.sync.dma_start(
                    out=dview(hh, 0,
                              [[CMW, 128], [128 * CMW, NCH], [1, GP]]),
                    in_=dview(zg_h, 0, [[64, 128], [GP, NCH], [1, GP]])),
                    "out", 16)
        kb.stage_fence()
        conv_wait(3)
        emit_stageS1(kb, st, whS, x0_h, x1cm_h, xiA_h)
        alloc_stageB(kb, ar, st)
        hwB = ar.hw
        alloc_stageA(kb, ar, st)
        nl_run = NL if FNL is None else FNL
        for lv in range(nl_run):
            kb.stage_fence()
            conv_wait(4 + lv * 5 + 1)
            emit_stageB(kb, st, whB, lv, xiA_h, xiB_h)
            kb.stage_fence()
            conv_wait(4 + lv * 5 + 4)
            emit_stageA(kb, st, whA, lv, xiB_h, xiA_h)
        alloc_final(kb, ar, st)
        print(f"[kernel] full arena high-water: {ar.hw} (B: {hwB})")
        kb.stage_fence()
        if not FSKIP_FINAL:
            emit_final(kb, st, x1cm_h, xiA_h, xout_h)
        kb.wait("sync", "out", kb.count["out"])
    return nc


def host_all_inputs(ins):
    import ml_dtypes
    wp = prep_weights_host(ins)
    ws = prep_s1_weights_host(ins)
    qsk = np.ascontiguousarray(np.stack(
        [np.asarray(ins["L0_ffn_q_scale"], np.float32),
         np.asarray(ins["L0_ffn_k_scale"], np.float32)], axis=1))
    com = {
        "wqp": ws["wqp"], "wkp": ws["wkp"], "wvt": ws["wvt"],
        "wot": ws["wot"],
        "gw": np.ascontiguousarray(np.asarray(ins["L0_attn_g_weight"],
                                              np.float32)),
        "wqkv": np.ascontiguousarray(np.asarray(ins["L0_ffn_Wqkv"],
                                                np.float32)),
        "wo": np.ascontiguousarray(np.asarray(ins["L0_ffn_Wo"], np.float32)),
        "qsk": qsk,
        "w1i": wp["w1i"], "b1i": wp["b1i"], "w2p": wp["w2p"],
        "gbb": wp["gbb"],
        "csts": host_consts(), "maskP": host_masks(),
        "identfi": np.eye(128, dtype=np.float32),
        "s1pats": host_s1_patterns(),
        "c32row": np.full(TB, YSC, ml_dtypes.bfloat16),
        "onecol128": np.ones((128, 4), ml_dtypes.bfloat16),
        "tbl4": host_tables(),
        "zguard": np.zeros((128, 64), np.float32),
    }
    return com


def kernel(**ins):
    set_seq(4096)
    x = np.asarray(ins["L0_attn_hidden_states"], np.float32)
    com = host_all_inputs(ins)
    nc = build_full()
    in_maps = [dict(com, x0=np.ascontiguousarray(x[b])) for b in range(8)]
    r = run_bass_kernel_spmd(nc, in_maps, core_ids=list(range(8)))
    return np.stack([np.asarray(r.results[b]["xout"], np.float32).T
                     for b in range(8)])


# version 06
# speedup vs baseline: 1.1823x; 1.1823x over previous
import numpy as np
import ml_dtypes
import concourse.bass as bass
import concourse.mybir as mybir
from concourse.bass_utils import run_bass_kernel_spmd
from contextlib import ExitStack

F32 = mybir.dt.float32
BF16 = mybir.dt.bfloat16
AF = mybir.ActivationFunctionType
ALU = mybir.AluOpType
AX = mybir.AxisListType

SEQ, D = 4096, 1024
TB = 512
NB = SEQ // TB
NP = NB // 2
NCH = D // 128
GP = 8
CMW = GP + SEQ
NL = 6
DI = 2730
DIP = 2816
MT = DIP // 128
NH, HD, WIN = 16, 64, 256
QK_SCALE = 8.0
FD1 = 16
X2S = float(1.0 / (np.sqrt(2.0) * 4.0))
YSC = 32.0
LNEPS = 1e-5
RMSEPS = 1e-5

ENGS = ("sync", "tensor", "vector", "scalar", "gpsimd")
B_PHASES = 3
FNL = None
FSKIP_FINAL = False
FCONV_MIN = False
FDBG_SCRATCH = False
FNO_GUARD = False
FDRAIN = False


def set_seq(seq):
    global SEQ, NB, NP, CMW
    SEQ = seq
    NB = SEQ // TB
    NP = NB // 2
    CMW = GP + SEQ


class Kb:
    def __init__(self, nc, ctx):
        self.nc = nc
        self.ctx = ctx
        self.E = ctx.enter_context
        self.eng = {"sync": nc.sync, "tensor": nc.tensor, "vector": nc.vector,
                    "scalar": nc.scalar, "gpsimd": nc.gpsimd}
        self.sem = {}
        self.count = {}
        self.mode = "straight"
        self.regs = {}
        self.regpool = {e: [] for e in ENGS}
        self.nregs = {e: 0 for e in ENGS}
        self.rec_last = None
        self.rec_delta = None
        self.body_delta = None

    def s(self, name):
        if name not in self.sem:
            self.sem[name] = self.E(self.nc.semaphore(name))
            self.count[name] = 0
        return self.sem[name]

    def inc(self, inst, name, n=1):
        inst.then_inc(self.s(name), n)
        if self.mode in ("straight", "record"):
            self.count[name] += n
            if self.mode == "record":
                self.rec_delta[name] = self.rec_delta.get(name, 0) + n
        else:
            self.body_delta[name] = self.body_delta.get(name, 0) + n
        return inst

    def _reg(self, engine):
        if self.regpool[engine]:
            return self.regpool[engine].pop()
        self.nregs[engine] += 1
        return self.E(self.eng[engine].register(
            f"r_{engine}_{self.nregs[engine]}"))

    def wait(self, engine, name, target):
        eng = self.eng[engine]
        sem = self.s(name)
        if self.mode == "straight":
            if target > 0:
                eng.wait_ge(sem, target)
            return
        key = (engine, name)
        t = max(target, 0)
        if self.mode == "record":
            if key in self.rec_last and t <= self.rec_last[key]:
                return
            eng.wait_ge(sem, t)
            self.rec_last[key] = t
            return
        assert key in self.regs, f"loop wait on {key} never recorded"
        if t <= self.rec_last[key]:
            return
        diff = t - self.rec_last[key]
        r = self.regs[key]
        eng.reg_add(r, r, diff)
        eng.wait_ge(sem, r)
        self.rec_last[key] = t

    def loop_record(self):
        assert self.mode == "straight"
        self.mode = "record"
        self.rec_last = {}
        self.rec_delta = {}

    def loop_enter(self):
        assert self.mode == "record"
        for key, last in self.rec_last.items():
            engine, name = key
            r = self._reg(engine)
            self.regs[key] = r
            self.eng[engine].reg_mov(r, last)
        self.mode = "loop"
        self.body_delta = {}

    def loop_iter_end(self):
        for name, d in self.body_delta.items():
            assert d == self.rec_delta.get(name, 0), (
                f"delta mismatch {name}: body {d} vs rec "
                f"{self.rec_delta.get(name, 0)}")
        self.body_delta = {}

    def loop_exit(self, extra_iters):
        assert self.mode == "loop"
        for name, d in self.rec_delta.items():
            self.count[name] += d * extra_iters
        for key, r in self.regs.items():
            self.regpool[key[0]].append(r)
        self.regs = {}
        self.mode = "straight"
        self.rec_last = self.rec_delta = self.body_delta = None

    DMA_PREF = ("xin", "xi", "wld", "wl", "out", "tb", "wot")

    def stage_fence(self):
        assert self.mode == "straight"
        for name in list(self.sem):
            if any(name.startswith(p) for p in self.DMA_PREF) and \
                    self.count[name]:
                self.eng["sync"].wait_ge(self.sem[name], self.count[name])
        f = self.s("fence")
        for e in ("tensor", "vector", "scalar", "sync"):
            self.eng[e].nop().then_inc(f, 1)
        self.count["fence"] += 4
        tgt = self.count["fence"]
        for e in ("tensor", "vector", "scalar", "sync"):
            self.eng[e].wait_ge(f, tgt)

    def stage_reset(self, drain=None):
        assert self.mode == "straight"
        if drain is None:
            drain = [n for n in self.sem
                     if any(n.startswith(p) for p in self.DMA_PREF)]
        for name in drain:
            if name in self.sem and self.count[name]:
                self.eng["sync"].wait_ge(self.sem[name], self.count[name])
        self.nc.all_engine_barrier()
        for name, h in self.sem.items():
            if name.startswith("wcv") or name.startswith("wco"):
                continue
            self.nc.gpsimd.sem_clear(h)
            self.count[name] = 0
        self.nc.all_engine_barrier()


def dview(t, offset, pattern):
    return bass.AP(t, offset, pattern)


UNROLL = True


def run_streamed(kb, emit_pair, n_pairs=None):
    nc = kb.nc
    if n_pairs is None:
        n_pairs = NP
    if UNROLL or n_pairs == 1:
        kb.loop_record()
        emit_pair(0, None)
        kb.mode = "straight"
        kb.rec_last = kb.rec_delta = None
        for jp in range(1, n_pairs):
            emit_pair(jp, None)
        return
    kb.loop_record()
    emit_pair(0, None)
    kb.loop_enter()
    with nc.Fori(1, n_pairs) as it:
        emit_pair(1, it)
        kb.loop_iter_end()
    kb.loop_exit(n_pairs - 1)


SP_A, SV_A, SA_A = 32, 37, 25


def emit_stageA(kb, st, wh, lv, src, dst):
    nc = kb.nc
    t, v, a, sy, g = nc.tensor, nc.vector, nc.scalar, nc.sync, nc.gpsimd

    w1h = wh["w1bf"]
    w2h = wh["w2bf"]
    b1h = wh["b1bf"]
    gbh = wh["gbb"]
    e_sp = kb.count.get("sp", 0)
    e_sv = kb.count.get("sv", 0)
    e_sa = kb.count.get("sa", 0)
    e_wld = kb.count.get("wld", 0)
    e_xin = kb.count.get("xin", 0)

    kb.inc(sy.dma_start(out=st.w2res[:], in_=dview(
        w2h, lv * (DIP * D), [[D, 128], [128 * D, MT], [1, D]])), "xin", 16)
    kb.inc(sy.dma_start(out=st.gbcol[:], in_=dview(
        gbh, lv * (128 * 3 * NCH), [[3 * NCH, 128], [1, 3 * NCH]])), "xin", 16)
    kb.inc(sy.dma_start(out=st.b1row[:], in_=dview(
        b1h, lv * (MT * 256), [[MT * 256, 1], [256, MT], [1, 256]])), "wld", 16)
    kb.inc(sy.dma_start(out=st.brow2bf[1:2, :], in_=st.cst_ones_row[0:1, 0:TB]),
           "xin", 16)
    kb.wait("vector", "xin", kb.count["xin"])
    kb.inc(v.tensor_copy(
        st.gb2bf[:],
        st.gbcol[:, 0:2 * NCH].rearrange("p (t c) -> p c t", c=NCH)), "sv")
    for m in range(MT):
        par = m % 2
        if m >= 2:
            kb.wait("sync", "sp", e_sp + m - 1)
        kb.inc(sy.dma_start(out=st.w1t[par][:], in_=dview(
            w1h, lv * (D * MT * 256) + m * 256,
            [[MT * 256, 128], [128 * MT * 256, NCH], [1, 256]])), "wld", 16)
        kb.wait("tensor", "wld", e_wld + 16 * (m + 2))
        kb.wait("tensor", "sv", e_sv + 1)
        if m >= 1:
            kb.wait("tensor", "sv", e_sv + 1 + (m - 1) + 1)
        for c in range(NCH):
            t.matmul(st.ps_rows[0:2, 0:256], st.gb2bf[:, c, :],
                     st.w1t[par][:, c, :], start=(c == 0), stop=False)
        mm = t.matmul(st.ps_rows[0:2, 0:256], st.zo_bf[:],
                      st.b1row[0:1, m, :], start=False, stop=True)
        kb.inc(mm, "sp")
        kb.wait("vector", "sp", e_sp + m + 1)
        kb.inc(v.tensor_copy(st.colsAB[0:2, m, :], st.ps_rows[0:2, 0:256]),
               "sv")
    cnt = lambda n: kb.count.get(n, 0)
    spB, svB, saB = cnt("sp"), cnt("sv"), cnt("sa")
    wldB, xinB, outB = cnt("wld"), cnt("xin"), cnt("out")
    assert (spB - e_sp, svB - e_sv, saB - e_sa) == (MT, 1 + MT, 0)
    assert wldB - e_wld == 16 * (MT + 1) and xinB - e_xin == 3 * 16

    def sp_of(j, k):
        return spB + j * SP_A + k

    def sv_of(j, k):
        return svB + j * SV_A + k

    def sa_of(j, k):
        return saB + j * SA_A + k

    def emit_blk(j, jd):
        par = j % 2
        x_t = st.xa[par]

        kb.wait("sync", "sv", sv_of(j - 2, SV_A))
        kb.wait("sync", "sa", sa_of(j - 2, 1))
        kb.inc(sy.dma_start(out=x_t[:], in_=dview(
            src, jd * TB + (GP - 1),
            [[CMW, 128], [128 * CMW, NCH], [1, TB + 1]])), "xin", 16)

        for m in range(MT):
            gidx = j * MT + m
            pj, pm = divmod(gidx - 2, MT)
            kb.wait("sync", "sp", sp_of(pj, 2 + pm + 1) if gidx >= 2
                    else spB)
            kb.inc(sy.dma_start(out=st.w1t[m % 2][:], in_=dview(
                w1h, lv * (D * MT * 256) + m * 256,
                [[MT * 256, 128], [128 * MT * 256, NCH], [1, 256]])),
                "wld", 16)

        kb.wait("vector", "xin", xinB + 16 * (j + 1))
        kb.wait("vector", "sp", sp_of(j - 1, 1))
        v.tensor_copy(st.xsbf[:, 0:4, :], x_t[:, 0:4, 1:TB + 1])
        kb.inc(v.tensor_copy(st.xsbf[:, 4:8, :], x_t[:, 4:8, 0:TB]), "sv")

        kb.wait("scalar", "xin", xinB + 16 * (j + 1))
        kb.wait("scalar", "sp", sp_of(j - 1, 1))
        a.activation(st.x2bf[:, 0:4, :], x_t[:, 0:4, 1:TB + 1], AF.Square)
        kb.inc(a.activation(st.x2bf[:, 4:8, :], x_t[:, 4:8, 0:TB], AF.Square),
               "sa")

        kb.wait("tensor", "sv", sv_of(j, 1))
        kb.wait("tensor", "sa", sa_of(j, 1))
        kb.wait("tensor", "sa", sa_of(j - 1, 2))
        kb.wait("tensor", "sv", sv_of(j - 1, 7))
        for c in range(NCH):
            t.matmul(st.ps_rows[0:1, 0:512], st.onescol_bf[:],
                     st.xsbf[:, c, :], start=(c == 0), stop=(c == NCH - 1))
        for c in range(NCH):
            mm = t.matmul(st.ps_ea[0:1, 0:512], st.onescol_bf[:],
                          st.x2bf[:, c, :], start=(c == 0), stop=(c == NCH - 1))
        kb.inc(mm, "sp")

        kb.wait("scalar", "sp", sp_of(j, 1))
        kb.wait("scalar", "sv", sv_of(j - 1, 6))
        a.copy(st.rows_sb[0:1, 0:512], st.ps_rows[0:1, 0:512])
        kb.inc(a.copy(st.rows_sb[0:1, 512:1024], st.ps_ea[0:1, 0:512]), "sa")

        kb.wait("vector", "sa", sa_of(j, 2))
        sx = st.rows_sb[0:1, 0:512]
        sxx = st.rows_sb[0:1, 512:1024]
        u = st.rows_sb[0:1, 1024:1536]
        var = st.rows_sb[0:1, 1536:2048]
        kb.inc(v.scalar_tensor_tensor(u, sx, 1.0 / D, sx, ALU.mult,
                                      ALU.mult), "sv")
        kb.wait("vector", "sv", sv_of(j, 2))
        kb.inc(v.tensor_sub(var, sxx, u), "sv")

        kb.wait("scalar", "sv", sv_of(j, 3))
        kb.inc(a.activation(u, var, AF.Sqrt, bias=st.C("lneps", 1),
                            scale=st.C("invD", 1)), "sa")

        kb.wait("vector", "sa", sa_of(j, 3))
        kb.wait("vector", "sp", sp_of(j - 1, 2 + MT))
        kb.inc(v.reciprocal(st.arow[0:1, :], u), "sv")
        kb.wait("vector", "sv", sv_of(j, 4))
        kb.inc(v.scalar_tensor_tensor(st.brow2[0:1, :], sx, -1.0 / D,
                                      st.arow[0:1, :], ALU.mult, ALU.mult),
               "sv")
        kb.wait("vector", "sv", sv_of(j, 5))
        kb.inc(v.tensor_copy(st.brow2bf[0:1, :], st.brow2[0:1, :]), "sv")

        kb.wait("tensor", "sv", sv_of(j, 6))
        kb.wait("tensor", "sv", sv_of(j - 1, 7))
        kb.inc(t.matmul(st.ps_ea[:], st.ones1_f32[:], st.arow[0:1, :],
                        start=True, stop=True), "sp")

        kb.wait("vector", "sp", sp_of(j, 2))
        kb.wait("vector", "sp", sp_of(j - 2, 2 + MT))
        for c in range(NCH):
            xv = x_t[:, c, 1:TB + 1] if c < 4 else x_t[:, c, 0:TB]
            zi = v.scalar_tensor_tensor(st.za[par][:, c, :], xv,
                                        st.gbcol[:, c:c + 1], st.ps_ea[:],
                                        ALU.mult, ALU.mult)
        kb.inc(zi, "sv")

        for m in range(MT):
            wpar = m % 2
            pa, pg = st.ps_ag[wpar * 2], st.ps_ag[wpar * 2 + 1]
            gidx = j * MT + m
            kb.wait("tensor", "wld", wldB + 16 * (gidx + 1))
            kb.wait("tensor", "sv", sv_of(j, 7))
            pj, pm = divmod(gidx - 2, MT)
            if gidx >= 2:
                kb.wait("tensor", "sv", sv_of(pj, 7 + pm + 1))
                kb.wait("tensor", "sa", sa_of(pj, 3 + pm + 1))
            for c in range(NCH):
                t.matmul(pa[:], st.w1t[wpar][:, c, 0:128],
                         st.za[par][:, c, :], start=(c == 0), stop=False)
            t.matmul(pa[:], st.colsAB[0:2, m, 0:128], st.brow2bf[0:2, :],
                     start=False, stop=True)
            for c in range(NCH):
                t.matmul(pg[:], st.w1t[wpar][:, c, 128:256],
                         st.za[par][:, c, :], start=(c == 0), stop=False)
            mm = t.matmul(pg[:], st.colsAB[0:2, m, 128:256],
                          st.brow2bf[0:2, :], start=False, stop=True)
            kb.inc(mm, "sp")

            kb.wait("scalar", "sp", sp_of(j, 2 + m + 1))
            if gidx >= 2:
                kb.wait("scalar", "sv", sv_of(pj, 7 + pm + 1))
            kb.inc(a.activation(st.gelu_t[wpar][:], pg[:], AF.Gelu), "sa")

            kb.wait("vector", "sa", sa_of(j, 3 + m + 1))
            kb.inc(v.tensor_mul(st.ha[:, m, :], pa[:], st.gelu_t[wpar][:]),
                   "sv")

        for o in range(NCH):
            py = st.ps_y[o % 2]
            kb.wait("tensor", "sv", sv_of(j, 7 + MT))
            oj, oo = (j, o - 2) if o >= 2 else (j - 1, NCH - 2 + o)
            kb.wait("tensor", "sv", sv_of(oj, 7 + MT + oo + 1))
            for m in range(MT):
                mm = t.matmul(py[:], st.w2res[:, m, o * 128:(o + 1) * 128],
                              st.ha[:, m, :], start=(m == 0),
                              stop=(m == MT - 1))
            kb.inc(mm, "sp")

            kb.wait("vector", "sp", sp_of(j, 2 + MT + o + 1))
            if o == 0:
                kb.wait("vector", "out", outB + 16 * j)
            rz = v.scalar_tensor_tensor(
                st.xout[:, o, :], py[:],
                st.gbcol[:, 2 * NCH + o:2 * NCH + o + 1],
                x_t[:, o, 1:TB + 1], ALU.add, ALU.add)
            kb.inc(rz, "sv")

        kb.wait("sync", "sv", sv_of(j, 7 + MT + NCH))
        kb.inc(sy.dma_start(out=dview(dst, jd * TB + GP,
                                      [[CMW, 128], [128 * CMW, NCH], [1, TB]]),
                            in_=st.xout[:]), "out", 16)

    def emit_pair(jp, jv):
        for p in range(2):
            j = jp * 2 + p
            jd = j if jv is None else jv * 2 + p
            emit_blk(j, jd)

    run_streamed(kb, emit_pair)
    assert kb.count["sp"] == spB + NB * SP_A, kb.count["sp"]
    assert kb.count["sv"] == svB + NB * SV_A
    assert kb.count["sa"] == saB + NB * SA_A


ARENA_BYTES = 207 * 1024


class Arena:
    def __init__(self, kb, size=ARENA_BYTES):
        self.buf = kb.E(kb.nc.sbuf_tensor("arena", [128, size],
                                          mybir.dt.int8))
        self.size = size
        self.perm_end = 0
        self.off = 0
        self.hw = 0

    log = []

    def _alloc(self, shape, dtype, off):
        Arena.log.append((off, shape, dtype))
        esz = {F32: 4, BF16: 2}[dtype]
        fe = int(np.prod(shape[1:]))
        nb = fe * esz
        assert off + nb <= self.size, f"arena overflow at {off}+{nb}"
        v = self.buf[0:shape[0], off:off + nb].bitcast(dtype)
        if len(shape) == 3:
            v = v.rearrange("p (a b) -> p a b", a=shape[1])
        elif len(shape) == 4:
            v = v.rearrange("p (a b c) -> p a b c", a=shape[1], b=shape[2])
        return v, off + ((nb + 63) // 64) * 64

    def perm(self, shape, dtype):
        v, self.perm_end = self._alloc(shape, dtype, self.perm_end)
        self.off = max(self.off, self.perm_end)
        self.hw = max(self.hw, self.perm_end)
        return v

    def region_begin(self):
        self.off = self.perm_end

    def alloc(self, shape, dtype):
        v, self.off = self._alloc(shape, dtype, self.off)
        self.hw = max(self.hw, self.off)
        return v


class Pool:
    pass


def alloc_common(kb, ar):
    nc, E = kb.nc, kb.E
    st = Pool()
    st.psum8 = E(nc.psum_tensor("banks8", [128, 8, 512], F32))
    st.bank = [st.psum8[:, i, :] for i in range(8)]
    st.ps_rows = st.bank[0]
    st.ps_ea = st.bank[1]
    st.ps_ag = [st.bank[2], st.bank[3], st.bank[4], st.bank[5]]
    st.ps_y = [st.bank[6], st.bank[7]]
    st.cst = ar.perm([128, CONSTW], F32)
    st.C = lambda name, parts=128: st.cst[0:parts, CCOL[name]:CCOL[name] + 1]
    st.onescol_bf = ar.perm([128, 1], BF16)
    st.ones1_f32 = ar.perm([1, 128], F32)
    st.zo_bf = ar.perm([1, 2], BF16)
    st.cst_ones_row = ar.perm([1, TB], BF16)
    return st


def alloc_stageA(kb, ar, st):
    ar.region_begin()
    st.xa = [ar.alloc([128, NCH, TB + 1], F32) for _ in range(2)]
    st.xsbf = ar.alloc([128, NCH, TB], BF16)
    st.x2bf = ar.alloc([128, NCH, TB], BF16)
    st.za = [ar.alloc([128, NCH, TB], BF16) for _ in range(2)]
    st.ha = ar.alloc([128, MT, TB], BF16)
    st.w1t = [ar.alloc([128, NCH, 256], BF16) for _ in range(2)]
    st.w2res = ar.alloc([128, MT, D], BF16)
    st.gelu_t = [ar.alloc([128, TB], BF16) for _ in range(2)]
    st.xout = ar.alloc([128, NCH, TB], F32)
    st.gbcol = ar.alloc([128, 3 * NCH], F32)
    st.gb2bf = ar.alloc([128, NCH, 2], BF16)
    st.b1row = ar.alloc([1, MT, 256], BF16)
    st.colsAB = ar.alloc([2, MT, 256], BF16)
    st.rows_sb = ar.alloc([1, 2048], F32)
    st.arow = ar.alloc([1, TB], F32)
    st.brow2 = ar.alloc([2, TB], F32)
    st.brow2bf = ar.alloc([2, TB], BF16)
    return st


def emit_const_init(kb, st, ch):
    nc = kb.nc
    v, sy = nc.vector, nc.sync
    kb.inc(sy.dma_start(out=st.cst[:], in_=dview(
        ch, 0, [[CONSTW, 128], [1, CONSTW]])), "xin", 16)
    kb.wait("vector", "xin", kb.count["xin"])
    v.tensor_copy(st.onescol_bf[:], st.cst[:, 0:1])
    v.tensor_copy(st.ones1_f32[0:1, :], st.cst[0:1, 2:130])
    v.tensor_copy(st.zo_bf[0:1, :], st.cst[0:1, 130:132])
    kb.inc(v.tensor_copy(
        st.cst_ones_row[0:1, :],
        st.cst[0:1, 0:1].broadcast_to([1, TB])), "sv")


CONSTW = 256


CVALS = {"lneps": LNEPS, "invD": 1.0 / D, "poly_b": 4.0,
         "rmseps": RMSEPS * YSC * YSC, "invHD": 1.0 / HD, "one": 1.0,
         "neg1": -1.0, "half": 0.5, "qks": QK_SCALE, "tiny": 1e-12}
CCOL = {k: 132 + i for i, k in enumerate(CVALS)}


def host_consts():
    c = np.zeros((128, CONSTW), np.float32)
    c[:, 0] = 1.0
    c[0, 2:130] = 1.0
    c[0, 130] = 0.0
    c[0, 131] = 1.0
    c[2, 2:130] = np.sqrt(8.0)
    for k, val in CVALS.items():
        c[:, CCOL[k]] = val
    return c


def build_dev_A():
    ctx = ExitStack()
    nc = bass.Bass("TRN2", target_bir_lowering=False)
    kb = Kb(nc, ctx)
    with ctx:
        xin_h = nc.dram_tensor("xin_cm", [D, CMW], F32, kind="ExternalInput")
        xout_h = nc.dram_tensor("xout_cm", [D, CMW], F32, kind="ExternalOutput")
        w1i_h = nc.dram_tensor("w1i", [NL, D, MT, 256], F32,
                               kind="ExternalInput")
        b1i_h = nc.dram_tensor("b1i", [NL, MT, 256], F32, kind="ExternalInput")
        w2p_h = nc.dram_tensor("w2p", [NL, DIP, D], F32, kind="ExternalInput")
        gbb_h = nc.dram_tensor("gbb", [NL, 128, 3 * NCH], F32,
                               kind="ExternalInput")
        cst_h = nc.dram_tensor("csts", [128, CONSTW], F32, kind="ExternalInput")
        w1bf_h = nc.dram_tensor("w1bf", [NL, D, MT, 256], BF16)
        w2bf_h = nc.dram_tensor("w2bf", [NL, DIP, D], BF16)
        b1bf_h = nc.dram_tensor("b1bf", [NL, MT, 256], BF16)
        wh = {"w1bf": w1bf_h, "w2bf": w2bf_h, "b1bf": b1bf_h, "gbb": gbb_h}
        ar = Arena(kb)
        st = alloc_common(kb, ar)
        emit_const_init(kb, st, cst_h)
        emit_w_convert(kb, ar, [(w1i_h, w1bf_h, NL * D * MT * 256),
                                (w2p_h, w2bf_h, NL * DIP * D),
                                (b1i_h, b1bf_h, NL * MT * 256)])
        alloc_stageA(kb, ar, st)
        print(f"[kernel] arena high-water: {ar.hw} bytes/partition")
        kb.stage_reset(drain=("xin", "wld"))
        emit_stageA(kb, st, wh, 0, xin_h, xout_h)
        kb.wait("sync", "out", kb.count["out"])
        kb.inc(nc.sync.dma_start(
            out=dview(xout_h, 0, [[CMW, 128], [128 * CMW, NCH], [1, GP]]),
            in_=dview(xin_h, 0, [[CMW, 128], [128 * CMW, NCH], [1, GP]])),
            "out", 16)
        kb.wait("sync", "out", kb.count["out"])
    return nc


def prep_weights_host(ins):
    W1 = np.asarray(ins["L0_ffn_W1"])
    b1 = np.asarray(ins["L0_ffn_b1"])
    W2 = np.asarray(ins["L0_ffn_W2"])
    w1p = np.zeros((NL, D, 2, DIP), np.float32)
    w1p[:, :, 0, :DI] = W1[:, :, :DI]
    w1p[:, :, 1, :DI] = W1[:, :, DI:]
    w1i = np.empty((NL, D, MT, 256), np.float32)
    w1i[:, :, :, 0:128] = w1p[:, :, 0].reshape(NL, D, MT, 128)
    w1i[:, :, :, 128:256] = w1p[:, :, 1].reshape(NL, D, MT, 128)
    b1p = np.zeros((NL, 2, DIP), np.float32)
    b1p[:, 0, :DI] = b1[:, :DI]
    b1p[:, 1, :DI] = b1[:, DI:]
    b1i = np.empty((NL, MT, 256), np.float32)
    b1i[:, :, 0:128] = b1p[:, 0].reshape(NL, MT, 128)
    b1i[:, :, 128:256] = b1p[:, 1].reshape(NL, MT, 128)
    w2p = np.zeros((NL, DIP, D), np.float32)
    w2p[:, :DI, :] = W2
    gbb = np.empty((NL, 128, 3 * NCH), np.float32)
    gbb[:, :, 0:NCH] = np.asarray(ins["L0_ffn_ln_g"]).reshape(
        NL, NCH, 128).transpose(0, 2, 1)
    gbb[:, :, NCH:2 * NCH] = np.asarray(ins["L0_ffn_ln_b"]).reshape(
        NL, NCH, 128).transpose(0, 2, 1)
    gbb[:, :, 2 * NCH:] = np.asarray(ins["L0_ffn_b2"]).reshape(
        NL, NCH, 128).transpose(0, 2, 1)
    return {"w1i": np.ascontiguousarray(w1i),
            "b1i": np.ascontiguousarray(b1i),
            "w2p": np.ascontiguousarray(w2p),
            "gbb": np.ascontiguousarray(gbb)}


def emit_w_convert(kb, ar, pairs, drain=True, bounce=None):
    nc = kb.nc
    g, sy = nc.gpsimd, nc.sync
    BW = 1024
    TILE = 128 * BW
    if bounce is None:
        ar.region_begin()
        bounce = [ar.alloc([128, BW], BF16) for _ in range(2)]
    bb = bounce
    k = 0
    marks = []
    for pr in pairs:
        (srct, dstt, total), base = pr[:3], (pr[3] if len(pr) > 3 else 0)
        nt = (total + TILE - 1) // TILE
        for i in range(nt):
            off = i * TILE
            cols = min(BW, (total - off) // 128)
            assert cols * 128 == min(TILE, total - off), (total, off)
            par = k % 2
            if k >= 2:
                kb.wait("gpsimd", f"wco{par}", kb.count.get(f"wco{par}", 0))
            kb.inc(g.dma_start(out=bb[par][:, 0:cols], in_=dview(
                srct, base + off, [[cols, 128], [1, cols]])), f"wcv{par}", 16)
            kb.wait("gpsimd", f"wcv{par}", kb.count[f"wcv{par}"])
            kb.inc(g.dma_start(out=dview(
                dstt, base + off, [[cols, 128], [1, cols]]),
                in_=bb[par][:, 0:cols]), f"wco{par}", 16)
            k += 1
        marks.append((kb.count.get("wco0", 0), kb.count.get("wco1", 0)))
    if drain:
        for p in range(2):
            kb.wait("sync", f"wco{p}", kb.count.get(f"wco{p}", 0))
    return marks


class Sched:
    def __init__(self, kb, names):
        self.kb = kb
        self.base = {n: kb.count.get(n, 0) for n in names}
        self.idx = {}
        self.per = {}
        self._ctr = {n: 0 for n in names}
        self._fin = False

    def blk_start(self, j):
        self.cur_par = j % 2
        if j == 2 and not self._fin:
            self.per = dict(self._ctr)
            self._fin = True

    DMA_SEMS = ("xin", "xi", "wld", "wl", "out", "tb", "wot", "wcv", "wco")

    def inc(self, inst, sem, point, n=None):
        if n is None:
            n = 16 if any(sem.startswith(p) for p in self.DMA_SEMS) else 1
        self.kb.inc(inst, sem, n)
        if not self._fin:
            self._ctr[sem] += n
            key = (point, self.cur_par)
            assert key not in self.idx, f"dup point {key}"
            self.idx[key] = (sem, self._ctr[sem])
        return inst

    def _sem_of(self, point):
        if point.startswith("wo"):
            return "wot0"
        if point == "x":
            return "xi0"
        if point.startswith("tbl"):
            return "tb0"
        if point == "outS":
            return "out"
        if point[0] == "p":
            return "sp"
        if point[0] == "a":
            return "sa"
        return "sv"

    def wait(self, eng, jj, point):
        key = (point, jj % 2)
        if key not in self.idx:
            assert jj < 0, f"forward ref {point} at block {jj}"
            self.kb.wait(eng, self._sem_of(point), 0)
            return
        sem, k = self.idx[key]
        if jj < 0:
            self.kb.wait(eng, sem, 0)
            return
        pair = jj // 2
        if pair > 0:
            assert self._fin, f"fwd ref to {point} before peel done"
            per = self.per[sem]
        else:
            per = 0
        self.kb.wait(eng, sem, self.base[sem] + pair * per + k)


def alloc_stageB(kb, ar, st):
    ar.region_begin()
    st.xb = [ar.alloc([128, NCH, TB + 1], F32) for _ in range(2)]
    st.xbbf = ar.alloc([128, NCH, TB], BF16)
    st.wqkvR = ar.alloc([128, NCH, 3072], BF16)
    st.woT = [ar.alloc([128, NCH, 128], BF16) for _ in range(2)]
    st.ftbl = [ar.alloc([128, 4, 4, 64], F32) for _ in range(2)]
    st.qsmat = ar.alloc([128, 4, 64], F32)
    st.qsrow = ar.alloc([1, 2, 64], F32)
    st.q2sb = ar.alloc([128, 16, 64], BF16)
    st.nrm = ar.alloc([128, 2, 16], F32)
    st.qn = ar.alloc([128, 16, 64], F32)
    st.qc = ar.alloc([128, 16, 64], F32)
    st.qr = ar.alloc([128, 16, 64], F32)
    st.qtokbf = [ar.alloc([128, 16, 64], BF16) for _ in range(2)]
    st.ktokbf = [ar.alloc([128, 16, 64], BF16) for _ in range(2)]
    st.q_cm = ar.alloc([128, NCH, TB], BF16)
    st.k_cm = ar.alloc([128, NCH, 256 + TB], BF16)
    st.vT = ar.alloc([128, 6, 1024], BF16)
    st.Em = [ar.alloc([128, 2, 384], BF16) for _ in range(2)]
    st.PT = [ar.alloc([128, 2, 384], BF16) for _ in range(2)]
    st.Ybf = ar.alloc([128, NCH, TB], BF16)
    st.xoutB = ar.alloc([128, NCH, TB], F32)
    st.nmax = ar.alloc([128, 4], F32)
    st.diag = [ar.alloc([128, 2, 128], BF16) for _ in range(2)]
    return st


def emit_stageB(kb, st, wh, lv, src, dst):
    nc = kb.nc
    t, v, a, sy, g = nc.tensor, nc.vector, nc.scalar, nc.sync, nc.gpsimd
    w3h = wh["wqkvbf"]
    w4h = wh["wobf"]
    qskh = wh["qsk"]
    tblh = wh["tbl4"]

    kb.inc(sy.dma_start(out=st.wqkvR[:, :, 0:1536], in_=dview(
        w3h, lv * (D * 3072), [[3072, 128], [128 * 3072, NCH], [1, 1536]])),
        "wld", 16)
    kb.inc(sy.dma_start(out=st.wqkvR[:, :, 1536:3072], in_=dview(
        w3h, lv * (D * 3072) + 1536,
        [[3072, 128], [128 * 3072, NCH], [1, 1536]])), "wld", 16)
    kb.inc(sy.dma_start(out=st.qsrow[:], in_=dview(
        qskh, lv * 128, [[128, 1], [64, 2], [1, 64]])), "wld", 16)
    kb.wait("tensor", "wld", kb.count["wld"])
    pb = st.bank[0]
    for i, (row, sw) in enumerate([(0, 0), (0, 1), (1, 0), (1, 1)]):
        kb.wait("tensor", "sv", kb.count["sv"])
        if not sw:
            mm = t.matmul(pb[:, 0:64], st.sq8row[:], st.qsrow[0:1, row, :],
                          start=True, stop=True)
        else:
            t.matmul(pb[:, 0:32], st.sq8row[:], st.qsrow[0:1, row, 32:64],
                     start=True, stop=True)
            mm = t.matmul(pb[:, 32:64], st.sq8row[:],
                          st.qsrow[0:1, row, 0:32], start=True, stop=True)
        kb.inc(mm, "sp")
        kb.wait("vector", "sp", kb.count["sp"])
        kb.inc(v.tensor_copy(st.qsmat[:, i, :], pb[:, 0:64]), "sv")
    v.memset(st.k_cm[:, :, 0:256], 0.0)
    kb.inc(v.memset(st.vT[:, 0:2, :], 0.0), "sv")

    s = Sched(kb, ("sp", "sv", "sa", "xi0", "xi1", "tb0", "tb1",
                   "wot0", "wot1", "out"))

    def emit_blk(j, jd):
        s.blk_start(j)
        par = j % 2
        x_t = st.xb[par]

        s.wait("sync", j - 2, "vXO7")
        ldx = sy.dma_start(out=x_t[:], in_=dview(
            src, jd * TB + (GP - 1),
            [[CMW, 128], [128 * CMW, NCH], [1, TB + 1]]))
        s.inc(ldx, f"xi{par}", "x")
        s.wait("sync", j - 2, "kTok3")
        for ti in range(4):
            ldt = sy.dma_start(out=st.ftbl[par][:, :, ti, :], in_=dview(
                tblh, ti * (SEQ * 64) + jd * (TB * 64),
                [[64, 128], [128 * 64, 4], [1, 64]]))
            s.inc(ldt, f"tb{par}", f"tbl{ti}")

        s.wait("vector", j, "x")
        s.wait("vector", j - 1, "pV3")
        v.tensor_copy(st.xbbf[:, 0:4, :], x_t[:, 0:4, 1:TB + 1])
        s.inc(v.tensor_copy(st.xbbf[:, 4:8, :], x_t[:, 4:8, 0:TB]), "sv",
              "vXS")
        s.wait("vector", j, "tbl3")
        for ti in range(4):
            fi = v.tensor_mul(
                st.ftbl[par][:, :, ti, :], st.ftbl[par][:, :, ti, :],
                st.qsmat[:, ti, :].unsqueeze(1).broadcast_to([128, 4, 64]))
        s.inc(fi, "sv", "vTF")

        pq = [st.bank[0], st.bank[1]]
        pk = [st.bank[2], st.bank[3]]
        pv = [st.bank[4], st.bank[5]]

        def emit_tr(tt):
            for side in ("Q", "K"):
                tok = (st.qtokbf if side == "Q" else st.ktokbf)[tt % 2]
                s.wait("tensor", j, f"{side.lower()}Tok{tt}")
                if side == "Q" and tt == 0:
                    s.wait("tensor", j - 1, "aTEK3")
                elif side == "Q":
                    s.wait("tensor", j, f"aTEK{tt - 1}")
                else:
                    s.wait("tensor", j, f"aTEQ{tt}")
                for c8 in range(8):
                    bk = st.bank[6 + c8 // 4][:].bitcast(BF16)
                    mm = t.transpose(
                        bk[:, (c8 % 4) * 128:(c8 % 4 + 1) * 128],
                        tok.rearrange("p h d -> p (h d)")
                        [:, c8 * 128:(c8 + 1) * 128], st.identbf[:])
                s.inc(mm, "sp", f"pT{side}{tt}")
                s.wait("scalar", j, f"pT{side}{tt}")
                for bi in range(2):
                    srcp = st.bank[6 + bi][:].bitcast(BF16)[:, 0:512] \
                        .rearrange("p (c w) -> p c w", c=4)
                    if side == "Q":
                        dcm = st.q_cm[:, bi * 4:(bi + 1) * 4,
                                      bass.ts(tt, 128)]
                    else:
                        dcm = st.k_cm[:, bi * 4:(bi + 1) * 4,
                                      256 + tt * 128:256 + (tt + 1) * 128]
                    ev = a.copy(dcm, srcp)
                s.inc(ev, "sa", f"aTE{side}{tt}")

        for tt in range(4):
            s.wait("tensor", j, "vXS")
            if tt == 0:
                if B_PHASES >= 2:
                    s.wait("tensor", j - 1, "aPTE31")
            else:
                s.wait("tensor", j, f"aQ2_{tt - 1}")
                s.wait("tensor", j, f"vQE_{tt - 1}")
            for hf in range(2):
                for c in range(NCH):
                    mm = t.matmul(pq[hf][:], st.xbbf[:, c, bass.ts(tt, 128)],
                                  st.wqkvR[:, c, hf * 512:(hf + 1) * 512],
                                  start=(c == 0), stop=(c == NCH - 1))
            s.inc(mm, "sp", f"pQ{tt}")
            if tt == 0:
                if B_PHASES >= 2:
                    s.wait("tensor", j - 1, "aPTE31")
            else:
                s.wait("tensor", j, f"aK2_{tt - 1}")
                s.wait("tensor", j, f"vKE_{tt - 1}")
            for hf in range(2):
                for c in range(NCH):
                    mm = t.matmul(pk[hf][:],
                                  st.xbbf[:, c, bass.ts(tt, 128)],
                                  st.wqkvR[:, c,
                                           1024 + hf * 512:1024 + (hf + 1) * 512],
                                  start=(c == 0), stop=(c == NCH - 1))
            s.inc(mm, "sp", f"pK{tt}")
            if tt == 0:
                if B_PHASES >= 2:
                    s.wait("tensor", j - 1, "vYE31")
            else:
                s.wait("tensor", j, f"vVE_{tt - 1}")
            for hf in range(2):
                for c in range(NCH):
                    mm = t.matmul(pv[hf][:],
                                  st.xbbf[:, c, bass.ts(tt, 128)],
                                  st.wqkvR[:, c,
                                           2048 + hf * 512:2048 + (hf + 1) * 512],
                                  start=(c == 0), stop=(c == NCH - 1))
            s.inc(mm, "sp", f"pV{tt}")
            if tt >= 1:
                emit_tr(tt - 1)

            for side, pqk, tix, tok in (("Q", pq, 0,
                                         st.qtokbf[tt % 2]),
                                        ("K", pk, 2, st.ktokbf[tt % 2])):
                s.wait("scalar", j, f"p{side}{tt}")
                prevn = (f"vNQ{tt}" if side == "K" else
                         (f"vNK{tt - 1}" if tt > 0 else None))
                if prevn:
                    s.wait("scalar", j, prevn)
                elif j >= 1:
                    s.wait("scalar", j - 1, "vNK3")
                a.activation(st.q2sb[:, 0:8, :], pqk[0][:], AF.Square)
                s.inc(a.activation(st.q2sb[:, 8:16, :], pqk[1][:],
                                   AF.Square), "sa", f"a{side}2_{tt}")
                s.wait("vector", j, f"a{side}2_{tt}")
                s.inc(v.reduce_sum(st.nrm[:, 0, :].unsqueeze(2), st.q2sb[:],
                                   AX.X), "sv", f"vN{side}{tt}")
                s.wait("scalar", j, f"vN{side}{tt}")
                s.inc(a.activation(st.nrm[:, 0, :], st.nrm[:, 0, :],
                                   AF.Sqrt), "sa", f"aS{side}{tt}")
                s.wait("vector", j, f"aS{side}{tt}")
                if tt > 1:
                    s.wait("vector", j, f"pT{side}{tt - 2}")
                else:
                    s.wait("vector", j - 1, f"pT{side}{tt + 2}")
                s.inc(v.tensor_scalar_max(st.nrm[:, 0, :], st.nrm[:, 0, :],
                                           1e-12), "sv", f"vCl{side}{tt}")
                s.wait("vector", j, f"vCl{side}{tt}")
                s.inc(v.reciprocal(st.nrm[:, 1, :], st.nrm[:, 0, :]), "sv",
                      f"vRc{side}{tt}")
                s.wait("vector", j, f"vRc{side}{tt}")
                rq0 = st.nrm[:, 1, 0:8].unsqueeze(1).broadcast_to([128, 64, 8])
                rq1 = st.nrm[:, 1, 8:16].unsqueeze(1).broadcast_to([128, 64, 8])
                qnT = st.qn[:].rearrange("p h d -> p d h")
                v.tensor_mul(qnT[:, :, 0:8],
                             pqk[0][:].rearrange("p (h d) -> p d h", d=64),
                             rq0)
                qe = v.tensor_mul(qnT[:, :, 8:16],
                                  pqk[1][:].rearrange("p (h d) -> p d h",
                                                      d=64),
                                  rq1)
                s.inc(qe, "sv", f"v{side}E_{tt}")
                cosb = st.ftbl[par][:, tt, tix, :].unsqueeze(1) \
                    .broadcast_to([128, 16, 64])
                sinb = st.ftbl[par][:, tt, tix + 1, :]
                v.tensor_mul(st.qc[:], st.qn[:], cosb)
                v.scalar_tensor_tensor(
                    st.qr[:, :, 0:32], st.qn[:, :, 32:64], -1.0,
                    sinb[:, 0:32].unsqueeze(1).broadcast_to([128, 16, 32]),
                    ALU.mult, ALU.mult)
                v.scalar_tensor_tensor(
                    st.qr[:, :, 32:64], st.qn[:, :, 0:32], 1.0,
                    sinb[:, 32:64].unsqueeze(1).broadcast_to([128, 16, 32]),
                    ALU.mult, ALU.mult)
                s.inc(v.tensor_add(tok[:], st.qc[:], st.qr[:]), "sv",
                      f"{side.lower()}Tok{tt}")

            s.wait("vector", j, f"pV{tt}")
            if B_PHASES >= 2:
                s.wait("vector", j - 1, "pPV31")
            v.tensor_copy(st.vT[:, 2 + tt, 0:512], pv[0][:])
            s.inc(v.tensor_copy(st.vT[:, 2 + tt, 512:1024], pv[1][:]), "sv",
                  f"vVE_{tt}")

        emit_tr(3)

        for c in range(NCH):
            if B_PHASES < 2:
                break
            for qt in range(4):
                u = c * 4 + qt
                upar = u % 2
                sb = [st.bank[2 * upar], st.bank[2 * upar + 1]]
                ptb = sb
                if u == 0:
                    s.wait("tensor", j, "aTEK3")
                    s.wait("tensor", j, "vVE_3")
                    s.wait("tensor", j - 1, "kH")
                    s.wait("tensor", j - 1, "vH")
                elif u >= 2:
                    s.wait("tensor", j, f"aPTE{u - 2}")
                t.matmul(sb[0][:, 0:384], st.q_cm[0:64, c, bass.ts(qt, 128)],
                         st.k_cm[0:64, c, qt * 128:qt * 128 + 384],
                         start=True, stop=True)
                s.inc(t.matmul(sb[1][:, 0:384],
                               st.q_cm[64:128, c, bass.ts(qt, 128)],
                               st.k_cm[64:128, c, qt * 128:qt * 128 + 384],
                               start=True, stop=True), "sp", f"pS{u}")
                s.wait("vector", j, f"pS{u}")
                sbp = st.psum8[:, 2 * upar:2 * upar + 2, 0:384]
                s.inc(v.tensor_reduce(st.nmax[:, 0:2].unsqueeze(2), sbp,
                                      AX.X, ALU.max, negate=True), "sv",
                      f"vMX{u}")
                s.wait("scalar", j, f"vMX{u}")
                if u >= 2:
                    s.wait("scalar", j, f"pPT{u - 2}")
                a.activation(st.Em[upar][:, 0, :], sb[0][:, 0:384], AF.Exp,
                             bias=st.nmax[:, 0:1])
                s.inc(a.activation(st.Em[upar][:, 1, :], sb[1][:, 0:384],
                                   AF.Exp, bias=st.nmax[:, 1:2]), "sa",
                      f"aEXP{u}")
                mask = st.maskB[:, 1 + qt, :] if (j == 0 and qt < 2) \
                    else st.maskB[:, 0, :]
                s.wait("vector", j, f"aEXP{u}")
                if u >= 2:
                    s.wait("vector", j, f"pPT{u - 2}")
                s.inc(v.tensor_mul(
                    st.Em[upar][:, 0:2, :], st.Em[upar][:, 0:2, :],
                    mask.unsqueeze(1).broadcast_to([128, 2, 384])), "sv",
                    f"vMk{u}")
                s.wait("vector", j, f"vMk{u}")
                s.inc(v.reduce_sum(st.nmax[:, 2:4].unsqueeze(2),
                                   st.Em[upar][:, 0:2, :], AX.X), "sv",
                      f"vSm{u}")
                s.wait("vector", j, f"vSm{u}")
                s.inc(v.reciprocal(st.nmax[:, 2:4], st.nmax[:, 2:4]), "sv",
                      f"vRr{u}")
                s.wait("vector", j, f"vRr{u}")
                v.tensor_scalar(st.diag[upar][:, 0, :], st.identbf[:],
                                st.nmax[:, 2:3], None, ALU.mult)
                s.inc(v.tensor_scalar(st.diag[upar][:, 1, :], st.identbf[:],
                                      st.nmax[:, 3:4], None, ALU.mult),
                      "sv", f"vD{u}")
                s.wait("tensor", j, f"vD{u}")
                for h in range(2):
                    for kc in range(3):
                        mm = t.matmul(
                            ptb[h][:, kc * 128:(kc + 1) * 128],
                            st.Em[upar][:, h, kc * 128:(kc + 1) * 128],
                            st.diag[upar][:, h, :], start=True, stop=True)
                s.inc(mm, "sp", f"pPT{u}")
                s.wait("scalar", j, f"pPT{u}")
                if u >= 2:
                    s.wait("scalar", j, f"pPV{u - 2}")
                s.inc(a.copy(st.PT[upar][:, 0:2, :],
                             st.psum8[:, 2 * upar:2 * upar + 2, 0:384]),
                      "sa", f"aPTE{u}")
                yb = st.bank[5][:, (u % 4) * 128:(u % 4 + 1) * 128]
                s.wait("tensor", j, f"aPTE{u}")
                if u >= 4:
                    s.wait("tensor", j, f"vYE{u - 4}")
                for h in range(2):
                    for kc in range(3):
                        mm = t.matmul(
                            yb[h * 64:(h + 1) * 64, :],
                            st.vT[:, qt + kc,
                                  (2 * c + h) * 64:(2 * c + h + 1) * 64],
                            st.PT[upar][:, h, kc * 128:(kc + 1) * 128],
                            start=(kc == 0), stop=(kc == 2),
                            tile_position=(0, 64 * h))
                s.inc(mm, "sp", f"pPV{u}")
                s.wait("vector", j, f"pPV{u}")
                if u == 0 and B_PHASES >= 3:
                    s.wait("vector", j - 1, "pWo7")
                s.inc(v.tensor_copy(st.Ybf[:, c, bass.ts(qt, 128)], yb),
                      "sv", f"vYE{u}")

        for o in range(NCH if B_PHASES >= 3 else 0):
            gidx = j * NCH + o
            if gidx >= 2:
                pj, pw = divmod(gidx - 2, NCH)
                s.wait("sync", pj, f"pWo{pw}")
            ldw = sy.dma_start(out=st.woT[o % 2][:], in_=dview(
                w4h, lv * (D * D) + o * 128,
                [[D, 128], [128 * D, NCH], [1, 128]]))
            s.inc(ldw, f"wot{o % 2}", f"wo{o}")

            po = st.bank[6 + o % 2]
            s.wait("tensor", j, "vYE31")
            s.wait("tensor", j, f"wo{o}")
            if o >= 2:
                s.wait("tensor", j, f"vXO{o - 2}")
            else:
                s.wait("tensor", j - 1, "vXO7")
            for c2 in range(NCH):
                mm = t.matmul(po[:], st.woT[o % 2][:, c2, :],
                              st.Ybf[:, c2, :], start=(c2 == 0),
                              stop=(c2 == NCH - 1))
            s.inc(mm, "sp", f"pWo{o}")
            s.wait("vector", j, f"pWo{o}")
            if o == 0:
                s.wait("vector", j - 1, "outS")
            s.inc(v.tensor_add(st.xoutB[:, o, :], po[:],
                               x_t[:, o, 1:TB + 1]), "sv", f"vXO{o}")

        if B_PHASES >= 2:
            s.wait("vector", j, "pS31")
        s.inc(v.tensor_copy(st.k_cm[:, :, 0:256],
                            st.k_cm[:, :, TB:TB + 256]), "sv", "kH")
        if B_PHASES >= 2:
            s.wait("vector", j, "pPV31")
        s.inc(v.tensor_copy(st.vT[:, 0:2, :], st.vT[:, 4:6, :]), "sv", "vH")

        if B_PHASES >= 3:
            s.wait("sync", j, "vXO7")
        s.inc(sy.dma_start(out=dview(dst, jd * TB + GP,
                                     [[CMW, 128], [128 * CMW, NCH], [1, TB]]),
                           in_=st.xoutB[:]), "out", "outS")

    def emit_pair(jp, jv):
        for p in range(2):
            j = jp * 2 + p
            jd = j if jv is None else jv * 2 + p
            emit_blk(j, jd)

    run_streamed(kb, emit_pair)


def host_tables():
    half = np.arange(0, HD, 2, dtype=np.float64)
    inv_freq = 1.0 / 10000.0 ** (half / HD)
    tpos = np.arange(SEQ, dtype=np.float64)
    freqs = tpos[:, None] * inv_freq[None, :]
    freqs = np.concatenate([freqs, freqs], -1)
    base_scale = (half + 0.4 * HD) / (1.4 * HD)
    power = (tpos - SEQ // 2) / (WIN // 2)
    scale = base_scale[None, :] ** power[:, None]
    scale = np.concatenate([scale, scale], -1)
    cosS = np.cos(freqs) * scale
    sinS = np.sin(freqs) * scale
    cosI = np.cos(freqs) / scale
    sinI = np.sin(freqs) / scale
    return np.stack([cosS, sinS, cosI, sinI]).astype(np.float32)


def host_masks():
    i = np.arange(128)[:, None]
    cc = np.arange(384)[None, :]
    rel = np.zeros((3, 128, 384), np.float32)
    rel[0] = ((cc >= i) & (cc <= i + 256)).astype(np.float32)
    for qt in range(2):
        keyg = qt * 128 - 256 + cc
        rel[1 + qt] = (((cc >= i) & (cc <= i + 256)) &
                       (keyg >= 0)).astype(np.float32)
    return rel


def build_dev_B(races=True):
    ctx = ExitStack()
    nc = bass.Bass("TRN2", target_bir_lowering=False,
                   detect_race_conditions=races)
    kb = Kb(nc, ctx)
    with ctx:
        xin_h = nc.dram_tensor("xin_cm", [D, CMW], F32, kind="ExternalInput")
        xout_h = nc.dram_tensor("xout_cm", [D, CMW], F32,
                                kind="ExternalOutput")
        wqkv_h = nc.dram_tensor("wqkv", [NL, D, 3072], F32,
                                kind="ExternalInput")
        wo_h = nc.dram_tensor("wo", [NL, D, D], F32, kind="ExternalInput")
        qsk_h = nc.dram_tensor("qsk", [NL, 2, 64], F32, kind="ExternalInput")
        tbl_h = nc.dram_tensor("tbl4", [4, SEQ, 64], F32,
                               kind="ExternalInput")
        zg_h = nc.dram_tensor("zguard", [128, 64], F32, kind="ExternalInput")
        msk_h = nc.dram_tensor("maskP", [3, 128, 384], F32,
                               kind="ExternalInput")
        id_h = nc.dram_tensor("identf", [128, 128], F32, kind="ExternalInput")
        cst_h = nc.dram_tensor("csts", [128, CONSTW], F32,
                               kind="ExternalInput")
        wqkvbf_h = nc.dram_tensor("wqkvbf", [NL, D, 3072], BF16)
        wobf_h = nc.dram_tensor("wobf", [NL, D, D], BF16)
        wh = {"wqkvbf": wqkvbf_h, "wobf": wobf_h, "qsk": qsk_h,
              "tbl4": tbl_h}
        ar = Arena(kb)
        st = alloc_common(kb, ar)
        emit_const_extras(kb, st, ar, cst_h, msk_h, id_h)
        emit_const_init(kb, st, cst_h)
        emit_w_convert(kb, ar, [(wqkv_h, wqkvbf_h, NL * D * 3072),
                                (wo_h, wobf_h, NL * D * D)])
        alloc_stageB(kb, ar, st)
        print(f"[kernel] arena high-water: {ar.hw} bytes/partition")
        kb.stage_reset(drain=("xin", "wld"))
        emit_stageB(kb, st, wh, 0, xin_h, xout_h)
        kb.wait("sync", "out", kb.count["out"])
        dbg_w = nc.dram_tensor("dbg_w", [256, 1024], BF16,
                               kind="ExternalOutput")
        kb.inc(nc.sync.dma_start(out=dview(dbg_w, 0, [[1024, 128], [1, 1024]]),
                                 in_=dview(wqkvbf_h, 256 * 3072 + 2048,
                                           [[3072, 128], [1, 1024]])),
               "out", 16)
        kb.inc(nc.sync.dma_start(
            out=dview(dbg_w, 128 * 1024, [[1024, 128], [1, 1024]]),
            in_=dview(wqkvbf_h, 384 * 3072 + 2048, [[3072, 128], [1, 1024]])),
            "out", 16)
        kb.wait("sync", "out", kb.count["out"])
        dbg_q = nc.dram_tensor("dbg_q", [128, NCH * TB], F32,
                               kind="ExternalOutput")
        dbg_k = nc.dram_tensor("dbg_k", [128, NCH * (256 + TB)], F32,
                               kind="ExternalOutput")
        dbg_v = nc.dram_tensor("dbg_v", [128, 6 * 1024], F32,
                               kind="ExternalOutput")
        dbg_y = nc.dram_tensor("dbg_y", [128, NCH * TB], F32,
                               kind="ExternalOutput")
        nc.all_engine_barrier()
        v = nc.vector
        dscr = st.xb[0]
        flat = dscr.rearrange("p a b -> p (a b)")
        dbg_qn = nc.dram_tensor("dbg_qn", [128, 1024], F32,
                                kind="ExternalOutput")
        dbg_sq = nc.dram_tensor("dbg_sq", [128, 1024], F32,
                                kind="ExternalOutput")
        dbg_nr = nc.dram_tensor("dbg_nr", [128, 32], F32,
                                kind="ExternalOutput")
        dbg_qt = nc.dram_tensor("dbg_qt", [128, 1024], F32,
                                kind="ExternalOutput")
        dbg_kt = nc.dram_tensor("dbg_kt", [128, 1024], F32,
                                kind="ExternalOutput")
        for name, buf, width in (
                ("dbg_qn", st.qn.rearrange("p a b -> p (a b)"), 1024),
                ("dbg_sq", st.q2sb.rearrange("p a b -> p (a b)"), 1024),
                ("dbg_nr", st.nrm.rearrange("p a b -> p (a b)"), 32),
                ("dbg_qt", st.qtokbf.rearrange("p a b -> p (a b)"), 1024),
                ("dbg_kt", st.ktokbf.rearrange("p a b -> p (a b)"), 1024),
                ("dbg_q", st.q_cm.rearrange("p a b -> p (a b)"), NCH * TB),
                                 ("dbg_k", st.k_cm.rearrange("p a b -> p (a b)"), NCH * (256 + TB)),
                                 ("dbg_v", st.vT.rearrange("p a b -> p (a b)"), 6 * 1024),
                                 ("dbg_y", st.Ybf.rearrange("p a b -> p (a b)"), NCH * TB)):
            h = {"dbg_q": dbg_q, "dbg_k": dbg_k, "dbg_v": dbg_v,
                 "dbg_y": dbg_y, "dbg_qn": dbg_qn, "dbg_sq": dbg_sq,
                 "dbg_nr": dbg_nr, "dbg_qt": dbg_qt, "dbg_kt": dbg_kt}[name]
            for c0 in range(0, width, 4096):
                w = min(4096, width - c0)
                kb.inc(v.tensor_copy(flat[:, 0:w], buf[:, c0:c0 + w]), "sv")
                kb.wait("sync", "sv", kb.count["sv"])
                kb.inc(nc.sync.dma_start(
                    out=dview(h, c0, [[width, 128], [1, w]]),
                    in_=flat[:, 0:w]), "out", 16)
                kb.wait("vector", "out", kb.count["out"])
        kb.wait("sync", "out", kb.count["out"])
        kb.inc(nc.sync.dma_start(
            out=dview(xout_h, 0, [[CMW, 128], [128 * CMW, NCH], [1, GP]]),
            in_=dview(xin_h, 0, [[CMW, 128], [128 * CMW, NCH], [1, GP]])),
            "out", 16)
        kb.wait("sync", "out", kb.count["out"])
    return nc


def emit_const_extras(kb, st, ar, cst_h, msk_h, id_h, pat_h=None):
    nc = kb.nc
    st.sq8row = ar.perm([1, 128], F32)
    st.identbf = ar.perm([128, 128], BF16)
    st.identf = ar.perm([128, 128], F32)
    st.maskB = ar.perm([128, 3, 384], BF16)
    kb.inc(nc.sync.dma_start(out=st.sq8row[:], in_=dview(
        cst_h, 2 * CONSTW + 2, [[CONSTW, 1], [1, 128]])), "xin", 16)
    kb.inc(nc.sync.dma_start(out=st.identf[:], in_=dview(
        id_h, 0, [[128, 128], [1, 128]])), "xin", 16)
    kb.inc(nc.gpsimd.dma_start(out=st.identbf[:], in_=dview(
        id_h, 0, [[128, 128], [1, 128]])), "wcv0", 16)
    for mi in range(3):
        kb.inc(nc.gpsimd.dma_start(out=st.maskB[:, mi, :], in_=dview(
            msk_h, mi * 128 * 384, [[384, 128], [1, 384]])), "wcv0", 16)
    if pat_h is not None:
        st.P2pat = ar.perm([128, 128], BF16)
        st.P1apat = ar.perm([128, 128], BF16)
        st.P1bpat = ar.perm([128, 128], BF16)
        st.trilm = ar.perm([128, 128], BF16)
        st.pairpat = ar.perm([128, 8, 16], BF16)
        st.expdpat = ar.perm([16, 8, 128], BF16)
        offs = 0
        for ap, rows, cols in ((st.P2pat, 128, 128), (st.P1apat, 128, 128),
                               (st.P1bpat, 128, 128), (st.trilm, 128, 128),
                               (st.pairpat, 128, 8 * 16),
                               (st.expdpat, 16, 8 * 128)):
            dst = ap if len(ap.shape) == 2 else ap.rearrange(
                "p a b -> p (a b)")
            kb.inc(nc.sync.dma_start(out=dst, in_=dview(
                pat_h, offs, [[cols, rows], [1, cols]])), "xin", 16)
            offs += rows * cols
    kb.wait("sync", "xin", kb.count["xin"])
    kb.wait("sync", "wcv0", kb.count["wcv0"])


def alloc_stageS1(kb, ar, st):
    ar.region_begin()
    st.x0t = [ar.alloc([128, 1024], F32) for _ in range(2)]
    st.x0cm = ar.alloc([128, NCH, TB], F32)
    st.statc = ar.alloc([128, 4, 2], F32)
    st.z1 = ar.alloc([128, NCH, TB], BF16)
    st.wqp = ar.alloc([128, NCH, 512], BF16)
    st.wkp = ar.alloc([128, NCH, 512], BF16)
    st.wvr = ar.alloc([128, NCH, 1024], BF16)
    st.wor = ar.alloc([128, NCH, 1024], BF16)
    st.colsQ = ar.alloc([1, 512], BF16)
    st.colsK = ar.alloc([1, 512], BF16)
    st.colsV = ar.alloc([1, 1024], BF16)
    st.q1 = ar.alloc([128, 4, TB], BF16)
    st.k1 = ar.alloc([128, 4, TB], BF16)
    st.kTb = ar.alloc([128, 4, TB], BF16)
    st.vb = ar.alloc([128, 4, 1024], BF16)
    st.acol = ar.alloc([128, 4], F32)
    st.qf0 = ar.alloc([128, TB], BF16)
    st.qf1 = ar.alloc([128, TB], BF16)
    st.qf2 = ar.alloc([32, TB], BF16)
    st.kfT = ar.alloc([128, 4, 288], BF16)
    st.M2e = ar.alloc([128, TB], BF16)
    st.tsb = [ar.alloc([128, 128], F32) for _ in range(2)]
    st.Asb = [ar.alloc([128, 128], BF16) for _ in range(2)]
    st.Sst = ar.alloc([128, 16, 3, 64], F32)
    st.Sstbf = ar.alloc([128, 16, 3, 64], BF16)
    st.y2bf = ar.alloc([128, TB], BF16)
    st.Y1 = ar.alloc([128, NCH, TB], BF16)
    st.att = ar.alloc([128, NCH, TB], F32)
    st.x1sb = ar.alloc([128, NCH, TB], F32)
    st.r16 = ar.alloc([16, TB], F32)
    st.r16b = ar.alloc([16, TB], BF16)
    st.statc2 = ar.alloc([128, 2], F32)
    st.sqscr = ar.alloc([128, 512], BF16)
    return st


def emit_stageS1(kb, st, wh, x0h, x1h, xi0h):
    nc = kb.nc
    t, v, a, sy, g = nc.tensor, nc.vector, nc.scalar, nc.sync, nc.gpsimd
    wq_h, wk_h, wv_h, wo_h = wh["wqpbf"], wh["wkpbf"], wh["wvtbf"], wh["wotbf"]
    gw_h = wh["gw"]

    for (dst_t, src_t, cols) in ((st.wqp, wq_h, 512), (st.wkp, wk_h, 512),
                                 (st.wvr, wv_h, 1024), (st.wor, wo_h, 1024)):
        kb.inc(sy.dma_start(out=dst_t[:], in_=dview(
            src_t, 0, [[cols, 128], [128 * cols, NCH], [1, cols]])),
            "wld", 16)
    kb.inc(sy.dma_start(out=st.acol[0:64, 0:1], in_=dview(
        gw_h, 0, [[1, 64], [1, 1]])), "wld", 16)
    kb.inc(sy.dma_start(out=st.acol[64:128, 0:1], in_=dview(
        gw_h, 0, [[1, 64], [1, 1]])), "wld", 16)
    kb.wait("vector", "wld", kb.count["wld"])
    kb.wait("tensor", "wld", kb.count["wld"])
    for c in range(NCH):
        kb.inc(v.tensor_scalar(st.wor[:, c, :], st.wor[:, c, :],
                               st.acol[:, 0:1], None, ALU.mult), "sv")
    kb.wait("tensor", "sv", kb.count["sv"])
    for (wt, colsb, cols) in ((st.wqp, st.colsQ, 512),
                              (st.wkp, st.colsK, 512),
                              (st.wvr, st.colsV, 1024)):
        for half in range(cols // 512):
            for c in range(NCH):
                mm = t.matmul(st.bank[0][0:1, 0:512], st.onescol_bf[:],
                              wt[:, c, half * 512:(half + 1) * 512],
                              start=(c == 0), stop=(c == NCH - 1))
            kb.inc(mm, "sp")
            kb.wait("vector", "sp", kb.count["sp"])
            kb.inc(v.tensor_copy(colsb[0:1, half * 512:(half + 1) * 512],
                                 st.bank[0][0:1, 0:512]), "sv")
            kb.wait("tensor", "sv", kb.count["sv"])
    kb.inc(v.memset(st.Sst[:], 0.0), "sv")
    kb.inc(v.memset(st.Sstbf[:], 0.0), "sv")
    kb.inc(v.memset(st.kfT[:, :, 272:288], 0.0), "sv")
    kb.wait("sync", "sv", kb.count["sv"])
    kb.inc(sy.dma_start(out=st.qf2[16:17, :], in_=dview(
        wh["c32row"], 0, [[TB, 1], [1, TB]])), "wld", 16)
    kb.inc(sy.dma_start(out=st.kfT[:, :, 272:273].squeeze(2), in_=dview(
        wh["onecol128"], 0, [[4, 128], [1, 4]]),
        allow_slow_non_contiguous=True), "wld", 16)
    kb.inc(sy.dma_start(out=st.brow2bf[1:2, :],
                        in_=st.cst_ones_row[0:1, 0:TB]), "wld", 16)
    kb.wait("sync", "wld", kb.count["wld"])

    s = Sched(kb, ("sp", "sv", "sa", "xi0", "xi1", "out"))
    P1 = [(it, jt) for it in range(4) for jt in range(it + 1)]

    def emit_blk(j, jd):
        s.blk_start(j)
        sx = st.rows_sb[0:1, 0:512]
        sxx = st.rows_sb[0:1, 512:1024]
        u = st.rows_sb[0:1, 1024:1536]
        var = st.rows_sb[0:1, 1536:2048]

        for tt in range(4):
            tpar = tt % 2
            gidx = j * 4 + tt
            if gidx >= 2:
                pj, ptt = divmod(gidx - 2, 4)
                s.wait("sync", pj, f"vST{ptt}")
                s.wait("sync", pj, f"aSQb{ptt}")
                s.wait("sync", pj, f"pXT{ptt}")
            ldx = sy.dma_start(out=st.x0t[tpar][:], in_=dview(
                x0h, (jd * TB + tt * 128) * D, [[D, 128], [1, D]]))
            s.inc(ldx, f"xi{tpar}", f"x{tt}")
            s.wait("tensor", j, f"x{tt}")
            if gidx >= 1:
                pj, ptt = divmod(gidx - 1, 4)
                s.wait("tensor", pj, f"aXT{ptt}")
            if j >= 1 and tt == 0:
                s.wait("tensor", j - 1, "aWo7")
            for c8 in range(8):
                bk = st.bank[6 + c8 // 4]
                mm = t.transpose(bk[:, (c8 % 4) * 128:(c8 % 4 + 1) * 128],
                                 st.x0t[tpar][:, c8 * 128:(c8 + 1) * 128],
                                 st.identf[:])
            s.inc(mm, "sp", f"pXT{tt}")
            s.wait("scalar", j, f"pXT{tt}")
            if j >= 1 and tt == 0:
                s.wait("scalar", j - 1, "vT1")
            for bi in range(2):
                ev = a.copy(st.x0cm[:, bi * 4:(bi + 1) * 4,
                                    tt * 128:(tt + 1) * 128],
                            st.bank[6 + bi][:, 0:512]
                            .rearrange("p (c w) -> p c w", c=4))
            s.inc(ev, "sa", f"aXT{tt}")
            s.wait("scalar", j, f"x{tt}")
            if j >= 1:
                s.wait("scalar", j - 1, "pSR")
            s.inc(a.activation(st.sqscr[:], st.x0t[tpar][:, 0:512],
                               AF.Square, accum_out=st.statc2[:, 0:1]), "sa",
                  f"aSQa{tt}")
            s.inc(a.activation(st.sqscr[:], st.x0t[tpar][:, 512:1024],
                               AF.Square, accum_out=st.statc2[:, 1:2]),
                  "sa", f"aSQb{tt}")
            s.wait("vector", j, f"x{tt}")
            if j >= 1:
                s.wait("vector", j - 1, "pSR")
            s.inc(v.reduce_sum(st.statc[:, tt, 0:1], st.x0t[tpar][:], AX.X),
                  "sv", f"vSTa{tt}")
            s.wait("vector", j, f"aSQb{tt}")
            s.inc(v.tensor_add(st.statc[:, tt, 1:2], st.statc2[:, 0:1],
                               st.statc2[:, 1:2]), "sv", f"vST{tt}")

        for tt in range(4):
            s.wait("tensor", j, f"vST{tt}")
        if j >= 1:
            s.wait("tensor", j - 1, "vRM0")
            s.wait("tensor", j - 1, "vU0a")
            s.wait("tensor", j - 1, "vXI")
        for tt in range(4):
            t.matmul(st.bank[0][0:1, tt * 128:(tt + 1) * 128],
                     st.statc[:, tt, 0:1], st.identf[:],
                     start=True, stop=True)
        for tt in range(4):
            mm = t.matmul(st.ps_ea[0:1, tt * 128:(tt + 1) * 128],
                          st.statc[:, tt, 1:2], st.identf[:],
                          start=True, stop=True)
        s.inc(mm, "sp", "pSR")
        s.wait("vector", j, "pSR")
        v.tensor_copy(sx, st.bank[0][0:1, 0:512])
        s.inc(v.tensor_copy(sxx, st.ps_ea[0:1, 0:512]), "sv", "vRM0")
        s.wait("vector", j, "vRM0")
        s.inc(v.scalar_tensor_tensor(u, sx, 1.0 / D, sx, ALU.mult, ALU.mult),
              "sv", "vRM1")
        s.wait("vector", j, "vRM1")
        s.inc(v.tensor_sub(var, sxx, u), "sv", "vRM2")
        s.wait("scalar", j, "vRM2")
        s.inc(a.activation(u, var, AF.Sqrt, bias=st.C("lneps", 1),
                           scale=st.C("invD", 1)), "sa", "aRS")
        s.wait("vector", j, "aRS")
        s.inc(v.reciprocal(st.arow[0:1, :], u), "sv", "vRM3")
        s.wait("vector", j, "vRM3")
        s.inc(v.scalar_tensor_tensor(st.brow2[0:1, :], sx, -1.0 / D,
                                     st.arow[0:1, :], ALU.mult, ALU.mult),
              "sv", "vRM4")
        s.wait("vector", j, "vRM4")
        s.inc(v.tensor_copy(st.brow2bf[0:1, :], st.brow2[0:1, :]), "sv",
              "vRM")
        s.wait("tensor", j, "vRM")
        if j >= 1:
            s.wait("tensor", j - 1, "vXI")
        s.inc(t.matmul(st.ps_ea[:], st.ones1_f32[:], st.arow[0:1, :],
                       start=True, stop=True), "sp", "pEA")
        if j >= 1:
            s.wait("tensor", j - 1, "vAC")
        for tt in range(4):
            mm = t.matmul(st.bank[4][:, tt:tt + 1],
                          st.arow[0:1, bass.ts(tt, 128)],
                          st.cst[0:1, 0:1], start=True, stop=True)
        s.inc(mm, "sp", "pAC")
        s.wait("vector", j, "pAC")
        s.inc(v.tensor_copy(st.acol[:], st.bank[4][:, 0:4]), "sv", "vAC")
        s.wait("vector", j, "pEA")
        s.wait("vector", j, "aXT3")
        if j >= 1:
            s.wait("vector", j - 1, "pVl")
        for c in range(NCH):
            zi = v.tensor_mul(st.z1[:, c, :], st.x0cm[:, c, :], st.ps_ea[:])
        s.inc(zi, "sv", "vZ")

        for mt4 in range(4):
            pp = st.bank[2 + mt4 % 2]
            s.wait("tensor", j, "vZ")
            if mt4 == 0 and j >= 1:
                s.wait("tensor", j - 1, "vYE15")
            if mt4 == 1 and j >= 1:
                s.wait("tensor", j - 1, "vSN15")
            if mt4 >= 2:
                s.wait("tensor", j, f"vQ{mt4 - 2}")
            for c in range(NCH):
                t.matmul(pp[:], st.wqp[:, c, bass.ts(mt4, 128)],
                         st.z1[:, c, :], start=(c == 0), stop=False)
            mm = t.matmul(pp[:], st.colsQ[0:1, bass.ts(mt4, 128)],
                          st.brow2bf[0:1, :], start=False, stop=True)
            s.inc(mm, "sp", f"pQ{mt4}")
            s.wait("vector", j, f"pQ{mt4}")
            s.inc(v.tensor_copy(st.q1[:, mt4, :], pp[:]), "sv",
                  f"vQ{mt4}")
        for mt4 in range(4):
            pp = st.bank[2 + mt4 % 2]
            s.wait("tensor", j, f"vK{mt4 - 2}" if mt4 >= 2
                   else f"vQ{mt4 + 2}")
            for c in range(NCH):
                t.matmul(pp[:], st.wkp[:, c, bass.ts(mt4, 128)],
                         st.z1[:, c, :], start=(c == 0), stop=False)
            mm = t.matmul(pp[:], st.colsK[0:1, bass.ts(mt4, 128)],
                          st.brow2bf[0:1, :], start=False, stop=True)
            s.inc(mm, "sp", f"pK{mt4}")
            s.wait("vector", j, f"pK{mt4}")
            s.inc(v.tensor_copy(st.k1[:, mt4, :], pp[:]), "sv",
                  f"vK{mt4}")
        for tt in range(4):
            ppA = st.bank[2 + tt % 2]
            ppB = st.bank[2 + (tt + 1) % 2]
            s.wait("tensor", j, f"vVE{tt - 1}" if tt >= 1 else "vK2")
            for c in range(NCH):
                t.matmul(ppA[:], st.z1[:, c, bass.ts(tt, 128)],
                         st.wkp[:, c, :], start=(c == 0), stop=False)
            mm = t.matmul(ppA[:], st.brow2bf[0:1, bass.ts(tt, 128)],
                          st.colsK[0:1, :], start=False, stop=True)
            s.inc(mm, "sp", f"pKT{tt}")
            s.wait("vector", j, f"pKT{tt}")
            s.wait("vector", j, "vAC")
            s.inc(v.tensor_copy(st.kTb[:, tt, :], ppA[:]),
                  "sv", f"vKT{tt}")
            s.wait("tensor", j, f"vKT{tt - 1}" if tt >= 1 else "vK3")
            for half in range(2):
                if half == 1:
                    s.wait("tensor", j, f"vVEh{tt}")
                for c in range(NCH):
                    t.matmul(ppB[:], st.z1[:, c, bass.ts(tt, 128)],
                             st.wvr[:, c, half * 512:(half + 1) * 512],
                             start=(c == 0), stop=False)
                mm = t.matmul(ppB[:], st.brow2bf[0:1, bass.ts(tt, 128)],
                              st.colsV[0:1, half * 512:(half + 1) * 512],
                              start=False, stop=True)
                s.inc(mm, "sp", f"pV{tt}_{half}")
                s.wait("vector", j, f"pV{tt}_{half}")
                s.inc(v.tensor_copy(
                    st.vb[:, tt, half * 512:(half + 1) * 512], ppB[:]), "sv",
                    f"vVEh{tt}" if half == 0 else f"vVE{tt}")
        s.inc(v.tensor_copy(st.statc2[:, 0:1], st.statc2[:, 0:1]), "sv",
              "pVl")

        for h in range(16):
            ht, hr = h // 4, 32 * (h % 4)
            qh = st.q1[hr:hr + 32, ht, :]
            kh = st.k1[hr:hr + 32, ht, :]
            hpar = h % 2
            s.wait("tensor", j, f"vQ{ht}")
            s.wait("tensor", j, f"vK{ht}")
            s.wait("tensor", j, "vVE3")
            if h >= 1:
                s.wait("tensor", j, f"vF{h - 1}")
            tp = (hr, 0)
            t.matmul(st.bank[4][:], st.P2pat[hr:hr + 32, :], qh,
                     start=True, stop=True, tile_position=tp)
            s.inc(t.matmul(st.bank[5][:], st.P1apat[hr:hr + 32, :], qh,
                           start=True, stop=True, tile_position=tp), "sp",
                  f"pF1{h}")
            s.wait("vector", j, f"pF1{h}")
            if h >= 1:
                s.wait("vector", j, f"pY{h - 1}_9")
            s.inc(v.tensor_copy(st.M2e[:], st.bank[4][:]), "sv", f"vM2{h}")
            s.wait("vector", j, f"vM2{h}")
            s.inc(v.tensor_mul(st.qf0[:], st.bank[5][:], st.M2e[:]), "sv",
                  f"vF0{h}")
            s.wait("tensor", j, f"vF0{h}")
            s.inc(t.matmul(st.bank[5][:], st.P1bpat[hr:hr + 32, :], qh,
                           start=True, stop=True, tile_position=tp), "sp",
                  f"pF2{h}")
            s.wait("vector", j, f"pF2{h}")
            v.tensor_mul(st.qf1[:], st.bank[5][:], st.M2e[:])
            s.inc(v.tensor_scalar(st.qf2[0:16, :], qh[0:16, :], 16.0, None,
                                  ALU.mult), "sv", f"vF{h}")
            for tt in range(4):
                kslice = st.kTb[:, tt, 32 * h:32 * h + 16]
                i0 = kslice.unsqueeze(2).broadcast_to([128, 16, 16])
                i1 = kslice.unsqueeze(1).broadcast_to([128, 16, 16])
                s.wait("vector", j, f"vKT{tt}")
                if h >= 1 and tt == 0:
                    s.wait("vector", j, f"pSN{h - 1}")
                v.scalar_tensor_tensor(
                    st.kfT[:, tt, 0:256].rearrange("p (a b) -> p a b", a=16),
                    i0, X2S, i1, ALU.mult, ALU.mult)
                fi = v.tensor_scalar(st.kfT[:, tt, 256:272], kslice, 0.5,
                                     None, ALU.mult)
            s.inc(fi, "sv", f"vKF{h}")
            yps = st.bank[2]
            s.wait("tensor", j, "vVE3")
            s.wait("tensor", j, f"vF{h}")
            if h >= 2:
                s.wait("tensor", j, f"vYE{h - 1}" if h % 2 == 0 else
                       f"vYE{h - 2}")
            t.matmul(yps[64 * hpar:64 * hpar + 64, :],
                     st.Sstbf[0:128, h, 0, :], st.qf0[:],
                     start=True, stop=False, tile_position=(0, 64 * hpar))
            t.matmul(yps[64 * hpar:64 * hpar + 64, :],
                     st.Sstbf[0:128, h, 1, :], st.qf1[:],
                     start=False, stop=False, tile_position=(0, 64 * hpar))
            t.matmul(yps[64 * hpar:64 * hpar + 64, :],
                     st.Sstbf[0:17, h, 2, :], st.qf2[0:17, :],
                     start=False, stop=False, tile_position=(0, 64 * hpar))
            for pi, (it, jt) in enumerate(P1):
                sb = st.bank[6 + pi % 2]
                s.wait("tensor", j, f"vKF{h}")
                if pi >= 2:
                    s.wait("tensor", j, f"aPL{h}_{pi - 2}")
                elif h >= 1:
                    s.wait("tensor", j, f"aPL{h - 1}_{8 + pi}")
                else:
                    s.wait("tensor", j, "aXT3")
                    if j >= 1:
                        s.wait("tensor", j - 1, "aWo7")
                mm = t.matmul(sb[:, 0:128], kh[:, bass.ts(jt, 128)],
                              qh[:, bass.ts(it, 128)], start=True, stop=True,
                              tile_position=tp)
                s.inc(mm, "sp", f"pS{h}_{pi}")
                s.wait("scalar", j, f"pS{h}_{pi}")
                if pi >= 2:
                    s.wait("scalar", j, f"vA{h}_{pi - 2}")
                elif h >= 1:
                    s.wait("scalar", j, f"vA{h - 1}_{8 + pi}")
                s.inc(a.activation(st.tsb[pi % 2][:], sb[:, 0:128],
                                   AF.Square, bias=st.C("poly_b")), "sa",
                      f"aPL{h}_{pi}")
                s.wait("vector", j, f"aPL{h}_{pi}")
                if pi >= 2:
                    s.wait("vector", j, f"pY{h}_{pi - 2}")
                elif h >= 1:
                    s.wait("vector", j, f"pY{h - 1}_{8 + pi}")
                if it == jt:
                    va = v.scalar_tensor_tensor(st.Asb[pi % 2][:],
                                                st.tsb[pi % 2][:], 16.0,
                                                st.trilm[:], ALU.add,
                                                ALU.mult)
                else:
                    va = v.tensor_scalar_add(st.Asb[pi % 2][:],
                                             st.tsb[pi % 2][:], 16.0)
                s.inc(va, "sv", f"vA{h}_{pi}")
                s.wait("tensor", j, f"vA{h}_{pi}")
                mm = t.matmul(yps[64 * hpar:64 * hpar + 64,
                                  bass.ts(it, 128)],
                              st.vb[:, jt, 64 * h:64 * h + 64],
                              st.Asb[pi % 2][:], start=False,
                              stop=(pi == len(P1) - 1),
                              tile_position=(0, 64 * hpar))
                s.inc(mm, "sp", f"pY{h}_{pi}")
            s.wait("tensor", j, f"vKF{h}")
            if h >= 1:
                s.wait("tensor", j, f"vSN{h - 1}")
            for f3 in range(3):
                rows = (128, 128, 17)[f3]
                lhs = (lambda tt_: st.kfT[:, tt_, 128 * f3:128 * f3 + rows]
                       if f3 < 2 else st.kfT[:, tt_, 256:273])
                for tt in range(4):
                    mm = t.matmul(
                        st.bank[3][0:rows, 64 * f3:64 * f3 + 64],
                        lhs(tt), st.vb[:, tt, 64 * h:64 * h + 64],
                        start=(tt == 0), stop=(tt == 3))
            s.inc(mm, "sp", f"pSN{h}")
            s.wait("vector", j, f"pSN{h}")
            s.wait("vector", j, f"pY{h}_9")
            for f3 in range(3):
                rows = (128, 128, 17)[f3]
                ad = v.tensor_add(st.Sst[0:rows, h, f3, :],
                                  st.Sst[0:rows, h, f3, :],
                                  st.bank[3][0:rows, 64 * f3:64 * f3 + 64])
            s.inc(ad, "sv", f"vSA{h}")
            s.wait("vector", j, f"vSA{h}")
            for f3 in range(3):
                rows = (128, 128, 17)[f3]
                cp = v.tensor_copy(st.Sstbf[0:rows, h, f3, :],
                                   st.Sst[0:rows, h, f3, :])
            s.inc(cp, "sv", f"vSN{h}")
            if hpar == 1:
                s.wait("vector", j, f"pY{h}_9")
                s.wait("vector", j, f"pY{h - 1}_9")
                if j >= 1 and h == 1:
                    s.wait("vector", j - 1, "pTU2")
                s.inc(v.tensor_copy(st.Y1[:, h // 2, :], yps[:]), "sv",
                      f"vYE{h}")

        for p8 in range(8):
            s.wait("scalar", j, f"vYE{2 * p8 + 1}")
            if p8 >= 1:
                s.wait("scalar", j, f"pRA{p8 - 1}")
            s.inc(a.activation(st.y2bf[:], st.Y1[:, p8, :], AF.Square),
                  "sa", f"aY2{p8}")
            s.wait("tensor", j, f"aY2{p8}")
            if p8 == 0:
                s.wait("tensor", j, "pSR")
            mm = t.matmul(st.bank[0][0:16, :], st.pairpat[:, p8, :],
                          st.y2bf[:], start=(p8 == 0), stop=(p8 == 7))
            s.inc(mm, "sp", f"pRA{p8}")
        s.wait("scalar", j, "pRA7")
        s.inc(a.activation(st.r16[:], st.bank[0][0:16, :], AF.Sqrt,
                           bias=st.C("rmseps", 16), scale=st.C("invHD", 16)),
              "sa", "aRR")
        s.wait("vector", j, "aRR")
        s.inc(v.reciprocal(st.r16[:], st.r16[:]), "sv", "vRR0")
        s.wait("vector", j, "vRR0")
        s.inc(v.tensor_copy(st.r16b[:], st.r16[:]), "sv", "vRR")
        for p8 in range(8):
            s.wait("tensor", j, "vRR")
            s.wait("tensor", j, f"vYN{p8 - 1}" if p8 >= 1 else "vF15")
            mm = t.matmul(st.bank[5][:], st.expdpat[:, p8, :], st.r16b[:],
                          start=True, stop=True)
            s.inc(mm, "sp", f"pRE{p8}")
            s.wait("vector", j, f"pRE{p8}")
            s.inc(v.tensor_mul(st.Y1[:, p8, :], st.Y1[:, p8, :],
                               st.bank[5][:]), "sv", f"vYN{p8}")

        for o in range(NCH):
            po = st.bank[6 + o % 2]
            s.wait("tensor", j, "vYN7")
            if o >= 2:
                s.wait("tensor", j, f"aWo{o - 2}")
            for c in range(NCH):
                mm = t.matmul(po[:], st.wor[:, c, o * 128:(o + 1) * 128],
                              st.Y1[:, c, :], start=(c == 0),
                              stop=(c == NCH - 1))
            s.inc(mm, "sp", f"pWo{o}")
            s.wait("scalar", j, f"pWo{o}")
            if o < 2:
                s.wait("scalar", j - 1, "outI")
                s.wait("scalar", j - 1, "vXI")
            s.inc(a.copy(st.att[:, o, :], po[:]), "sa", f"aWo{o}")
        for c in range(NCH):
            s.wait("scalar", j, f"aWo{c}")
            if c >= 1:
                s.wait("scalar", j, f"pTS1_{c - 1}")
            elif j >= 1:
                s.wait("scalar", j - 1, "vU0a")
            s.inc(a.activation(st.y2bf[:], st.att[:, c, :], AF.Square),
                  "sa", f"aTS1_{c}")
            s.wait("tensor", j, f"aTS1_{c}")
            if c == 0:
                s.wait("tensor", j, "pRA7")
            s.inc(t.matmul(st.bank[0][0:1, 0:512], st.onescol_bf[:],
                           st.y2bf[:], start=(c == 0), stop=(c == NCH - 1)),
                  "sp", f"pTS1_{c}")
        for c in range(NCH):
            s.wait("vector", j, f"aWo{c}")
            if c == 0:
                s.wait("vector", j, "vYN7")
                s.wait("vector", j, "pWo7")
            cp = v.tensor_copy(st.Y1[:, c, :], st.att[:, c, :])
        s.inc(cp, "sv", "vAB")
        s.wait("tensor", j, "vAB")
        s.wait("tensor", j, "vZ")
        for c in range(NCH):
            mm = t.matmul(st.ps_ea[0:1, 0:512], st.onescol_bf[:],
                          st.Y1[:, c, :], start=(c == 0), stop=(c == NCH - 1))
        s.inc(mm, "sp", "pTS2")
        s.wait("vector", j, "pTS1_7")
        s.wait("vector", j, "pTS2")
        v.tensor_copy(sxx, st.bank[0][0:1, 0:512])
        s.inc(v.tensor_copy(sx, st.ps_ea[0:1, 0:512]), "sv", "vT0a")
        s.wait("vector", j, "vT0a")
        s.inc(v.scalar_tensor_tensor(u, sx, 1.0 / D, sx, ALU.mult, ALU.mult),
              "sv", "vT0b")
        s.wait("vector", j, "vT0b")
        s.inc(v.tensor_sub(var, sxx, u), "sv", "vT0c")
        s.wait("scalar", j, "vT0c")
        s.inc(a.activation(u, var, AF.Sqrt, bias=st.C("lneps", 1),
                           scale=st.C("invD", 1)), "sa", "aT0")
        s.wait("vector", j, "aT0")
        s.inc(v.reciprocal(st.arow[0:1, :], u), "sv", "vT0d")
        s.wait("vector", j, "vT0d")
        s.inc(v.scalar_tensor_tensor(st.brow2[0:1, :], sx, -1.0 / D,
                                     st.arow[0:1, :], ALU.mult, ALU.mult),
              "sv", "vT0")
        s.wait("tensor", j, "vT0")
        s.inc(t.matmul(st.ps_ea[:], st.ones1_f32[:], st.arow[0:1, :],
                       start=True, stop=True), "sp", "pT0")
        s.wait("tensor", j, "vYN7")
        s.wait("tensor", j, "pRE7")
        s.inc(t.matmul(st.bank[5][:], st.ones1_f32[:], st.brow2[0:1, :],
                       start=True, stop=True), "sp", "pT0b")
        s.wait("vector", j, "pT0")
        s.wait("vector", j, "pT0b")
        if j >= 1:
            s.wait("vector", j - 1, "outX")
        for c in range(NCH):
            v.tensor_mul(st.x1sb[:, c, :], st.att[:, c, :], st.ps_ea[:])
            vi = v.tensor_add(st.x1sb[:, c, :], st.x1sb[:, c, :],
                              st.bank[5][:])
        s.inc(vi, "sv", "vT1a")
        s.wait("vector", j, "vT1a")
        for c in range(NCH):
            vi = v.tensor_add(st.x1sb[:, c, :], st.x1sb[:, c, :],
                              st.x0cm[:, c, :])
        s.inc(vi, "sv", "vT1")
        for c in range(NCH):
            s.wait("scalar", j, "vT1")
            if c >= 1:
                s.wait("scalar", j, f"pTU1_{c - 1}")
            s.inc(a.activation(st.y2bf[:], st.x1sb[:, c, :], AF.Square),
                  "sa", f"aTU1_{c}")
            s.wait("tensor", j, f"aTU1_{c}")
            if c == 0:
                s.wait("tensor", j, "pTS1_7")
                s.wait("tensor", j, "vT0a")
            s.inc(t.matmul(st.bank[0][0:1, 0:512], st.onescol_bf[:],
                           st.y2bf[:], start=(c == 0), stop=(c == NCH - 1)),
                  "sp", f"pTU1_{c}")
        for c in range(NCH):
            s.wait("vector", j, "vT1")
            cp = v.tensor_copy(st.Y1[:, c, :], st.x1sb[:, c, :])
        s.inc(cp, "sv", "vXB")
        s.wait("tensor", j, "vXB")
        s.wait("tensor", j, "vT1a")
        for c in range(NCH):
            mm = t.matmul(st.ps_ea[0:1, 0:512], st.onescol_bf[:],
                          st.Y1[:, c, :], start=(c == 0), stop=(c == NCH - 1))
        s.inc(mm, "sp", "pTU2")
        s.wait("vector", j, "pTU1_7")
        s.wait("vector", j, "pTU2")
        v.tensor_copy(sxx, st.bank[0][0:1, 0:512])
        s.inc(v.tensor_copy(sx, st.ps_ea[0:1, 0:512]), "sv", "vU0a")
        s.wait("vector", j, "vU0a")
        s.inc(v.scalar_tensor_tensor(u, sx, 1.0 / D, sx, ALU.mult, ALU.mult),
              "sv", "vU0b")
        s.wait("vector", j, "vU0b")
        s.inc(v.tensor_sub(var, sxx, u), "sv", "vU0c")
        s.wait("scalar", j, "vU0c")
        s.inc(a.activation(u, var, AF.Sqrt, bias=st.C("lneps", 1),
                           scale=st.C("invD", 1)), "sa", "aU0")
        s.wait("vector", j, "aU0")
        s.inc(v.reciprocal(st.arow[0:1, :], u), "sv", "vU0d")
        s.wait("vector", j, "vU0d")
        s.inc(v.scalar_tensor_tensor(st.brow2[0:1, :], sx, -1.0 / D,
                                     st.arow[0:1, :], ALU.mult, ALU.mult),
              "sv", "vU0")
        s.wait("tensor", j, "vU0")
        s.wait("tensor", j, "vT1a")
        s.inc(t.matmul(st.ps_ea[:], st.ones1_f32[:], st.arow[0:1, :],
                       start=True, stop=True), "sp", "pU0")
        s.inc(t.matmul(st.bank[5][:], st.ones1_f32[:], st.brow2[0:1, :],
                       start=True, stop=True), "sp", "pU0b")
        s.wait("vector", j, "pU0")
        s.wait("vector", j, "pU0b")
        for c in range(NCH):
            v.tensor_mul(st.att[:, c, :], st.x1sb[:, c, :], st.ps_ea[:])
            vi = v.tensor_add(st.att[:, c, :], st.att[:, c, :], st.bank[5][:])
        s.inc(vi, "sv", "vXI")
        s.wait("sync", j, "vT1")
        s.inc(sy.dma_start(out=dview(x1h, jd * TB + GP,
                                     [[CMW, 128], [128 * CMW, NCH], [1, TB]]),
                           in_=st.x1sb[:]), "out", "outX")
        s.wait("sync", j, "vXI")
        s.inc(sy.dma_start(out=dview(xi0h, jd * TB + GP,
                                     [[CMW, 128], [128 * CMW, NCH], [1, TB]]),
                           in_=st.att[:]), "out", "outI")

    def emit_pair(jp, jv):
        for p in range(2):
            j = jp * 2 + p
            jd = j if jv is None else jv * 2 + p
            emit_blk(j, jd)

    run_streamed(kb, emit_pair)


def host_s1_patterns():
    import ml_dtypes
    P2 = np.zeros((128, 128), np.float32)
    P1a = np.zeros((128, 128), np.float32)
    P1b = np.zeros((128, 128), np.float32)
    for b in range(4):
        for m in range(128):
            P2[32 * b + m % 16, m] = 1.0
            P1a[32 * b + m // 16, m] = YSC * X2S
            P1b[32 * b + 8 + m // 16, m] = YSC * X2S
    tril = np.triu(np.ones((128, 128), np.float32))
    pair = np.zeros((128, 8, 16), np.float32)
    for k in range(128):
        for p8 in range(8):
            pair[k, p8, 2 * p8 + k // 64] = 1.0
    expd = np.zeros((16, 8, 128), np.float32)
    for p8 in range(8):
        for m in range(128):
            expd[2 * p8 + m // 64, p8, m] = 1.0
    flat = np.concatenate([P2.ravel(), P1a.ravel(), P1b.ravel(),
                           tril.ravel(), pair.ravel(), expd.ravel()])
    return flat.astype(ml_dtypes.bfloat16)


def prep_s1_weights_host(ins):
    Wq = np.asarray(ins["L0_attn_Wq"])
    Wk = np.asarray(ins["L0_attn_Wk"])
    Wv = np.asarray(ins["L0_attn_Wv"])
    Wo = np.asarray(ins["L0_attn_Wo"])
    wqp = np.zeros((D, 512), np.float32)
    wkp = np.zeros((D, 512), np.float32)
    for h in range(16):
        wqp[:, 32 * h:32 * h + 16] = Wq[16 * h:16 * h + 16, :].T
        wkp[:, 32 * h:32 * h + 16] = Wk[16 * h:16 * h + 16, :].T
    return {"wqp": np.ascontiguousarray(wqp),
            "wkp": np.ascontiguousarray(wkp),
            "wvt": np.ascontiguousarray(Wv.T.astype(np.float32)),
            "wot": np.ascontiguousarray(Wo.T.astype(np.float32))}


def build_dev_S1(races=True):
    import ml_dtypes
    ctx = ExitStack()
    nc = bass.Bass("TRN2", target_bir_lowering=False,
                   detect_race_conditions=races)
    kb = Kb(nc, ctx)
    with ctx:
        x0_h = nc.dram_tensor("x0", [SEQ, D], F32, kind="ExternalInput")
        x1_h = nc.dram_tensor("x1_cm", [D, CMW], F32, kind="ExternalOutput")
        xi0_h = nc.dram_tensor("xi0_cm", [D, CMW], F32, kind="ExternalOutput")
        wqp_h = nc.dram_tensor("wqp", [D, 512], F32, kind="ExternalInput")
        wkp_h = nc.dram_tensor("wkp", [D, 512], F32, kind="ExternalInput")
        wvt_h = nc.dram_tensor("wvt", [D, D], F32, kind="ExternalInput")
        wot_h = nc.dram_tensor("wot", [D, D], F32, kind="ExternalInput")
        gw_h = nc.dram_tensor("gw", [64], F32, kind="ExternalInput")
        msk_h = nc.dram_tensor("maskP", [3, 128, 384], F32,
                               kind="ExternalInput")
        id_h = nc.dram_tensor("identfi", [128, 128], F32,
                              kind="ExternalInput")
        cst_h = nc.dram_tensor("csts", [128, CONSTW], F32,
                               kind="ExternalInput")
        pat_h = nc.dram_tensor("s1pats", [host_s1_patterns().size], BF16,
                               kind="ExternalInput")
        c32_h = nc.dram_tensor("c32row", [TB], BF16, kind="ExternalInput")
        oc_h = nc.dram_tensor("onecol128", [128, 4], BF16,
                              kind="ExternalInput")
        wqbf = nc.dram_tensor("wqpbf", [D, 512], BF16)
        wkbf = nc.dram_tensor("wkpbf", [D, 512], BF16)
        wvbf = nc.dram_tensor("wvtbf", [D, D], BF16)
        wobf = nc.dram_tensor("wotbf", [D, D], BF16)
        wh = {"wqpbf": wqbf, "wkpbf": wkbf, "wvtbf": wvbf, "wotbf": wobf,
              "gw": gw_h, "c32row": c32_h, "onecol128": oc_h}
        ar = Arena(kb)
        st = alloc_common(kb, ar)
        emit_const_extras(kb, st, ar, cst_h, msk_h, id_h, pat_h)
        emit_const_init(kb, st, cst_h)
        emit_w_convert(kb, ar, [(wqp_h, wqbf, D * 512),
                                (wkp_h, wkbf, D * 512),
                                (wvt_h, wvbf, D * D),
                                (wot_h, wobf, D * D)])
        alloc_stageS1(kb, ar, st)
        st.rows_sb = ar.alloc([1, 2048], F32)
        st.arow = ar.alloc([1, TB], F32)
        st.brow2 = ar.alloc([2, TB], F32)
        st.brow2bf = ar.alloc([2, TB], BF16)
        print(f"[kernel] S1 arena high-water: {ar.hw} bytes/partition")
        kb.stage_reset(drain=("xin", "wld"))
        emit_stageS1(kb, st, wh, x0_h, x1_h, xi0_h)
        kb.wait("sync", "out", kb.count["out"])
        nc.all_engine_barrier()
        v = nc.vector
        dumps = [("dbg_z", st.z1.rearrange("p a b -> p (a b)"), NCH * TB),
                 ("dbg_q1", st.q1.rearrange("p a b -> p (a b)"), 4 * TB),
                 ("dbg_kT", st.kTb.rearrange("p a b -> p (a b)"), 4 * TB),
                 ("dbg_vb", st.vb.rearrange("p a b -> p (a b)"), 4 * 1024),
                 ("dbg_ya", st.Y1.rearrange("p a b -> p (a b)"), 8 * TB),
                 ("dbg_Y1", st.Y1.rearrange("p a b -> p (a b)"), NCH * TB),
                 ("dbg_at", st.att.rearrange("p a b -> p (a b)"), NCH * TB),
                 ("dbg_S", st.Sstbf.rearrange("p a b c -> p (a b c)"),
                  16 * 3 * 64),
                 ("dbg_qf", st.qf0, TB),
                 ("dbg_wq", st.wqp.rearrange("p a b -> p (a b)"), NCH * 512),
                 ("dbg_cq", st.colsQ, 512),
                 ("dbg_br", st.brow2bf, TB)]
        nparts = {"dbg_cq": 1, "dbg_br": 2}
        hnd = {}
        for nm, buf, w in dumps:
            hnd[nm] = nc.dram_tensor(nm, [nparts.get(nm, 128), w], F32,
                                     kind="ExternalOutput")
        stg = st.x0t[0]
        for nm, buf, w in dumps:
            np_ = nparts.get(nm, 128)
            for c0 in range(0, w, 1024):
                ww = min(1024, w - c0)
                kb.inc(v.tensor_copy(stg[0:np_, 0:ww],
                                     buf[0:np_, c0:c0 + ww]), "sv")
                kb.wait("sync", "sv", kb.count["sv"])
                kb.inc(nc.sync.dma_start(
                    out=dview(hnd[nm], c0, [[w, np_], [1, ww]]),
                    in_=stg[0:np_, 0:ww]), "out", 16)
                kb.wait("vector", "out", kb.count["out"])
        kb.wait("sync", "out", kb.count["out"])
    return nc


def alloc_final(kb, ar, st):
    ar.region_begin()
    st.f_x1 = [ar.alloc([128, NCH, TB], F32) for _ in range(2)]
    st.f_xi = [ar.alloc([128, NCH, TB], F32) for _ in range(2)]
    st.f_xib = ar.alloc([128, NCH, TB], BF16)
    st.f_x2b = ar.alloc([128, NCH, TB], BF16)
    st.f_out = ar.alloc([128, NCH, TB], F32)
    st.f_rows = ar.alloc([1, 2048], F32)
    st.f_arow = ar.alloc([1, TB], F32)
    st.f_brow = ar.alloc([1, TB], F32)
    return st


def emit_final(kb, st, x1h, xih, dst):
    nc = kb.nc
    t, v, a, sy = nc.tensor, nc.vector, nc.scalar, nc.sync
    s = Sched(kb, ("sp", "sv", "sa", "xi0", "xi1", "tb0", "tb1", "out"))

    def emit_blk(j, jd):
        s.blk_start(j)
        par = j % 2
        x1t, xit = st.f_x1[par], st.f_xi[par]
        sx = st.f_rows[0:1, 0:512]
        sxx = st.f_rows[0:1, 512:1024]
        u = st.f_rows[0:1, 1024:1536]
        var = st.f_rows[0:1, 1536:2048]

        s.wait("sync", j - 2, "vR7")
        s.inc(sy.dma_start(out=x1t[:], in_=dview(
            x1h, jd * TB + GP, [[CMW, 128], [128 * CMW, NCH], [1, TB]])),
            f"xi{par}", "x1")
        s.wait("sync", j - 2, "aSQ")
        s.wait("sync", j - 2, "vXB")
        s.inc(sy.dma_start(out=xit[:], in_=dview(
            xih, jd * TB + GP, [[CMW, 128], [128 * CMW, NCH], [1, TB]])),
            f"tb{par}", "xz")

        s.wait("vector", j, "xz")
        s.wait("vector", j - 1, "pST")
        s.inc(v.tensor_copy(st.f_xib[:], xit[:]), "sv", "vXB")
        s.wait("scalar", j, "xz")
        s.wait("scalar", j - 1, "pST")
        s.inc(a.activation(st.f_x2b[:], xit[:], AF.Square), "sa", "aSQ")

        s.wait("tensor", j, "vXB")
        s.wait("tensor", j, "aSQ")
        s.wait("tensor", j - 1, "aEV")
        for c in range(NCH):
            t.matmul(st.ps_rows[0:1, 0:512], st.onescol_bf[:],
                     st.f_xib[:, c, :], start=(c == 0), stop=(c == NCH - 1))
        for c in range(NCH):
            mm = t.matmul(st.ps_ea[0:1, 0:512], st.onescol_bf[:],
                          st.f_x2b[:, c, :], start=(c == 0),
                          stop=(c == NCH - 1))
        s.inc(mm, "sp", "pST")

        s.wait("scalar", j, "pST")
        s.wait("scalar", j - 1, "vBr")
        a.copy(sx, st.ps_rows[0:1, 0:512])
        s.inc(a.copy(sxx, st.ps_ea[0:1, 0:512]), "sa", "aEV")
        s.wait("vector", j, "aEV")
        s.inc(v.scalar_tensor_tensor(u, sx, 1.0 / D, sx, ALU.mult, ALU.mult),
              "sv", "vV1")
        s.wait("vector", j, "vV1")
        s.inc(v.tensor_sub(var, sxx, u), "sv", "vV2")
        s.wait("scalar", j, "vV2")
        s.inc(a.activation(u, var, AF.Sqrt, bias=st.C("lneps", 1),
                           scale=st.C("invD", 1)), "sa", "aSq")
        s.wait("vector", j, "aSq")
        s.inc(v.reciprocal(st.f_arow[0:1, :], u), "sv", "vRc")
        s.wait("vector", j, "vRc")
        s.inc(v.scalar_tensor_tensor(st.f_brow[0:1, :], sx, -1.0 / D,
                                     st.f_arow[0:1, :], ALU.mult, ALU.mult),
              "sv", "vBr")

        s.wait("tensor", j, "vBr")
        s.wait("tensor", j - 1, "vR7")
        t.matmul(st.bank[2][:], st.ones1_f32[:], st.f_arow[0:1, :],
                 start=True, stop=True)
        s.inc(t.matmul(st.bank[3][:], st.ones1_f32[:], st.f_brow[0:1, :],
                       start=True, stop=True), "sp", "pEx")

        s.wait("vector", j, "pEx")
        s.wait("vector", j, "x1")
        s.wait("vector", j - 1, "outS7")
        for c in range(NCH):
            v.tensor_mul(st.f_out[:, c, :], xit[:, c, :], st.bank[2][:])
            v.tensor_add(st.f_out[:, c, :], st.f_out[:, c, :], st.bank[3][:])
            if c == NCH - 1:
                s.inc(v.tensor_add(st.f_out[:, c, :], st.f_out[:, c, :],
                                   x1t[:, c, :]), "sv", "vR7")
            else:
                v.tensor_add(st.f_out[:, c, :], st.f_out[:, c, :],
                             x1t[:, c, :])

        s.wait("sync", j, "vR7")
        di = sy.dma_start(out=dview(dst, jd * TB,
                                    [[SEQ, 128], [128 * SEQ, NCH], [1, TB]]),
                          in_=st.f_out[:])
        s.inc(di, "out", "outS7")

    def emit_pair(jp, jv):
        for p in range(2):
            j = jp * 2 + p
            jd = j if jv is None else jv * 2 + p
            emit_blk(j, jd)

    run_streamed(kb, emit_pair)


def build_full(races=False):
    ctx = ExitStack()
    nc = bass.Bass("TRN2", target_bir_lowering=False,
                   detect_race_conditions=races,
                   dynamic_dma_scratch_size=16384)
    kb = Kb(nc, ctx)
    with ctx:
        x0_h = nc.dram_tensor("x0", [SEQ, D], F32, kind="ExternalInput")
        xout_h = nc.dram_tensor("xout", [D, SEQ], F32, kind="ExternalOutput")
        wqp_h = nc.dram_tensor("wqp", [D, 512], F32, kind="ExternalInput")
        wkp_h = nc.dram_tensor("wkp", [D, 512], F32, kind="ExternalInput")
        wvt_h = nc.dram_tensor("wvt", [D, D], F32, kind="ExternalInput")
        wot_h = nc.dram_tensor("wot", [D, D], F32, kind="ExternalInput")
        gw_h = nc.dram_tensor("gw", [64], F32, kind="ExternalInput")
        wqkv_h = nc.dram_tensor("wqkv", [NL, D, 3072], F32,
                                kind="ExternalInput")
        wo_h = nc.dram_tensor("wo", [NL, D, D], F32, kind="ExternalInput")
        qsk_h = nc.dram_tensor("qsk", [NL, 2, 64], F32, kind="ExternalInput")
        w1i_h = nc.dram_tensor("w1i", [NL, D, MT, 256], F32,
                               kind="ExternalInput")
        b1i_h = nc.dram_tensor("b1i", [NL, MT, 256], F32,
                               kind="ExternalInput")
        w2p_h = nc.dram_tensor("w2p", [NL, DIP, D], F32, kind="ExternalInput")
        gbb_h = nc.dram_tensor("gbb", [NL, 128, 3 * NCH], F32,
                               kind="ExternalInput")
        cst_h = nc.dram_tensor("csts", [128, CONSTW], F32,
                               kind="ExternalInput")
        msk_h = nc.dram_tensor("maskP", [3, 128, 384], F32,
                               kind="ExternalInput")
        id_h = nc.dram_tensor("identfi", [128, 128], F32,
                              kind="ExternalInput")
        pat_h = nc.dram_tensor("s1pats", [host_s1_patterns().size], BF16,
                               kind="ExternalInput")
        c32_h = nc.dram_tensor("c32row", [TB], BF16, kind="ExternalInput")
        oc_h = nc.dram_tensor("onecol128", [128, 4], BF16,
                              kind="ExternalInput")
        tbl_h = nc.dram_tensor("tbl4", [4, SEQ, 64], F32,
                               kind="ExternalInput")
        zg_h = nc.dram_tensor("zguard", [128, 64], F32, kind="ExternalInput")
        wqbf = nc.dram_tensor("wqpbf", [D, 512], BF16)
        wkbf = nc.dram_tensor("wkpbf", [D, 512], BF16)
        wvbf = nc.dram_tensor("wvtbf", [D, D], BF16)
        wobfS = nc.dram_tensor("wotbf", [D, D], BF16)
        wqkvbf = nc.dram_tensor("wqkvbf", [NL, D, 3072], BF16)
        wobf = nc.dram_tensor("wobf", [NL, D, D], BF16)
        w1bf = nc.dram_tensor("w1bf", [NL, D, MT, 256], BF16)
        w2bf = nc.dram_tensor("w2bf", [NL, DIP, D], BF16)
        b1bf = nc.dram_tensor("b1bf", [NL, MT, 256], BF16)
        x1cm_h = nc.dram_tensor("x1cm", [D, CMW], F32,
                                kind="ExternalOutput" if FDBG_SCRATCH
                                else "Internal")
        xiA_h = nc.dram_tensor("xiA", [D, CMW], F32,
                               kind="ExternalOutput" if FDBG_SCRATCH
                               else "Internal")
        xiB_h = nc.dram_tensor("xiB", [D, CMW], F32,
                               kind="ExternalOutput" if FDBG_SCRATCH
                               else "Internal")

        whS = {"wqpbf": wqbf, "wkpbf": wkbf, "wvtbf": wvbf, "wotbf": wobfS,
               "gw": gw_h, "c32row": c32_h, "onecol128": oc_h}
        whB = {"wqkvbf": wqkvbf, "wobf": wobf, "qsk": qsk_h, "tbl4": tbl_h}
        whA = {"w1bf": w1bf, "w2bf": w2bf, "b1bf": b1bf, "gbb": gbb_h}

        ar = Arena(kb, size=ARENA_BYTES)
        st = alloc_common(kb, ar)
        emit_const_extras(kb, st, ar, cst_h, msk_h, id_h, pat_h)
        emit_const_init(kb, st, cst_h)
        bnc = [ar.perm([128, 1024], BF16) for _ in range(2)]
        conv_list = [(wqp_h, wqbf, D * 512, 0),
                     (wkp_h, wkbf, D * 512, 0),
                     (wvt_h, wvbf, D * D, 0),
                     (wot_h, wobfS, D * D, 0)]
        for lv in range(NL):
            conv_list += [
                (wqkv_h, wqkvbf, D * 3072, lv * D * 3072),
                (wo_h, wobf, D * D, lv * D * D),
                (w1i_h, w1bf, D * MT * 256, lv * D * MT * 256),
                (w2p_h, w2bf, DIP * D, lv * DIP * D),
                (b1i_h, b1bf, MT * 256, lv * MT * 256)]
        cmarks = emit_w_convert(kb, ar, conv_list, drain=FDRAIN, bounce=bnc)

        def conv_wait(idx):
            c0, c1 = cmarks[idx]
            kb.wait("sync", "wco0", c0)
            kb.wait("sync", "wco1", c1)
        alloc_stageS1(kb, ar, st)
        st.rows_sb = ar.alloc([1, 2048], F32)
        st.arow = ar.alloc([1, TB], F32)
        st.brow2 = ar.alloc([2, TB], F32)
        st.brow2bf = ar.alloc([2, TB], BF16)
        if not FNO_GUARD:
            for hh in (xiA_h, xiB_h):
                kb.inc(nc.sync.dma_start(
                    out=dview(hh, 0,
                              [[CMW, 128], [128 * CMW, NCH], [1, GP]]),
                    in_=dview(zg_h, 0, [[64, 128], [GP, NCH], [1, GP]])),
                    "out", 16)
        kb.stage_fence()
        conv_wait(3)
        emit_stageS1(kb, st, whS, x0_h, x1cm_h, xiA_h)
        alloc_stageB(kb, ar, st)
        hwB = ar.hw
        alloc_stageA(kb, ar, st)
        nl_run = NL if FNL is None else FNL
        for lv in range(nl_run):
            kb.stage_fence()
            conv_wait(4 + lv * 5 + 1)
            emit_stageB(kb, st, whB, lv, xiA_h, xiB_h)
            kb.stage_fence()
            conv_wait(4 + lv * 5 + 4)
            emit_stageA(kb, st, whA, lv, xiB_h, xiA_h)
        alloc_final(kb, ar, st)
        print(f"[kernel] full arena high-water: {ar.hw} (B: {hwB})")
        kb.stage_fence()
        if not FSKIP_FINAL:
            emit_final(kb, st, x1cm_h, xiA_h, xout_h)
        kb.wait("sync", "out", kb.count["out"])
    return nc


def host_all_inputs(ins):
    import ml_dtypes
    wp = prep_weights_host(ins)
    ws = prep_s1_weights_host(ins)
    qsk = np.ascontiguousarray(np.stack(
        [np.asarray(ins["L0_ffn_q_scale"], np.float32),
         np.asarray(ins["L0_ffn_k_scale"], np.float32)], axis=1))
    com = {
        "wqp": ws["wqp"], "wkp": ws["wkp"], "wvt": ws["wvt"],
        "wot": ws["wot"],
        "gw": np.ascontiguousarray(np.asarray(ins["L0_attn_g_weight"],
                                              np.float32)),
        "wqkv": np.ascontiguousarray(np.asarray(ins["L0_ffn_Wqkv"],
                                                np.float32)),
        "wo": np.ascontiguousarray(np.asarray(ins["L0_ffn_Wo"], np.float32)),
        "qsk": qsk,
        "w1i": wp["w1i"], "b1i": wp["b1i"], "w2p": wp["w2p"],
        "gbb": wp["gbb"],
        "csts": host_consts(), "maskP": host_masks(),
        "identfi": np.eye(128, dtype=np.float32),
        "s1pats": host_s1_patterns(),
        "c32row": np.full(TB, YSC, ml_dtypes.bfloat16),
        "onecol128": np.ones((128, 4), ml_dtypes.bfloat16),
        "tbl4": host_tables(),
        "zguard": np.zeros((128, 64), np.float32),
    }
    return com


def kernel(**ins):
    set_seq(4096)
    x = np.asarray(ins["L0_attn_hidden_states"], np.float32)
    com = host_all_inputs(ins)
    nc = build_full()
    in_maps = [dict(com, x0=np.ascontiguousarray(x[b])) for b in range(8)]
    r = run_bass_kernel_spmd(nc, in_maps, core_ids=list(range(8)))
    return np.stack([np.asarray(r.results[b]["xout"], np.float32).T
                     for b in range(8)])
